# version 26
# baseline (speedup 1.0000x reference)
import contextlib
from collections import defaultdict

import numpy as np
import concourse.bass as bass
import concourse.mybir as mybir
from concourse.bass_utils import run_bass_kernel_spmd

F32 = mybir.dt.float32
BF16 = mybir.dt.bfloat16
I32 = mybir.dt.int32
AF = mybir.ActivationFunctionType
ALU = mybir.AluOpType
AX = mybir.AxisListType

NCORES = 8
D = 1024
TC = 2048
TH = 1024
EPS = 1e-5


class Plan:

    ENGINES = ("sync", "tensor", "scalar", "vector", "gpsimd")

    def __init__(self, nc, ctx):
        self.nc = nc
        self.steps = {e: [] for e in self.ENGINES}
        self.cnt = defaultdict(int)
        self.water = defaultdict(int)
        self.own = defaultdict(int)
        self.sems = {}
        self._ctx = ctx
        for s in ("sPE", "sACT", "sDVE", "sPOOL", "sCC"):
            self._sem(s)

    def _sem(self, s):
        if s not in self.sems:
            self.sems[s] = self._ctx.enter_context(self.nc.semaphore(s))
        return self.sems[s]

    def _mk_waits(self, engine, waits):
        out = []
        for s, v in waits:
            if v is None or v <= 0:
                continue
            if v <= self.water[(engine, s)]:
                continue
            self._sem(s)
            self.water[(engine, s)] = v
            out.append((self.sems[s], v))
        return out

    OWN = {"vector": "sDVE", "scalar": "sACT", "gpsimd": "sPOOL"}

    def add(self, engine, fn, waits=(), inc=None, inc_n=1, is_dma=False):
        waits = list(waits)
        own = self.OWN.get(engine)
        if own is not None and not is_dma:
            prev = self.own[(engine, own)]
            if prev > 0:
                waits.insert(0, (own, prev))
            if inc is None:
                inc = own
        wl = self._mk_waits(engine, waits)
        ck = None
        ih = None
        if inc is not None:
            ih = self._sem(inc)
            self.cnt[inc] += inc_n
            self.own[(engine, inc)] += inc_n
            ck = self.cnt[inc]

        def step(eng, fn=fn, wl=wl, ih=ih, inc_n=inc_n):
            for h, v in wl:
                eng.wait_ge(h, v)
            r = fn(eng)
            if ih is not None:
                r.then_inc(ih, inc_n)
        self.steps[engine].append(step)
        return ck

    def dma(self, queue, out_ap, in_ap, waits=(), sem=None, transpose=False,
            **kw):
        assert sem is not None
        if transpose:
            fn = lambda eng, o=out_ap, i=in_ap: eng.dma_start_transpose(
                out=o, in_=i)
        else:
            fn = lambda eng, o=out_ap, i=in_ap, kw=kw: eng.dma_start(
                out=o, in_=i, **kw)
        return self.add(queue, fn, waits=waits, inc=sem, inc_n=16,
                        is_dma=True)

    def wait_only(self, engine, waits):
        wl = self._mk_waits(engine, list(waits))
        if not wl:
            return

        def step(eng, wl=wl):
            for h, v in wl:
                eng.wait_ge(h, v)
        self.steps[engine].append(step)

    def emit(self, block):
        for e in self.ENGINES:
            body = self.steps[e]

            def runner(eng, body=body):
                for f in body:
                    f(eng)
            getattr(block, e)(runner)


def build_nc(tap=False):
    nc = bass.Bass("TRN2", target_bir_lowering=False)
    ctx = contextlib.ExitStack()
    nc._keepalive = ctx

    d_x = nc.dram_tensor("xc", [TC, D], F32, kind="ExternalInput").ap()
    d_wk = nc.dram_tensor("wk", [D, 2 * D], F32, kind="ExternalInput").ap()
    d_wu = nc.dram_tensor("wu", [D, D], F32, kind="ExternalInput").ap()
    d_wo = nc.dram_tensor("wo", [D, D], F32, kind="ExternalInput").ap()
    d_kq = nc.dram_tensor("kq", [D, D], F32, kind="ExternalInput").ap()
    d_la = nc.dram_tensor("la", [D, 16], F32, kind="ExternalInput").ap()
    d_lb = nc.dram_tensor("lb", [16, D], F32, kind="ExternalInput").ap()
    d_bpw = nc.dram_tensor("bpw", [1, D], F32, kind="ExternalInput").ap()
    d_bpk = nc.dram_tensor("bpk", [1, D], F32, kind="ExternalInput").ap()
    d_bu = nc.dram_tensor("bu", [1, D], F32, kind="ExternalInput").ap()
    d_bo = nc.dram_tensor("bo", [1, D], F32, kind="ExternalInput").ap()
    d_fb = nc.dram_tensor("fb", [1, D], F32, kind="ExternalInput").ap()
    d_mask = nc.dram_tensor("mask", [8, 1], F32, kind="ExternalInput").ap()
    d_id = nc.dram_tensor("ident", [128, 128], F32, kind="ExternalInput").ap()
    d_out = nc.dram_tensor("yout", [TC, D], F32, kind="ExternalOutput").ap()

    cc_in = nc.dram_tensor("cc_in", [128, 8], F32)
    cc_out = nc.dram_tensor("cc_out", [1024, 8], F32)
    cc_in2 = nc.dram_tensor("cc_in2", [128, 8], F32)
    cc_out2 = nc.dram_tensor("cc_out2", [1024, 8], F32)
    tapd = {}
    if tap:
        for nm, shape, dt in [
                ("t_xT", [128, 16384], BF16), ("t_y", [128, 16384], BF16),
                ("t_A", [128, 16384], BF16), ("t_carry", [128, 24], F32),
                ("t_kq", [128, 8192], BF16), ("t_wo", [128, 8192], BF16),
                ("t_ln1", [128, 16384], BF16),
                ("t_lnf", [128, 16384], BF16), ("t_qs", [128, 11], F32),
                ("t_qrow", [1, 133], F32)]:
            tapd[nm] = nc.dram_tensor(nm, shape, dt, kind="ExternalOutput").ap()

    def sb(name, shape, dt=F32):
        return ctx.enter_context(nc.sbuf_tensor(name, shape, dt))

    xT = sb("xT", [128, 8 * TC], BF16)
    yB = sb("yB", [128, 8 * TC], BF16)
    aB = sb("aB", [128, 8 * TC], BF16)
    U = sb("U", [128, 10240], F32)
    QT = sb("QT", [128, 512], I32)
    Bg = sb("Bg", [128, TH], F32)
    Bq = sb("Bq", [128, TH], F32)
    Bu = sb("Bu", [128, TH], F32)
    Bb = sb("Bb", [128, TH], F32)
    Bg2 = [Bg, None]

    def Z(nt):
        return wbf[nt // 2][0:16, (nt % 2) * 512:(nt % 2 + 1) * 512]
    Bq2 = [Bq, Bq]
    Bu2 = [Bu, Bu]
    zt = sb("zt", [128, TH], BF16)
    brow = sb("brow", [1, 2304], BF16)
    wbf = [sb(f"wbf{i}", [128, 1024], BF16) for i in range(3)]
    woB = sb("woB", [128, 8 * D], BF16)
    kqB = sb("kqB", [128, 8 * D], BF16)
    laB = sb("laB", [128, 128], BF16)
    lbB = sb("lbB", [16, D], BF16)
    bcol = sb("bcol", [128, 48], F32)
    carry = sb("carry", [128, 24], F32)
    slocT = sb("slocT", [8, 128], F32)
    mskT = sb("mskT", [8, 1], F32)
    idT = sb("idT", [128, 128], F32)
    onest = sb("onest", [1, 128], F32)
    qsc = sb("qsc", [128, 12], F32)
    qrow = sb("qrow", [1, 136], F32)
    LNW = sb("LNW", [128, 16], F32)
    flt = sb("flt", [128, 2], F32)
    Gt = Bg[0:8, 0:1024]
    rowb = Bq[0:1, 0:1024]

    def XL(j):
        return U[:, j * 1024:(j + 1) * 1024]

    def WQ(j):
        return U[:, 4096 + (j % 2) * 1024: 5120 + (j % 2) * 1024]

    def MO(d):
        return U[:, d * 512:(d + 1) * 512]

    def XR(j):
        return U[:, 4096 + (j % 2) * 1024: 5120 + (j % 2) * 1024]

    def RT(j):
        return U[:, 6144 + (j % 2) * 1024: 7168 + (j % 2) * 1024]

    def LNT(j):
        return U[:, 8192 + (j % 2) * 1024: 9216 + (j % 2) * 1024]

    SCR = Bg[:, 0:1024]
    Bg2[1] = U[:, 3 * 1024:4 * 1024]

    P0 = ctx.enter_context(nc.psum_tensor("P0", [128, 2048], F32))
    P1 = ctx.enter_context(nc.psum_tensor("P1", [128, 2048], F32))

    p = Plan(nc, ctx)

    def flush_dve(waits=()):
        return p.add("vector", lambda e: e.tensor_copy(out=flt[:, 0:1],
                                                       in_=flt[:, 1:2]),
                     waits=waits, inc="sDVE")

    def flush_act(waits=()):
        return p.add("scalar", lambda e: e.copy(out=flt[:, 0:1],
                                                in_=flt[:, 1:2]),
                     waits=waits, inc="sACT")

    p.dma("sync", idT[:], d_id, sem="sS")
    p.dma("sync", mskT[:], d_mask, sem="sS")
    p.dma("sync", U[0:16, 0:1024], d_lb, sem="sS")
    ck_lb = p.add("vector",
                  lambda e: e.tensor_copy(out=lbB[:], in_=U[0:16, 0:1024]),
                  waits=[("sS", 48)], inc="sDVE")
    p.dma("sync",
          U[:, 1024:1152].rearrange("p (kc j) -> p kc j", j=16),
          d_la.rearrange("(kc p) j -> p kc j", p=128),
          waits=[("sDVE", ck_lb)], sem="sS")
    ck_la = p.add("vector",
                  lambda e: e.tensor_copy(out=laB[:], in_=U[:, 1024:1152]),
                  waits=[("sS", 64)], inc="sDVE")
    p.add("vector", lambda e: e.memset(flt[:], 0.0), inc="sDVE")
    p.add("vector", lambda e: e.memset(onest[:], 1.0), inc="sDVE")
    p.add("vector", lambda e: e.memset(zt[:], 0.0), inc="sDVE")
    p.add("vector", lambda e: e.memset(carry[:], 0.0), inc="sDVE")
    p.add("vector", lambda e: e.memset(qrow[0:1, 131:132], 7.0), inc="sDVE")
    ck_zt = p.cnt["sDVE"]

    for v, dsrc in enumerate([d_bpw, d_bpk, d_bu, d_bo, d_fb]):
        ck_row = p.dma("sync", rowb, dsrc, sem="sB")
        ck_pe = None
        for c in range(8):
            ck_pe = p.add(
                "tensor",
                lambda e, v=v, c=c: e.transpose(
                    out=P1[:, 1536 + v * 8 + c: 1537 + v * 8 + c],
                    in_=Bq[0:1, c * 128:(c + 1) * 128],
                    identity=idT[0:1, 0:1]),
                waits=[("sB", ck_row), ("sS", 64)], inc="sPE")
        ck_bias_pe = ck_pe
        ck_bias_evac = p.add(
            "scalar",
            lambda e, v=v: e.copy(out=bcol[:, v * 8:(v + 1) * 8],
                                  in_=P1[:, 1536 + v * 8: 1544 + v * 8]),
            waits=[("sPE", ck_pe)], inc="sACT")
        p.wait_only("sync", [("sPE", ck_pe)])
    p.add("vector",
          lambda e: e.tensor_scalar(out=bcol[:, 8:16], in0=bcol[:, 8:16],
                                    scalar1=-2.0, scalar2=None, op0=ALU.mult),
          waits=[("sACT", ck_bias_evac)], inc="sDVE")
    ck_bpk2 = p.cnt["sDVE"]

    ck_q_ld = [None] * 8
    ck_q_red = [None] * 8

    def passA_load(s):
        ck_q_ld[s] = p.dma("gpsimd", WQ(s), d_kq[s * 128:(s + 1) * 128, :],
                           waits=[("sDVE", ck_q_red[s - 2])], sem=f"sQ{s % 2}")

    def passA_reduce(s):
        ck_q_red[s] = p.add(
            "vector",
            lambda e, s=s: e.reduce_max(out=qsc[:, s:s + 1], in_=WQ(s),
                                        axis=AX.X, apply_absolute_value=True),
            waits=[(f"sQ{s % 2}", ck_q_ld[s])], inc="sDVE")

    QB = U[:, 9216:10240]
    T2 = U[:, 5120:6144]
    ck_qb_ld = [None] * 8
    ck_quant_act = None
    ck_qd_prev = None
    ck_t2_prev = None

    def passB_load(s, extra=()):
        w = list(extra)
        if ck_qd_prev is not None:
            w.append(("sDVE", ck_qd_prev))
        ck_qb_ld[s] = p.dma("gpsimd", QB, d_kq[s * 128:(s + 1) * 128, :],
                            waits=w, sem="sQL0")

    def quant_ph2(s, extra=()):
        nonlocal ck_quant_act, ck_qd_prev, ck_t2_prev
        ck_v = p.add("scalar", lambda e: e.activation(
            out=QB, in_=QB, func=AF.Copy, bias=0.0, scale=qsc[:, 10:11]),
            waits=[("sQL0", ck_qb_ld[s])] + list(extra), inc="sACT")
        ck_r = None
        wt2 = [("sDVE", ck_t2_prev)] if ck_t2_prev is not None             else [("sDVE", ck_q_red[7])]
        for hh in range(2):
            p.add("scalar", lambda e, hh=hh: e.copy(
                out=QT[:], in_=QB[:, hh * 512:(hh + 1) * 512]), inc="sACT")
            ck_r = p.add("scalar", lambda e, hh=hh: e.activation(
                out=T2[:, hh * 512:(hh + 1) * 512], in_=QT[:],
                func=AF.Copy, bias=-1.0, scale=1.0),
                waits=wt2 if hh == 0 else (), inc="sACT")
        p.add("vector", lambda e: e.tensor_tensor(
            out=QB, in0=QB, in1=T2, op=ALU.subtract),
            waits=[("sACT", ck_r)], inc="sDVE")
        p.add("vector", lambda e: e.scalar_tensor_tensor(
            out=T2, in0=QB, scalar=1.5, in1=T2,
            op0=ALU.is_gt, op1=ALU.add), inc="sDVE")
        ck_qd_prev = p.add("vector", lambda e: e.scalar_tensor_tensor(
            out=T2, in0=QB, scalar=0.5, in1=T2,
            op0=ALU.is_ge, op1=ALU.add), inc="sDVE")
        ck_t2_prev = ck_clip = p.add("vector", lambda e: e.tensor_scalar(
            out=T2, in0=T2, scalar1=-8.0, scalar2=7.0,
            op0=ALU.max, op1=ALU.min), inc="sDVE")
        ck_quant_act = p.add("scalar", lambda e, s=s: e.activation(
            out=kqB[:, s * 1024:(s + 1) * 1024], in_=T2,
            func=AF.Copy, bias=0.0, scale=qsc[:, 9:10]),
            waits=[("sDVE", ck_clip)], inc="sACT")

    ck_xl = [None] * 4
    ck_xq = {}
    ck_tp_g = [None] * 4
    ck_xe_bank = [None] * 4
    for g in range(4):
        for j in range(4):
            w = []
            if g == 0 and j == 0:
                w = [("sDVE", ck_lb)]
            if g == 0 and j == 1:
                w = [("sDVE", ck_la)]
            if g > 0:
                w = [("sPE", ck_tp_g[g - 1])]
            qn, sn = [("sync", "sX"), ("scalar", "sX2"),
                      ("gpsimd", "sX3")][(4 * g + j) % 3]
            ck_xq[sn] = ck_xl[j] = p.dma(
                qn, XL(j),
                d_x[(4 * g + j) * 128:(4 * g + j + 1) * 128, :],
                waits=w, sem=sn)
        for fc in range(8):
            bank = (g * 8 + fc) % 4
            ck_pe = None
            for j in range(4):
                w = [(sn, ck) for sn, ck in ck_xq.items()]
                if ck_xe_bank[bank] is not None:
                    w.append(ck_xe_bank[bank])
                ck_pe = p.add(
                    "tensor",
                    lambda e, j=j, fc=fc, bank=bank: e.transpose(
                        out=P0[:, bank * 512 + j * 128: bank * 512 + (j + 1) * 128],
                        in_=XL(j)[:, fc * 128:(fc + 1) * 128],
                        identity=idT[:]),
                    waits=w, inc="sPE")
            ck_tp_g[g] = ck_pe
            ck_ev = p.add("scalar", lambda e, fc=fc, g=g, bank=bank: e.copy(
                out=xT[:, fc * 2048 + g * 512: fc * 2048 + (g + 1) * 512],
                in_=P0[:, bank * 512:(bank + 1) * 512]),
                waits=[("sPE", ck_pe)], inc="sACT")
            ck_xe_bank[bank] = ("sACT", ck_ev)
    ck_phase1_pe = ck_tp_g[3]
    ck_xT_done = [ck_xe_bank[b] for b in range(4)]
    for s in range(2):
        ck_q_ld[s] = p.dma("gpsimd", WQ(s), d_kq[s * 128:(s + 1) * 128, :],
                           waits=[("sDVE", ck_la)], sem=f"sQ{s % 2}")
    if tap:
        p.dma("gpsimd", tapd["t_xT"], xT[:], waits=ck_xT_done, sem="sTAP")

    wq_rot = 8
    ck_wo_cast = None

    def wo_slice(s):
        nonlocal ck_wo_cast
        if ck_wo_cast is None:
            w = [("sDVE", ck_q_red[6])]
        else:
            w = [("sACT", ck_wo_cast)]
        ck_ld = p.dma("sync", WQ(0), d_wo[s * 128:(s + 1) * 128, :],
                      waits=w, sem="sQW0")
        ck_wo_cast = p.add("scalar", lambda e, s=s: e.copy(
            out=woB[:, s * 1024:(s + 1) * 1024], in_=WQ(0)),
            waits=[("sQW0", ck_ld)], inc="sACT")

    Wr = [P0[:, 0:1024], P0[:, 1024:2048]]
    Kr = P1[:, 0:1024]
    Ur = P1[:, 1024:2048]
    Bq2[1] = U[:, 6144:7168]
    Bu2[1] = U[:, 8192:9216]
    Bb2 = U[:, 7168:8192]

    ck_castw = [None] * 3
    ck_wld = [None] * 3
    ck_wst_cast = [None] * 3
    ck_wbf_used = [None] * 3

    def w_srcs(c):
        wkr = d_wk.rearrange("(kc p) n -> p kc n", p=128)
        return [
            wkr[:, :, c * 128:(c + 1) * 128],
            wkr[:, :, 1024 + c * 128:1024 + (c + 1) * 128],
            d_wu.rearrange("(kc p) n -> p kc n",
                           p=128)[:, :, c * 128:(c + 1) * 128],
        ]

    def weights_dma(c):
        for i, src_ap in enumerate(w_srcs(c)):
            w = []
            if ck_wst_cast[i] is not None:
                w.append(("sACT", ck_wst_cast[i]))
            else:
                w.append(("sPE", ck_phase1_pe))
                w.append(("sDVE", ck_lb))
            ck_wld[i] = p.dma(
                "sync" if i == 0 else "scalar",
                XL(i).rearrange("p (kc n) -> p kc n", n=128),
                src_ap, waits=w, sem=f"sWS{i}")

    def weights_cast():
        for i in range(3):
            wpe = [(f"sWS{i}", ck_wld[i])]
            if ck_wbf_used[i] is not None:
                wpe.append(("sPE", ck_wbf_used[i]))
            ck_wst_cast[i] = ck_castw[i] = p.add(
                "scalar", lambda e, i=i: e.copy(out=wbf[i][:], in_=XL(i)),
                waits=wpe, inc="sACT")

    ck_g_pr = [None, None]
    ck_q_last = None
    ck_uev_last = None
    ck_carry_pr = [None, None]
    ck_castA_pr = [None, None]
    ck_fA_pr = [None, None]
    ck_casty = [None] * 8
    ck_castA = [None] * 8
    ck_cy = [None] * 8
    ck_cA = [None] * 8
    pend = []
    ck_ccinA = [None]
    ck_fy_last = None
    ck_qmax = [None]
    ck_qscale = [None]
    ck_qcols = [None]

    def emit_pend():
        while pend:
            pend.pop(0)()

    W0S = [U[:, 9216:10240], U[:, 7168:8192], U[:, 8192:9216]]
    ck_castw_c0 = [None] * 3
    for i, src_ap in enumerate(w_srcs(0)):
        ck_ld = p.dma(["sync", "scalar", "gpsimd"][i],
                      W0S[i].rearrange("p (kc n) -> p kc n", n=128),
                      src_ap, sem=f"sW0{i}")
        ck_castw_c0[i] = ck_castw[i] = p.add(
            "scalar", lambda e, i=i, W=W0S[i]: e.copy(out=wbf[i][:], in_=W),
            waits=[(f"sW0{i}", ck_ld)], inc="sACT")

    for c in range(8):
        for th in range(2):
            i = 2 * c + th
            pr = i % 2

            def mm_block(tgt, region, extra, th=th):
                ckl = None
                for kc in range(8):
                    for nt in range(2):
                        ckl = p.add(
                            "tensor",
                            lambda e, tgt=tgt, region=region, kc=kc, nt=nt,
                            th=th: e.matmul(
                                out=region[:, nt * 512:(nt + 1) * 512],
                                lhsT=wbf[tgt][:, kc * 128:(kc + 1) * 128],
                                rhs=xT[:, kc * 2048 + th * 1024 + nt * 512:
                                       kc * 2048 + th * 1024 +
                                       (nt + 1) * 512],
                                start=(kc == 0), stop=(kc == 7)),
                            waits=extra if (kc == 0 and nt == 0) else (),
                            inc="sPE" if (kc == 7 and nt == 1) else None)
                return ckl

            ew = [("sACT", ck_castw[0])] if th == 0 else []
            if i == 10:
                ew.append(("sACT", ck_qcols[0]))
            if ck_g_pr[pr] is not None:
                ew.append(("sACT", ck_g_pr[pr]))
            elif i == 0:
                ew += list(ck_xT_done)
            else:
                ew += list(ck_xT_done[2:4])
            ck_w = mm_block(0, Wr[pr], ew)

            ek = [("sACT", ck_castw[1])] if th == 0 else []
            if ck_q_last is not None:
                ek.append(("sACT", ck_q_last))
            ck_k = mm_block(1, Kr, ek)

            eu = [("sACT", ck_castw[2])] if th == 0 else []
            if ck_uev_last is not None:
                eu.append(("sDVE", ck_uev_last))
            elif i == 0:
                eu.append(("sACT", ck_bias_evac))
            ck_u = mm_block(2, Ur, eu)
            if th == 1:
                ck_wbf_used = [ck_w, ck_k, ck_u]

            if th == 0 and c < 7:
                weights_dma(c + 1)

            Bgx, Bqx, Bux = Bg2[pr], Bq2[pr], Bu2[pr]
            wg = [("sPE", ck_w)]
            if ck_carry_pr[pr] is not None:
                wg.append(("sDVE", ck_carry_pr[pr]))
            ck_g = p.add("scalar", lambda e, c=c, Bgx=Bgx, region=Wr[pr]:
                         e.activation(
                             out=Bgx[:], in_=region, func=AF.Sigmoid,
                             bias=bcol[:, c:c + 1], scale=1.0),
                         waits=wg, inc="sACT")
            ck_g_pr[pr] = ck_g
            emit_pend()
            wq2 = [("sPE", ck_k)]
            if ck_fA_pr[pr] is not None:
                wq2.append(("sDVE", ck_fA_pr[pr]))
            elif i == 0:
                wq2.append(("sPE", ck_bias_pe))
                wq2.append(("sDVE", ck_bpk2))
            ck_qe = p.add("scalar", lambda e, c=c, Bqx=Bqx: e.activation(
                out=Bqx[:], in_=Kr, func=AF.Sigmoid,
                bias=bcol[:, 8 + c:9 + c], scale=-2.0),
                waits=wq2, inc="sACT")
            ck_q_last = ck_qe
            if th == 1 and c < 7:
                weights_cast()
            if i >= 8:
                wo_slice(i - 8)
            if i == 8:
                passB_load(0, extra=[("sACT", ck_castw_c0[0])])
            if 10 <= i <= 13:
                quant_ph2(i - 10)
                passB_load(i - 9)

            wu = [("sPE", ck_u)]
            if ck_castA_pr[pr] is not None:
                wu.append(("sACT", ck_castA_pr[pr]))
            if i == 0:
                wu.append(("sACT", ck_bias_evac))
            if i == 1:
                wu.append(("sACT", ck_castw_c0[2]))
            ck_uev = p.add("vector", lambda e, c=c, Bux=Bux: e.tensor_scalar(
                out=Bux[:], in0=Ur, scalar1=bcol[:, 16 + c:17 + c],
                scalar2=None, op0=ALU.add), waits=wu, inc="sDVE")
            ck_uev_last = ck_uev

            ck_stt = p.add("vector", lambda e, Bgx=Bgx, Bqx=Bqx:
                           e.scalar_tensor_tensor(
                               out=Bb[:], in0=Bgx[:], scalar=-2.0,
                               in1=Bqx[:], op0=ALU.mult, op1=ALU.mult),
                           waits=[("sACT", ck_g), ("sACT", ck_qe)],
                           inc="sDVE")
            ck_aa = p.add("scalar", lambda e, Bqx=Bqx: e.activation(
                out=Bqx[:], in_=Bb[:], func=AF.Copy, bias=1.0, scale=1.0),
                waits=[("sDVE", ck_stt)], inc="sACT")
            wb = [("sDVE", ck_stt)]
            if i == 0:
                wb.append(("sACT", ck_castw_c0[1]))
            ck_b = p.add("vector", lambda e, Bgx=Bgx, Bux=Bux:
                         e.tensor_tensor(out=Bb2, in0=Bgx[:], in1=Bux[:],
                                         op=ALU.mult),
                         waits=wb, inc="sDVE")
            if th == 0:
                ini_y, wy_extra = 0.0, []
            else:
                ini_y = carry[:, c:c + 1]
                wy_extra = [("sDVE", ck_cy[c])]
            p.add("vector", lambda e, ini_y=ini_y, Bgx=Bgx, Bqx=Bqx:
                  e.tensor_tensor_scan(
                      out=Bgx[:], data0=Bqx[:], data1=Bb2, initial=ini_y,
                      op0=ALU.mult, op1=ALU.add),
                  waits=[("sDVE", ck_b), ("sACT", ck_aa)] + wy_extra,
                  inc="sDVE")
            ck_fy = ck_fy_last = flush_dve()
            wcy = [("sGL", ck_ccinA[0])] if i >= 14 else []
            ck_cy[c] = ck_carry_pr[pr] = p.add(
                "vector", lambda e, c=c, Bgx=Bgx: e.tensor_copy(
                    out=carry[:, c:c + 1], in_=Bgx[:, 1023:1024]),
                waits=wcy, inc="sDVE")
            if th == 0:
                ini_a = 1.0
                wa_extra = [("sDVE", ck_zt)] if i == 0 else []
            else:
                ini_a = carry[:, 8 + c:9 + c]
                wa_extra = [("sDVE", ck_cA[c])]
            p.add("vector", lambda e, ini_a=ini_a, Bqx=Bqx, Bux=Bux:
                  e.tensor_tensor_scan(
                      out=Bux[:], data0=Bqx[:], data1=zt[:], initial=ini_a,
                      op0=ALU.mult, op1=ALU.add),
                  waits=wa_extra, inc="sDVE")
            ck_fa = flush_dve()
            ck_fA_pr[pr] = ck_fa
            ck_cA[c] = p.add("vector", lambda e, c=c, Bux=Bux: e.tensor_copy(
                out=carry[:, 8 + c:9 + c], in_=Bux[:, 1023:1024]),
                inc="sDVE")
            if i < 8:
                passA_reduce(i)
                if i <= 5:
                    passA_load(i + 2)
            if i == 13:
                ck_fcarA = flush_dve()
                ck_ccinA[0] = p.dma("sync", cc_in.ap(), carry[:, 0:8],
                                    waits=[("sDVE", ck_fcarA)], sem="sGL")
                p.wait_only("gpsimd", [("sGL", ck_ccinA[0])])
                p.add("gpsimd", lambda e: e.collective_compute(
                    "AllGather", ALU.bypass,
                    replica_groups=[list(range(NCORES))],
                    ins=[cc_in.ap().opt()], outs=[cc_out.ap().opt()]),
                    inc="sCC")
            if i == 7:
                p.add("vector", lambda e: e.reduce_max(
                    out=qsc[:, 8:9], in_=qsc[:, 0:8], axis=AX.X,
                    apply_absolute_value=False), inc="sDVE")
                ck_qmax[0] = flush_dve()
            if i == 8:
                ck_qtp = p.add("tensor", lambda e: e.transpose(
                    out=P0[0:1, 16:144], in_=qsc[:, 8:9], identity=idT[:]),
                    waits=[("sDVE", ck_qmax[0]),
                           ("sACT", ck_g_pr[0])], inc="sPE")
                ck_qrow = p.add("scalar", lambda e: e.copy(
                    out=qrow[0:1, 0:128], in_=P0[0:1, 16:144]),
                    waits=[("sPE", ck_qtp)], inc="sACT")
                p.add("vector", lambda e: e.reduce_max(
                    out=qrow[0:1, 128:129], in_=qrow[0:1, 0:128],
                    axis=AX.X, apply_absolute_value=False),
                    waits=[("sACT", ck_qrow)], inc="sDVE")
                p.add("vector", lambda e: e.reciprocal(
                    out=qrow[0:1, 132:133], in_=qrow[0:1, 131:132]),
                    inc="sDVE")
                flush_dve()
                p.add("vector", lambda e: e.tensor_tensor(
                    out=qrow[0:1, 129:130], in0=qrow[0:1, 128:129],
                    in1=qrow[0:1, 132:133], op=ALU.mult), inc="sDVE")
                flush_dve()
                p.add("vector", lambda e: e.reciprocal(
                    out=qrow[0:1, 130:131], in_=qrow[0:1, 129:130]),
                    inc="sDVE")
                ck_qscale[0] = flush_dve()
            if i == 9:
                ck_qb2 = p.add("tensor", lambda e: e.matmul(
                    out=P0[:, 16:17], lhsT=onest[0:1, :],
                    rhs=qrow[0:1, 129:130], start=True, stop=True),
                    waits=[("sDVE", ck_qscale[0])], inc="sPE")
                ck_qb2 = p.add("tensor", lambda e: e.matmul(
                    out=P0[:, 17:18], lhsT=onest[0:1, :],
                    rhs=qrow[0:1, 130:131], start=True, stop=True),
                    inc="sPE")
                ck_qcols[0] = p.add("scalar", lambda e: e.copy(
                    out=qsc[:, 9:11], in_=P0[:, 16:18]),
                    waits=[("sPE", ck_qb2)], inc="sACT")

            def mk_casts(c=c, th=th, pr=pr, Bgx=Bgx, Bux=Bux,
                         ck_fy=ck_fy, ck_fa=ck_fa):
                def do():
                    ck_casty[c] = p.add("scalar", lambda e: e.copy(
                        out=yB[:, c * 2048 + th * 1024:
                               c * 2048 + (th + 1) * 1024], in_=Bgx[:]),
                        waits=[("sDVE", ck_fy)], inc="sACT")
                    ck_castA[c] = ck_castA_pr[pr] = p.add(
                        "scalar", lambda e: e.copy(
                            out=aB[:, c * 2048 + th * 1024:
                                   c * 2048 + (th + 1) * 1024], in_=Bux[:]),
                        waits=[("sDVE", ck_fa)], inc="sACT")
                return do
            pend.append(mk_casts())
    emit_pend()
    ck_pool_casty = p.cnt["sACT"]
    ck_qt_ld = {}
    qt_sem = {6: "sQT0", 7: "sQT1", 4: "sQT2", 5: "sQT3"}
    for s, buf in ((6, Bq), (7, Bu)):
        ck_qt_ld[s] = p.dma(
            "gpsimd", buf[:], d_kq[s * 128:(s + 1) * 128, :],
            waits=[("sDVE", ck_fA_pr[0]), ("sACT", ck_pool_casty)],
            sem=qt_sem[s])

    ck_qtail_prev = [None]

    def quant_tail(s, buf):
        nonlocal ck_quant_act
        ck_v = p.add("scalar", lambda e, buf=buf: e.activation(
            out=buf[:], in_=buf[:], func=AF.Copy, bias=0.0,
            scale=qsc[:, 10:11]),
            waits=[(qt_sem[s], ck_qt_ld[s])], inc="sACT")
        ck_r = None
        for hh in range(2):
            p.add("scalar", lambda e, buf=buf, hh=hh: e.copy(
                out=QT[:], in_=buf[:, hh * 512:(hh + 1) * 512]), inc="sACT")
            wt2 = []
            if hh == 0:
                wt2 = [("sDVE", ck_qtail_prev[0] if ck_qtail_prev[0]
                        is not None else ck_corr_c[7])]
            ck_r = p.add("scalar", lambda e, hh=hh: e.activation(
                out=Bb[:, hh * 512:(hh + 1) * 512], in_=QT[:],
                func=AF.Copy, bias=-1.0, scale=1.0),
                waits=wt2, inc="sACT")
        p.add("vector", lambda e, buf=buf: e.tensor_tensor(
            out=buf[:], in0=buf[:], in1=Bb[:], op=ALU.subtract),
            waits=[("sACT", ck_r)], inc="sDVE")
        p.add("vector", lambda e, buf=buf: e.scalar_tensor_tensor(
            out=Bb[:], in0=buf[:], scalar=1.5, in1=Bb[:],
            op0=ALU.is_gt, op1=ALU.add), inc="sDVE")
        p.add("vector", lambda e, buf=buf: e.scalar_tensor_tensor(
            out=Bb[:], in0=buf[:], scalar=0.5, in1=Bb[:],
            op0=ALU.is_ge, op1=ALU.add), inc="sDVE")
        ck_clip = p.add("vector", lambda e: e.tensor_scalar(
            out=Bb[:], in0=Bb[:], scalar1=-8.0, scalar2=7.0,
            op0=ALU.max, op1=ALU.min), inc="sDVE")
        ck_qtail_prev[0] = ck_clip
        ck_quant_act = p.add("scalar", lambda e, s=s: e.activation(
            out=kqB[:, s * 1024:(s + 1) * 1024], in_=Bb[:],
            func=AF.Copy, bias=0.0, scale=qsc[:, 9:10]),
            waits=[("sDVE", ck_clip)], inc="sACT")
    free_T = {0: ("sACT", ck_g_pr[0]), 1: ("sACT", ck_q_last)}

    ck_fcar = flush_dve()
    ck_ccinB = p.dma("sync", cc_in2.ap(), carry[:, 0:8],
                     waits=[("sDVE", ck_fcar)], sem="sGL")
    p.wait_only("gpsimd", [("sGL", ck_ccinB)])
    p.add("gpsimd", lambda e: e.collective_compute(
        "AllGather", ALU.bypass, replica_groups=[list(range(NCORES))],
        ins=[cc_in2.ap().opt()], outs=[cc_out2.ap().opt()]), inc="sCC")
    p.wait_only("gpsimd", [("sCC", 2)])
    ck_G = p.dma("sync", Gt,
                 cc_out.ap().rearrange("(r p) c -> r (p c)", p=128),
                 waits=[("sACT", ck_pool_casty), ("sCC", 1)], sem="sGL2")
    Gtv = Bg[0:8, 0:1024].rearrange("r (p c) -> r c p", c=8)
    ck_mm_sin = None
    for c8 in range(7):
        w = [("sGL2", ck_G), ("sS", 64)]
        if c8 == 0:
            w.append(("sACT", ck_q_last))
        ck_mm_sin = p.add("tensor", lambda e, c8=c8: e.matmul(
            out=P1[:, c8:c8 + 1],
            lhsT=Gtv[:, c8],
            rhs=mskT[0:8, 0:1], start=True, stop=True),
            waits=w, inc="sPE")
    p.add("scalar", lambda e: e.copy(
        out=carry[:, 16:23], in_=P1[:, 0:7]),
        waits=[("sPE", ck_mm_sin)], inc="sACT")
    ck_sin = flush_act()
    ck_corr_c = [None] * 8
    for c in range(7):
        ck_corr_c[c] = p.add(
            "vector", lambda e, c=c: e.scalar_tensor_tensor(
                out=yB[:, c * 2048:(c + 1) * 2048],
                in0=aB[:, c * 2048:(c + 1) * 2048],
                scalar=carry[:, 16 + c:17 + c],
                in1=yB[:, c * 2048:(c + 1) * 2048],
                op0=ALU.mult, op1=ALU.add),
            waits=[("sACT", ck_sin), ("sACT", ck_castA[c]),
                   ("sACT", ck_casty[c])], inc="sDVE")
    gB = {}

    def emit_gatherB():
        if gB:
            return
        ck_GB = p.dma("sync", Gt,
                      cc_out2.ap().rearrange("(r p) c -> r (p c)", p=128),
                      waits=[("sCC", 2), ("sPE", ck_mm_sin)], sem="sGL2")
        ck_mm_sinB = p.add("tensor", lambda e: e.matmul(
            out=P1[:, 7:8], lhsT=Gtv[:, 7],
            rhs=mskT[0:8, 0:1], start=True, stop=True),
            waits=[("sGL2", ck_GB)], inc="sPE")
        p.add("scalar", lambda e: e.copy(
            out=carry[:, 23:24], in_=P1[:, 7:8]),
            waits=[("sPE", ck_mm_sinB)], inc="sACT")
        gB["sinB"] = flush_act()
        ck_corr_c[7] = gB["corr"] = p.add(
            "vector", lambda e: e.scalar_tensor_tensor(
                out=yB[:, 7 * 2048:8 * 2048],
                in0=aB[:, 7 * 2048:8 * 2048],
                scalar=carry[:, 23:24],
                in1=yB[:, 7 * 2048:8 * 2048],
                op0=ALU.mult, op1=ALU.add),
            waits=[("sACT", gB["sinB"]), ("sACT", ck_castA[7]),
                   ("sACT", ck_casty[7])], inc="sDVE")
        quant_ph2(4)
        passB_load(5)
        quant_ph2(5)

    if tap:
        emit_gatherB()
        p.dma("gpsimd", tapd["t_y"], yB[:], waits=[("sDVE", gB["corr"])],
              sem="sTAP")
        p.dma("gpsimd", tapd["t_A"], aB[:], waits=[("sDVE", gB["corr"])],
              sem="sTAP")
        p.dma("gpsimd", tapd["t_carry"], carry[:], waits=[("sACT", ck_sin)], sem="sTAP")
        p.dma("gpsimd", tapd["t_wo"], woB[:], waits=[("sACT", ck_wo_cast)], sem="sTAP")
        p.wait_only("scalar", [("sTAP", p.cnt["sTAP"])])


    tp_rot = [None] * 4
    po_use = [None] * 4
    ln_state = {"lnt_pe": [None, None], "lnt_cast": [None, None],
                "rstt": [None, None], "std": None}

    LNWv = LNW.rearrange("p (a b) -> p a b", b=8)

    def ln_pair(pars, store_tiles=None, extra_apply_waits=()):
        for par in pars:
            p8 = par * 8
            p.add("vector", lambda e, par=par, p8=p8: e.scalar_tensor_tensor(
                out=SCR, in0=RT(par), scalar=1.0, in1=RT(par),
                op0=ALU.mult, op1=ALU.mult,
                accum_out=LNW[:, p8 + 2:p8 + 3]), inc="sDVE")
        p.add("vector", lambda e: e.tensor_tensor(
            out=LNWv[:, :, 3], in0=LNWv[:, :, 0], in1=LNWv[:, :, 1],
            op=ALU.add), inc="sDVE")
        p.add("vector", lambda e: e.tensor_scalar(
            out=LNWv[:, :, 4], in0=LNWv[:, :, 3],
            scalar1=1.0 / 1024.0, scalar2=None, op0=ALU.mult), inc="sDVE")
        p.add("vector", lambda e: e.scalar_tensor_tensor(
            out=LNWv[:, :, 5], in0=LNWv[:, :, 4], scalar=-1.0,
            in1=LNWv[:, :, 4], op0=ALU.mult, op1=ALU.mult), inc="sDVE")
        p.add("vector", lambda e: e.tensor_scalar(
            out=LNWv[:, :, 6], in0=LNWv[:, :, 5],
            scalar1=EPS, scalar2=None, op0=ALU.add), inc="sDVE")
        p.add("vector", lambda e: e.tensor_scalar(
            out=LNWv[:, :, 7], in0=LNWv[:, :, 2],
            scalar1=1.0 / 1024.0, scalar2=None, op0=ALU.mult), inc="sDVE")
        ck_b = p.add("vector", lambda e: e.tensor_tensor(
            out=LNWv[:, :, 7], in0=LNWv[:, :, 7], in1=LNWv[:, :, 6],
            op=ALU.add), inc="sDVE")
        p.add("scalar", lambda e: e.activation(
            out=LNWv[:, :, 5], in_=LNWv[:, :, 7],
            func=AF.Ln, bias=0.0, scale=1.0),
            waits=[("sDVE", ck_b)], inc="sACT")
        p.add("scalar", lambda e: e.activation(
            out=LNWv[:, :, 5], in_=LNWv[:, :, 5],
            func=AF.Exp, bias=0.0, scale=-0.5), inc="sACT")
        ck_std = flush_act()
        ln_state["std"] = ck_std
        cks = {}
        for par in pars:
            p8 = par * 8
            wl = list(extra_apply_waits)
            if store_tiles is not None:
                wl.append(("sST", p.cnt["sST"]))
            prev = ln_state["lnt_cast"][par]
            if prev is not None:
                wl.append(prev)
            prev_pe = ln_state["lnt_pe"][par]
            if prev_pe is not None:
                wl.append(("sPE", prev_pe))
            wl.append(("sACT", ck_std))
            cks[par] = p.add("vector", lambda e, par=par, p8=p8:
                             e.tensor_scalar(
                                 out=LNT(par), in0=RT(par),
                                 scalar1=LNW[:, p8 + 4:p8 + 5],
                                 scalar2=LNW[:, p8 + 5:p8 + 6],
                                 op0=ALU.subtract, op1=ALU.mult),
                             waits=wl, inc="sDVE")
        return cks

    p.dma("sync", U[0:1, 7168:8192], d_bo,
          waits=[("sDVE", ck_fy_last)], sem="sBR")
    p.dma("sync", U[32:33, 7168:8192], d_fb, sem="sBR")
    ck_brows = p.cnt["sBR"]
    bo_bf = brow[0:1, 0:1024]
    fb_bf = brow[0:1, 1024:2048]
    ones_bf = brow[0:1, 2048:2176]
    p.add("scalar", lambda e: e.copy(out=bo_bf, in_=U[0:1, 7168:8192]),
          waits=[("sBR", ck_brows)], inc="sACT")
    p.add("scalar", lambda e: e.copy(out=fb_bf, in_=U[32:33, 7168:8192]),
          inc="sACT")
    p.add("vector", lambda e: e.memset(ones_bf, 1.0), inc="sDVE")
    ck_brows_bf = p.cnt["sACT"]
    ck_ones_bf = p.cnt["sDVE"]

    ck_tile_rstt = [None, None]
    ck_xr = [None, None]
    ck_lnF = None
    tp_idx = 0

    def tok_tile_mms(tt, lhs_fn, rhs_tile, bias_row, extra_w, extra_last=(),
                     kc_waits=None, mid_hook=None):
        sl = tt % 2
        w = list(extra_w)
        if ck_tile_rstt[sl] is not None:
            w.append(("sDVE", ck_tile_rstt[sl]))
        for kc in range(8):
            if kc == 7 and mid_hook is not None:
                mid_hook()
            for h in range(2):
                wl = list(w) if (kc == 0 and h == 0) else []
                if kc_waits is not None and h == 0:
                    wl.append(kc_waits(kc))
                p.add("tensor", lambda e, kc=kc, h=h, sl=sl, tt=tt: e.matmul(
                    out=P0[:, sl * 1024 + h * 512: sl * 1024 + (h + 1) * 512],
                    lhsT=lhs_fn(kc, tt),
                    rhs=rhs_tile[:, kc * 1024 + h * 512:
                                 kc * 1024 + (h + 1) * 512],
                    start=(kc == 0), stop=False),
                    waits=wl, inc=None)
        ck = None
        for h in range(2):
            wl = [("sACT", ck_brows_bf), ("sDVE", ck_ones_bf)] + \
                list(extra_last)
            ck = p.add("tensor", lambda e, h=h, sl=sl, bias_row=bias_row:
                       e.matmul(
                           out=P0[:, sl * 1024 + h * 512:
                                  sl * 1024 + (h + 1) * 512],
                           lhsT=ones_bf,
                           rhs=bias_row[0:1, h * 512:(h + 1) * 512],
                           start=False, stop=True),
                       waits=wl if h == 0 else (),
                       inc="sPE" if h == 1 else None)
        return ck

    tp_jobs = []

    def emit_tp(tt, par, ck_apply):
        nonlocal ck_lnF, tp_idx
        ck_pe_last = None
        for kg in range(2):
            bank = 4 + (tp_idx % 4)
            tpw = [("sDVE", ck_apply)]
            if tp_rot[tp_idx % 4] is not None:
                tpw.append(tp_rot[tp_idx % 4])
            else:
                tpw.append(("sACT", gB["sinB"]))
            ck_pe = None
            for i in range(4):
                ck_pe = p.add("tensor", lambda e, kg=kg, i=i, par=par,
                              bank=bank: e.transpose(
                                  out=P1[:, (bank - 4) * 512 + i * 128:
                                         (bank - 4) * 512 +
                                         (i + 1) * 128],
                                  in_=LNT(par)[:, (kg * 4 + i) * 128:
                                               (kg * 4 + i + 1) * 128],
                                  identity=idT[:]),
                              waits=tpw if i == 0 else (), inc="sPE")
            ck_lnF = p.add("scalar", lambda e, kg=kg, tt=tt, bank=bank:
                           e.copy(
                               out=xT.rearrange(
                                   "p (kc t) -> p kc t",
                                   t=2048)[:, kg * 4:(kg + 1) * 4,
                                           tt * 128:(tt + 1) * 128],
                               in_=P1[:, (bank - 4) * 512:
                                      (bank - 4 + 1) * 512].rearrange(
                                   "p (kc t) -> p kc t", t=128)),
                           waits=[("sPE", ck_pe)], inc="sACT")
            tp_rot[tp_idx % 4] = ("sACT", ck_lnF)
            tp_idx += 1
            ck_pe_last = ck_pe
        ln_state["lnt_pe"][par] = ck_pe_last

    for pr in range(8):
        tiles = [2 * pr, 2 * pr + 1]
        p.add("vector", lambda e: e.memset(LNWv[:, :, 0:3], 0.0),
              waits=[("sACT", ln_state["std"])], inc="sDVE")
        for tt in tiles:
            sl = tt % 2
            extra_w = [("sACT", ck_wo_cast)]

            if tt == 1:
                extra_w.append(("sACT", ck_g_pr[1]))
            ck_pe = tok_tile_mms(
                tt, lambda kc, tt: yB[:, kc * 2048 + tt * 128:
                                      kc * 2048 + (tt + 1) * 128],
                woB, bo_bf, extra_w,
                kc_waits=lambda kc: ("sDVE", ck_corr_c[kc]),
                mid_hook=emit_gatherB if tt == 0 else None)
            w = []
            if ck_tile_rstt[sl] is not None:
                w.append(("sDVE", ck_tile_rstt[sl]))
            if tt < 2:
                w.append(("sACT", ck_wo_cast))
            if tt == 1:
                w.append(("sACT", ck_quant_act))
            ck_xr[sl] = p.dma("sync", XR(sl),
                              d_x[tt * 128:(tt + 1) * 128, :],
                              waits=w, sem=f"sXR{sl}")
            ck_rstt = None
            for h in range(2):
                ck_rstt = p.add("vector", lambda e, h=h, sl=sl, p8=sl * 8:
                                e.scalar_tensor_tensor(
                                    out=RT(sl)[:, h * 512:(h + 1) * 512],
                                    in0=P0[:, sl * 1024 + h * 512:
                                           sl * 1024 + (h + 1) * 512],
                                    scalar=1.0,
                                    in1=XR(sl)[:, h * 512:(h + 1) * 512],
                                    op0=ALU.mult, op1=ALU.add,
                                    accum_out=LNW[:, p8 + h:p8 + h + 1]),
                                waits=[("sPE", ck_pe),
                                       (f"sXR{sl}", ck_xr[sl])],
                                inc="sDVE")
            ck_tile_rstt[sl] = ck_rstt
            ln_state["rstt"][sl] = ck_rstt
        for job in tp_jobs:
            emit_tp(*job)
        tp_jobs = []
        cks = ln_pair([0, 1])
        for tt in tiles:
            par = tt % 2
            ck_apply = cks[par]
            ck_cast = p.add("scalar", lambda e, tt=tt, par=par: e.copy(
                out=aB[:, tt * 1024:(tt + 1) * 1024], in_=LNT(par)),
                waits=[("sDVE", ck_apply), ("sDVE", ck_corr_c[tt // 2])],
                inc="sACT")
            ln_state["lnt_cast"][par] = ("sACT", ck_cast)
            tp_jobs.append((tt, par, ck_apply))
        if pr == 1:
            quant_tail(6, Bq)
        if pr == 2:
            quant_tail(7, Bu)
    for job in tp_jobs:
        emit_tp(*job)
    tp_jobs = []

    ck_z_ev = None
    for nt in range(4):
        bank = nt % 4
        w = [("sACT", ck_lnF),
             ("sDVE", ck_tile_rstt[0]), ("sDVE", ck_tile_rstt[1])]
        ck_pe = None
        for kc in range(8):
            ck_pe = p.add("tensor", lambda e, kc=kc, nt=nt, bank=bank: e.matmul(
                out=P0[0:16, bank * 512:(bank + 1) * 512],
                lhsT=laB[:, kc * 16:(kc + 1) * 16],
                rhs=xT[:, kc * 2048 + nt * 512: kc * 2048 + (nt + 1) * 512],
                start=(kc == 0), stop=(kc == 7)),
                waits=w if kc == 0 else (), inc="sPE" if kc == 7 else None)
        ck_z_ev = p.add("scalar", lambda e, nt=nt, bank=bank: e.copy(
            out=Z(nt), in_=P0[0:16, bank * 512:(bank + 1) * 512]),
            waits=[("sPE", ck_pe)], inc="sACT")
    ck_tile_rstt = [None, None]

    for pr in range(8):
        tiles = [2 * pr, 2 * pr + 1]
        p.add("vector", lambda e: e.memset(LNWv[:, :, 0:3], 0.0),
              waits=[("sACT", ln_state["std"])], inc="sDVE")
        for tt in tiles:
            sl = tt % 2
            nt = tt // 4
            extra_w = [("sACT", ck_quant_act), ("sACT", ck_z_ev)]
            w = list(extra_w)
            if ck_tile_rstt[sl] is not None:
                w.append(("sDVE", ck_tile_rstt[sl]))
            for kc in range(8):
                for h in range(2):
                    p.add("tensor", lambda e, kc=kc, h=h, sl=sl, tt=tt:
                          e.matmul(
                              out=P0[:, sl * 1024 + h * 512:
                                     sl * 1024 + (h + 1) * 512],
                              lhsT=xT[:, kc * 2048 + tt * 128:
                                      kc * 2048 + (tt + 1) * 128],
                              rhs=kqB[:, kc * 1024 + h * 512:
                                      kc * 1024 + (h + 1) * 512],
                              start=(kc == 0), stop=False),
                          waits=w if (kc == 0 and h == 0) else (), inc=None)
            for h in range(2):
                p.add("tensor", lambda e, h=h, sl=sl, tt=tt, nt=nt: e.matmul(
                    out=P0[:, sl * 1024 + h * 512: sl * 1024 + (h + 1) * 512],
                    lhsT=Z(nt)[:, (tt % 4) * 128:(tt % 4 + 1) * 128],
                    rhs=lbB[0:16, h * 512:(h + 1) * 512],
                    start=False, stop=False),
                    waits=[("sDVE", ck_lb)] if h == 0 else (), inc=None)
            ck_pe = None
            for h in range(2):
                ck_pe = p.add("tensor", lambda e, h=h, sl=sl: e.matmul(
                    out=P0[:, sl * 1024 + h * 512: sl * 1024 + (h + 1) * 512],
                    lhsT=ones_bf,
                    rhs=fb_bf[0:1, h * 512:(h + 1) * 512],
                    start=False, stop=True),
                    waits=[("sACT", ck_brows_bf), ("sDVE", ck_ones_bf)]
                    if h == 0 else (), inc="sPE" if h == 1 else None)
            ck_rstt = None
            for h in range(2):
                ck_rstt = p.add("vector", lambda e, h=h, sl=sl, tt=tt,
                                p8=sl * 8: e.scalar_tensor_tensor(
                                    out=RT(sl)[:, h * 512:(h + 1) * 512],
                                    in0=P0[:, sl * 1024 + h * 512:
                                           sl * 1024 + (h + 1) * 512],
                                    scalar=1.0,
                                    in1=aB[:, tt * 1024 + h * 512:
                                           tt * 1024 + (h + 1) * 512],
                                    op0=ALU.mult, op1=ALU.add,
                                    accum_out=LNW[:, p8 + h:p8 + h + 1]),
                                waits=[("sPE", ck_pe)], inc="sDVE")
            ck_tile_rstt[sl] = ck_rstt
        cks = ln_pair([0, 1], store_tiles=True)
        for tt in tiles:
            par = tt % 2
            ck_st = p.dma("scalar", d_out[tt * 128:(tt + 1) * 128, :],
                          LNT(par), waits=[("sDVE", cks[par])], sem="sST")
            ln_state["lnt_cast"][par] = ("sST", ck_st)
            ln_state["lnt_pe"][par] = None

    if tap:
        lc = ln_state["lnt_cast"]
        p.dma("gpsimd", tapd["t_ln1"], aB[:],
              waits=[("sACT", p.cnt["sACT"]), ("sDVE", p.cnt["sDVE"])],
              sem="sTAP")
        p.dma("gpsimd", tapd["t_lnf"], xT[:],
              waits=[("sACT", p.cnt["sACT"])], sem="sTAP")
        p.dma("gpsimd", tapd["t_kq"], kqB[:],
              waits=[("sDVE", p.cnt["sDVE"]), ("sACT", p.cnt["sACT"])],
              sem="sTAP")
        p.dma("gpsimd", tapd["t_qs"], qsc[:, 0:11],
              waits=[("sDVE", p.cnt["sDVE"])], sem="sTAP")
        p.dma("gpsimd", tapd["t_qrow"], qrow[0:1, 0:133],
              waits=[("sDVE", p.cnt["sDVE"])], sem="sTAP")

    p.wait_only("sync", [("sST", p.cnt["sST"])])
    p.wait_only("gpsimd", [("sST", p.cnt["sST"])])
    p.wait_only("gpsimd", [("sGL", p.cnt["sGL"])])
    p.wait_only("gpsimd", [("sGL2", p.cnt["sGL2"])])
    if tap:
        p.wait_only("gpsimd", [("sTAP", p.cnt["sTAP"])])

    with nc.Block() as block:
        p.emit(block)
    return nc


_NC_CACHE = None


def kernel(L0_attn_x, L0_attn_Wp, L0_attn_bp, L0_attn_Wu, L0_attn_bu,
           L0_attn_Wo, L0_attn_bo, L0_ffn_kernel, L0_ffn_bias, L0_ffn_a,
           L0_ffn_b):
    global _NC_CACHE
    x = np.ascontiguousarray(np.asarray(L0_attn_x, dtype=np.float32))
    wk = np.ascontiguousarray(np.asarray(L0_attn_Wp, dtype=np.float32)[:, :2048])
    wu = np.ascontiguousarray(np.asarray(L0_attn_Wu, dtype=np.float32))
    wo = np.ascontiguousarray(np.asarray(L0_attn_Wo, dtype=np.float32))
    kq = np.ascontiguousarray(np.asarray(L0_ffn_kernel, dtype=np.float32))
    la = np.ascontiguousarray(np.asarray(L0_ffn_a, dtype=np.float32))
    lb = np.ascontiguousarray(np.asarray(L0_ffn_b, dtype=np.float32))
    bp = np.asarray(L0_attn_bp, dtype=np.float32)
    bpw = np.ascontiguousarray(bp[:1024].reshape(1, 1024))
    bpk = np.ascontiguousarray(bp[1024:2048].reshape(1, 1024))
    bu = np.ascontiguousarray(np.asarray(L0_attn_bu, np.float32).reshape(1, 1024))
    bo = np.ascontiguousarray(np.asarray(L0_attn_bo, np.float32).reshape(1, 1024))
    fb = np.ascontiguousarray(np.asarray(L0_ffn_bias, np.float32).reshape(1, 1024))
    ident = np.eye(128, dtype=np.float32)

    if _NC_CACHE is None:
        _NC_CACHE = build_nc()
    nc = _NC_CACHE

    in_maps = []
    for c in range(NCORES):
        b, h = c // 2, c % 2
        mask = np.zeros((8, 1), np.float32)
        if h == 1:
            mask[c - 1, 0] = 1.0
        in_maps.append(dict(
            xc=np.ascontiguousarray(x[b, h * TC:(h + 1) * TC, :]),
            wk=wk, wu=wu, wo=wo, kq=kq, la=la, lb=lb,
            bpw=bpw, bpk=bpk, bu=bu, bo=bo, fb=fb,
            mask=mask, ident=ident))
    res = run_bass_kernel_spmd(nc, in_maps, core_ids=list(range(NCORES)))
    out = np.empty((4, 4096, 1024), dtype=np.float32)
    for c in range(NCORES):
        b, h = c // 2, c % 2
        out[b, h * TC:(h + 1) * TC, :] = np.asarray(res.results[c]["yout"])
    return out


# version 27
# speedup vs baseline: 1.0124x; 1.0124x over previous
import contextlib
from collections import defaultdict

import numpy as np
import concourse.bass as bass
import concourse.mybir as mybir
from concourse.bass_utils import run_bass_kernel_spmd

F32 = mybir.dt.float32
BF16 = mybir.dt.bfloat16
I32 = mybir.dt.int32
AF = mybir.ActivationFunctionType
ALU = mybir.AluOpType
AX = mybir.AxisListType

NCORES = 8
D = 1024
TC = 2048
TH = 1024
EPS = 1e-5


class Plan:

    ENGINES = ("sync", "tensor", "scalar", "vector", "gpsimd")

    def __init__(self, nc, ctx):
        self.nc = nc
        self.steps = {e: [] for e in self.ENGINES}
        self.cnt = defaultdict(int)
        self.water = defaultdict(int)
        self.own = defaultdict(int)
        self.sems = {}
        self._ctx = ctx
        for s in ("sPE", "sACT", "sDVE", "sPOOL", "sCC"):
            self._sem(s)

    def _sem(self, s):
        if s not in self.sems:
            self.sems[s] = self._ctx.enter_context(self.nc.semaphore(s))
        return self.sems[s]

    def _mk_waits(self, engine, waits):
        out = []
        for s, v in waits:
            if v is None or v <= 0:
                continue
            if v <= self.water[(engine, s)]:
                continue
            self._sem(s)
            self.water[(engine, s)] = v
            out.append((self.sems[s], v))
        return out

    OWN = {"vector": "sDVE", "scalar": "sACT", "gpsimd": "sPOOL"}

    def add(self, engine, fn, waits=(), inc=None, inc_n=1, is_dma=False):
        waits = list(waits)
        own = self.OWN.get(engine)
        if own is not None and not is_dma:
            prev = self.own[(engine, own)]
            if prev > 0:
                waits.insert(0, (own, prev))
            if inc is None:
                inc = own
        wl = self._mk_waits(engine, waits)
        ck = None
        ih = None
        if inc is not None:
            ih = self._sem(inc)
            self.cnt[inc] += inc_n
            self.own[(engine, inc)] += inc_n
            ck = self.cnt[inc]

        def step(eng, fn=fn, wl=wl, ih=ih, inc_n=inc_n):
            for h, v in wl:
                eng.wait_ge(h, v)
            r = fn(eng)
            if ih is not None:
                r.then_inc(ih, inc_n)
        self.steps[engine].append(step)
        return ck

    def dma(self, queue, out_ap, in_ap, waits=(), sem=None, transpose=False,
            **kw):
        assert sem is not None
        if transpose:
            fn = lambda eng, o=out_ap, i=in_ap: eng.dma_start_transpose(
                out=o, in_=i)
        else:
            fn = lambda eng, o=out_ap, i=in_ap, kw=kw: eng.dma_start(
                out=o, in_=i, **kw)
        return self.add(queue, fn, waits=waits, inc=sem, inc_n=16,
                        is_dma=True)

    def wait_only(self, engine, waits):
        wl = self._mk_waits(engine, list(waits))
        if not wl:
            return

        def step(eng, wl=wl):
            for h, v in wl:
                eng.wait_ge(h, v)
        self.steps[engine].append(step)

    def emit(self, block):
        for e in self.ENGINES:
            body = self.steps[e]

            def runner(eng, body=body):
                for f in body:
                    f(eng)
            getattr(block, e)(runner)


def build_nc(tap=False):
    nc = bass.Bass("TRN2", target_bir_lowering=False)
    ctx = contextlib.ExitStack()
    nc._keepalive = ctx

    d_x = nc.dram_tensor("xc", [TC, D], F32, kind="ExternalInput").ap()
    d_wk = nc.dram_tensor("wk", [D, 2 * D], F32, kind="ExternalInput").ap()
    d_wu = nc.dram_tensor("wu", [D, D], F32, kind="ExternalInput").ap()
    d_wo = nc.dram_tensor("wo", [D, D], F32, kind="ExternalInput").ap()
    d_kq = nc.dram_tensor("kq", [D, D], F32, kind="ExternalInput").ap()
    d_la = nc.dram_tensor("la", [D, 16], F32, kind="ExternalInput").ap()
    d_lb = nc.dram_tensor("lb", [16, D], F32, kind="ExternalInput").ap()
    d_bpw = nc.dram_tensor("bpw", [1, D], F32, kind="ExternalInput").ap()
    d_bpk = nc.dram_tensor("bpk", [1, D], F32, kind="ExternalInput").ap()
    d_bu = nc.dram_tensor("bu", [1, D], F32, kind="ExternalInput").ap()
    d_bo = nc.dram_tensor("bo", [1, D], F32, kind="ExternalInput").ap()
    d_fb = nc.dram_tensor("fb", [1, D], F32, kind="ExternalInput").ap()
    d_mask = nc.dram_tensor("mask", [8, 1], F32, kind="ExternalInput").ap()
    d_id = nc.dram_tensor("ident", [128, 128], F32, kind="ExternalInput").ap()
    d_out = nc.dram_tensor("yout", [TC, D], F32, kind="ExternalOutput").ap()

    cc_in = nc.dram_tensor("cc_in", [128, 8], F32)
    cc_out = nc.dram_tensor("cc_out", [1024, 8], F32)
    cc_in2 = nc.dram_tensor("cc_in2", [128, 8], F32)
    cc_out2 = nc.dram_tensor("cc_out2", [1024, 8], F32)
    tapd = {}
    if tap:
        for nm, shape, dt in [
                ("t_xT", [128, 16384], BF16), ("t_y", [128, 16384], BF16),
                ("t_A", [128, 16384], BF16), ("t_carry", [128, 24], F32),
                ("t_kq", [128, 8192], BF16), ("t_wo", [128, 8192], BF16),
                ("t_ln1", [128, 16384], BF16),
                ("t_lnf", [128, 16384], BF16), ("t_qs", [128, 11], F32),
                ("t_qrow", [1, 133], F32)]:
            tapd[nm] = nc.dram_tensor(nm, shape, dt, kind="ExternalOutput").ap()

    def sb(name, shape, dt=F32):
        return ctx.enter_context(nc.sbuf_tensor(name, shape, dt))

    xT = sb("xT", [128, 8 * TC], BF16)
    yB = sb("yB", [128, 8 * TC], BF16)
    aB = sb("aB", [128, 8 * TC], BF16)
    U = sb("U", [128, 10240], F32)
    QT = sb("QT", [128, 512], I32)
    Bg = sb("Bg", [128, TH], F32)
    Bq = sb("Bq", [128, TH], F32)
    Bu = sb("Bu", [128, TH], F32)
    Bb = sb("Bb", [128, TH], F32)
    Bg2 = [Bg, None]

    def Z(nt):
        return wbf[nt // 2][0:16, (nt % 2) * 512:(nt % 2 + 1) * 512]
    Bq2 = [Bq, Bq]
    Bu2 = [Bu, Bu]
    zt = sb("zt", [128, TH], BF16)
    brow = sb("brow", [1, 2304], BF16)
    wbf = [sb(f"wbf{i}", [128, 1024], BF16) for i in range(3)]
    woB = sb("woB", [128, 8 * D], BF16)
    kqB = sb("kqB", [128, 8 * D], BF16)
    laB = sb("laB", [128, 128], BF16)
    lbB = sb("lbB", [16, D], BF16)
    bcol = sb("bcol", [128, 48], F32)
    carry = sb("carry", [128, 24], F32)
    slocT = sb("slocT", [8, 128], F32)
    mskT = sb("mskT", [8, 1], F32)
    idT = sb("idT", [128, 128], F32)
    onest = sb("onest", [1, 128], F32)
    qsc = sb("qsc", [128, 12], F32)
    qrow = sb("qrow", [1, 136], F32)
    LNW = sb("LNW", [128, 16], F32)
    flt = sb("flt", [128, 2], F32)
    Gt = Bg[0:8, 0:1024]
    rowb = Bq[0:1, 0:1024]

    def XL(j):
        return U[:, j * 1024:(j + 1) * 1024]

    def WQ(j):
        return U[:, 4096 + (j % 2) * 1024: 5120 + (j % 2) * 1024]

    def MO(d):
        return U[:, d * 512:(d + 1) * 512]

    def XR(j):
        return U[:, 4096 + (j % 2) * 1024: 5120 + (j % 2) * 1024]

    def RT(j):
        return U[:, 6144 + (j % 2) * 1024: 7168 + (j % 2) * 1024]

    def LNT(j):
        return U[:, 8192 + (j % 2) * 1024: 9216 + (j % 2) * 1024]

    SCR = Bg[:, 0:1024]
    Bg2[1] = U[:, 3 * 1024:4 * 1024]

    P0 = ctx.enter_context(nc.psum_tensor("P0", [128, 2048], F32))
    P1 = ctx.enter_context(nc.psum_tensor("P1", [128, 2048], F32))

    p = Plan(nc, ctx)

    def flush_dve(waits=()):
        return p.add("vector", lambda e: e.tensor_copy(out=flt[:, 0:1],
                                                       in_=flt[:, 1:2]),
                     waits=waits, inc="sDVE")

    def flush_act(waits=()):
        return p.add("scalar", lambda e: e.copy(out=flt[:, 0:1],
                                                in_=flt[:, 1:2]),
                     waits=waits, inc="sACT")

    p.dma("sync", idT[:], d_id, sem="sS")
    p.dma("sync", mskT[:], d_mask, sem="sS")
    p.dma("sync", U[0:16, 0:1024], d_lb, sem="sS")
    ck_lb = p.add("vector",
                  lambda e: e.tensor_copy(out=lbB[:], in_=U[0:16, 0:1024]),
                  waits=[("sS", 48)], inc="sDVE")
    p.dma("sync",
          U[:, 1024:1152].rearrange("p (kc j) -> p kc j", j=16),
          d_la.rearrange("(kc p) j -> p kc j", p=128),
          waits=[("sDVE", ck_lb)], sem="sS")
    ck_la = p.add("vector",
                  lambda e: e.tensor_copy(out=laB[:], in_=U[:, 1024:1152]),
                  waits=[("sS", 64)], inc="sDVE")
    p.add("vector", lambda e: e.memset(flt[:], 0.0), inc="sDVE")
    p.add("vector", lambda e: e.memset(onest[:], 1.0), inc="sDVE")
    p.add("vector", lambda e: e.memset(zt[:], 0.0), inc="sDVE")
    p.add("vector", lambda e: e.memset(carry[:], 0.0), inc="sDVE")
    p.add("vector", lambda e: e.memset(qrow[0:1, 131:132], 7.0), inc="sDVE")
    ck_zt = p.cnt["sDVE"]

    for v, dsrc in enumerate([d_bpw, d_bpk, d_bu, d_bo, d_fb]):
        ck_row = p.dma("sync", rowb, dsrc, sem="sB")
        ck_pe = None
        for c in range(8):
            ck_pe = p.add(
                "tensor",
                lambda e, v=v, c=c: e.transpose(
                    out=P1[:, 1536 + v * 8 + c: 1537 + v * 8 + c],
                    in_=Bq[0:1, c * 128:(c + 1) * 128],
                    identity=idT[0:1, 0:1]),
                waits=[("sB", ck_row), ("sS", 64)], inc="sPE")
        ck_bias_pe = ck_pe
        ck_bias_evac = p.add(
            "scalar",
            lambda e, v=v: e.copy(out=bcol[:, v * 8:(v + 1) * 8],
                                  in_=P1[:, 1536 + v * 8: 1544 + v * 8]),
            waits=[("sPE", ck_pe)], inc="sACT")
        p.wait_only("sync", [("sPE", ck_pe)])
    p.add("vector",
          lambda e: e.tensor_scalar(out=bcol[:, 8:16], in0=bcol[:, 8:16],
                                    scalar1=-2.0, scalar2=None, op0=ALU.mult),
          waits=[("sACT", ck_bias_evac)], inc="sDVE")
    ck_bpk2 = p.cnt["sDVE"]

    ck_q_ld = [None] * 8
    ck_q_red = [None] * 8

    def passA_load(s):
        ck_q_ld[s] = p.dma("gpsimd", WQ(s), d_kq[s * 128:(s + 1) * 128, :],
                           waits=[("sDVE", ck_q_red[s - 2])], sem=f"sQ{s % 2}")

    def passA_reduce(s):
        ck_q_red[s] = p.add(
            "vector",
            lambda e, s=s: e.reduce_max(out=qsc[:, s:s + 1], in_=WQ(s),
                                        axis=AX.X, apply_absolute_value=True),
            waits=[(f"sQ{s % 2}", ck_q_ld[s])], inc="sDVE")

    QB = U[:, 9216:10240]
    T2 = U[:, 5120:6144]
    ck_qb_ld = [None] * 8
    ck_quant_act = None
    ck_qd_prev = None
    ck_t2_prev = None

    def passB_load(s, extra=()):
        w = list(extra)
        if ck_qd_prev is not None:
            w.append(("sDVE", ck_qd_prev))
        ck_qb_ld[s] = p.dma("gpsimd", QB, d_kq[s * 128:(s + 1) * 128, :],
                            waits=w, sem="sQL0")

    def quant_ph2(s, extra=()):
        nonlocal ck_quant_act, ck_qd_prev, ck_t2_prev
        ck_v = p.add("scalar", lambda e: e.activation(
            out=QB, in_=QB, func=AF.Copy, bias=0.0, scale=qsc[:, 10:11]),
            waits=[("sQL0", ck_qb_ld[s])] + list(extra), inc="sACT")
        ck_r = None
        wt2 = [("sDVE", ck_t2_prev)] if ck_t2_prev is not None             else [("sDVE", ck_q_red[7])]
        for hh in range(2):
            p.add("scalar", lambda e, hh=hh: e.copy(
                out=QT[:], in_=QB[:, hh * 512:(hh + 1) * 512]), inc="sACT")
            ck_r = p.add("scalar", lambda e, hh=hh: e.activation(
                out=T2[:, hh * 512:(hh + 1) * 512], in_=QT[:],
                func=AF.Copy, bias=-1.0, scale=1.0),
                waits=wt2 if hh == 0 else (), inc="sACT")
        p.add("vector", lambda e: e.tensor_tensor(
            out=QB, in0=QB, in1=T2, op=ALU.subtract),
            waits=[("sACT", ck_r)], inc="sDVE")
        p.add("vector", lambda e: e.scalar_tensor_tensor(
            out=T2, in0=QB, scalar=1.5, in1=T2,
            op0=ALU.is_gt, op1=ALU.add), inc="sDVE")
        ck_qd_prev = p.add("vector", lambda e: e.scalar_tensor_tensor(
            out=T2, in0=QB, scalar=0.5, in1=T2,
            op0=ALU.is_ge, op1=ALU.add), inc="sDVE")
        ck_t2_prev = ck_clip = p.add("vector", lambda e: e.tensor_scalar(
            out=T2, in0=T2, scalar1=-8.0, scalar2=7.0,
            op0=ALU.max, op1=ALU.min), inc="sDVE")
        ck_quant_act = p.add("scalar", lambda e, s=s: e.activation(
            out=kqB[:, s * 1024:(s + 1) * 1024], in_=T2,
            func=AF.Copy, bias=0.0, scale=qsc[:, 9:10]),
            waits=[("sDVE", ck_clip)], inc="sACT")

    def w_srcs(c):
        wkr = d_wk.rearrange("(kc p) n -> p kc n", p=128)
        return [
            wkr[:, :, c * 128:(c + 1) * 128],
            wkr[:, :, 1024 + c * 128:1024 + (c + 1) * 128],
            d_wu.rearrange("(kc p) n -> p kc n",
                           p=128)[:, :, c * 128:(c + 1) * 128],
        ]

    W0S = [U[:, 9216:10240], U[:, 7168:8192], U[:, 8192:9216]]
    ck_castw_c0 = [None] * 3
    for i, src_ap in enumerate(w_srcs(0)):
        ck_ld = p.dma(["sync", "scalar", "gpsimd"][i],
                      W0S[i].rearrange("p (kc n) -> p kc n", n=128),
                      src_ap, sem=f"sW0{i}")
        ck_castw_c0[i] = p.add(
            "scalar", lambda e, i=i, W=W0S[i]: e.copy(out=wbf[i][:], in_=W),
            waits=[(f"sW0{i}", ck_ld)], inc="sACT")

    XG3 = [U[:, 6144:7168], U[:, 7168:8192], U[:, 8192:9216],
           U[:, 9216:10240]]
    xg3_w = [[], [("sACT", ck_castw_c0[1])], [("sACT", ck_castw_c0[2])],
             [("sACT", ck_castw_c0[0])]]

    ck_xl = [None] * 4
    ck_xq = {}
    ck_tp_g = [None] * 4
    ck_xe_bank = [None] * 4
    ck_xg3 = {}
    for g in range(4):
        for j in range(4):
            if g == 3:
                w = list(xg3_w[j])
            else:
                w = []
                if g == 0 and j == 0:
                    w = [("sDVE", ck_lb)]
                if g == 0 and j == 1:
                    w = [("sDVE", ck_la)]
                if g > 0:
                    w = [("sPE", ck_tp_g[g - 1])]
            qn, sn = [("sync", "sX"), ("scalar", "sX2"),
                      ("gpsimd", "sX3")][(4 * g + j) % 3]
            dst = XL(j)
            if g == 3:
                dst = XG3[j]
                sn = "sY" + sn[2:].replace("X", "0")
            ck_xl[j] = p.dma(
                qn, dst,
                d_x[(4 * g + j) * 128:(4 * g + j + 1) * 128, :],
                waits=w, sem=sn)
            if g == 3:
                ck_xg3[sn] = ck_xl[j]
            else:
                ck_xq[sn] = ck_xl[j]
        if g == 3:
            break
        for fc in range(8):
            bank = (g * 8 + fc) % 4
            ck_pe = None
            for j in range(4):
                w = [(sn, ck) for sn, ck in ck_xq.items()]
                if ck_xe_bank[bank] is not None:
                    w.append(ck_xe_bank[bank])
                ck_pe = p.add(
                    "tensor",
                    lambda e, j=j, fc=fc, bank=bank: e.transpose(
                        out=P0[:, bank * 512 + j * 128: bank * 512 + (j + 1) * 128],
                        in_=XL(j)[:, fc * 128:(fc + 1) * 128],
                        identity=idT[:]),
                    waits=w, inc="sPE")
            ck_tp_g[g] = ck_pe
            ck_ev = p.add("scalar", lambda e, fc=fc, g=g, bank=bank: e.copy(
                out=xT[:, fc * 2048 + g * 512: fc * 2048 + (g + 1) * 512],
                in_=P0[:, bank * 512:(bank + 1) * 512]),
                waits=[("sPE", ck_pe)], inc="sACT")
            ck_xe_bank[bank] = ("sACT", ck_ev)
    ck_phase1_pe = ck_tp_g[2]
    ck_xT_done = [ck_xe_bank[b] for b in range(4)]
    ck_tp3 = [None]

    def emit_tp_g3():
        for fc in range(8):
            bank = 2 + fc % 2
            ck_pe = None
            for j in range(4):
                w = [(sn, ck) for sn, ck in ck_xg3.items()]
                if ck_xe_bank[bank] is not None:
                    w.append(ck_xe_bank[bank])
                ck_pe = p.add(
                    "tensor",
                    lambda e, j=j, fc=fc, bank=bank: e.transpose(
                        out=P0[:, bank * 512 + j * 128:
                               bank * 512 + (j + 1) * 128],
                        in_=XG3[j][:, fc * 128:(fc + 1) * 128],
                        identity=idT[:]),
                    waits=w, inc="sPE")
            ck_ev = p.add("scalar", lambda e, fc=fc, bank=bank: e.copy(
                out=xT[:, fc * 2048 + 3 * 512: fc * 2048 + 4 * 512],
                in_=P0[:, bank * 512:(bank + 1) * 512]),
                waits=[("sPE", ck_pe)], inc="sACT")
            ck_xe_bank[bank] = ("sACT", ck_ev)
        ck_tp3[0] = ck_pe
        ck_xT_done[2] = ck_xe_bank[2]
        ck_xT_done[3] = ck_xe_bank[3]
    for s in range(2):
        ck_q_ld[s] = p.dma("gpsimd", WQ(s), d_kq[s * 128:(s + 1) * 128, :],
                           waits=[("sDVE", ck_la)], sem=f"sQ{s % 2}")
    if tap:
        emit_tp_g3()
        p.dma("gpsimd", tapd["t_xT"], xT[:], waits=ck_xT_done, sem="sTAP")

    wq_rot = 8
    ck_wo_cast = None

    def wo_slice(s):
        nonlocal ck_wo_cast
        if ck_wo_cast is None:
            w = [("sDVE", ck_q_red[6])]
        else:
            w = [("sACT", ck_wo_cast)]
        ck_ld = p.dma("sync", WQ(0), d_wo[s * 128:(s + 1) * 128, :],
                      waits=w, sem="sQW0")
        ck_wo_cast = p.add("scalar", lambda e, s=s: e.copy(
            out=woB[:, s * 1024:(s + 1) * 1024], in_=WQ(0)),
            waits=[("sQW0", ck_ld)], inc="sACT")

    Wr = [P0[:, 0:1024], P0[:, 1024:2048]]
    Kr = P1[:, 0:1024]
    Ur = P1[:, 1024:2048]
    Bq2[1] = U[:, 6144:7168]
    Bu2[1] = U[:, 8192:9216]
    Bb2 = U[:, 7168:8192]

    ck_castw = [None] * 3
    ck_wld = [None] * 3
    ck_wst_cast = [None] * 3
    ck_wbf_used = [None] * 3

    def weights_dma(c):
        for i, src_ap in enumerate(w_srcs(c)):
            w = []
            if ck_wst_cast[i] is not None:
                w.append(("sACT", ck_wst_cast[i]))
            else:
                w.append(("sPE", ck_phase1_pe))
                w.append(("sDVE", ck_lb))
            ck_wld[i] = p.dma(
                "sync" if i == 0 else "scalar",
                XL(i).rearrange("p (kc n) -> p kc n", n=128),
                src_ap, waits=w, sem=f"sWS{i}")

    def weights_cast():
        for i in range(3):
            wpe = [(f"sWS{i}", ck_wld[i])]
            if ck_wbf_used[i] is not None:
                wpe.append(("sPE", ck_wbf_used[i]))
            ck_wst_cast[i] = ck_castw[i] = p.add(
                "scalar", lambda e, i=i: e.copy(out=wbf[i][:], in_=XL(i)),
                waits=wpe, inc="sACT")

    ck_g_pr = [None, None]
    ck_q_last = None
    ck_uev_last = None
    ck_carry_pr = [None, None]
    ck_castA_pr = [None, None]
    ck_fA_pr = [None, None]
    ck_casty = [None] * 8
    ck_castA = [None] * 8
    ck_cy = [None] * 8
    ck_cA = [None] * 8
    pend = []
    ck_ccinA = [None]
    ck_fy_last = None
    ck_qmax = [None]
    ck_qscale = [None]
    ck_qcols = [None]

    def emit_pend():
        while pend:
            pend.pop(0)()

    ck_castw = list(ck_castw_c0)

    for c in range(8):
        for th in range(2):
            i = 2 * c + th
            pr = i % 2

            def mm_block(tgt, region, extra, th=th):
                ckl = None
                for kc in range(8):
                    for nt in range(2):
                        ckl = p.add(
                            "tensor",
                            lambda e, tgt=tgt, region=region, kc=kc, nt=nt,
                            th=th: e.matmul(
                                out=region[:, nt * 512:(nt + 1) * 512],
                                lhsT=wbf[tgt][:, kc * 128:(kc + 1) * 128],
                                rhs=xT[:, kc * 2048 + th * 1024 + nt * 512:
                                       kc * 2048 + th * 1024 +
                                       (nt + 1) * 512],
                                start=(kc == 0), stop=(kc == 7)),
                            waits=extra if (kc == 0 and nt == 0) else (),
                            inc="sPE" if (kc == 7 and nt == 1) else None)
                return ckl

            ew = [("sACT", ck_castw[0])] if th == 0 else []
            if i == 10:
                ew.append(("sACT", ck_qcols[0]))
            if ck_g_pr[pr] is not None:
                ew.append(("sACT", ck_g_pr[pr]))
            elif i == 0:
                ew += list(ck_xT_done)
            else:
                ew += list(ck_xT_done[2:4])
            ck_w = mm_block(0, Wr[pr], ew)

            ek = [("sACT", ck_castw[1])] if th == 0 else []
            if ck_q_last is not None:
                ek.append(("sACT", ck_q_last))
            ck_k = mm_block(1, Kr, ek)

            eu = [("sACT", ck_castw[2])] if th == 0 else []
            if ck_uev_last is not None:
                eu.append(("sDVE", ck_uev_last))
            elif i == 0:
                eu.append(("sACT", ck_bias_evac))
            ck_u = mm_block(2, Ur, eu)
            if th == 1:
                ck_wbf_used = [ck_w, ck_k, ck_u]

            if th == 0 and c < 7:
                weights_dma(c + 1)
            if i == 0 and ck_tp3[0] is None:
                emit_tp_g3()

            Bgx, Bqx, Bux = Bg2[pr], Bq2[pr], Bu2[pr]
            wg = [("sPE", ck_w)]
            if ck_carry_pr[pr] is not None:
                wg.append(("sDVE", ck_carry_pr[pr]))
            ck_g = p.add("scalar", lambda e, c=c, Bgx=Bgx, region=Wr[pr]:
                         e.activation(
                             out=Bgx[:], in_=region, func=AF.Sigmoid,
                             bias=bcol[:, c:c + 1], scale=1.0),
                         waits=wg, inc="sACT")
            ck_g_pr[pr] = ck_g
            emit_pend()
            wq2 = [("sPE", ck_k)]
            if ck_fA_pr[pr] is not None:
                wq2.append(("sDVE", ck_fA_pr[pr]))
            elif i == 0:
                wq2.append(("sPE", ck_bias_pe))
                wq2.append(("sDVE", ck_bpk2))
            ck_qe = p.add("scalar", lambda e, c=c, Bqx=Bqx: e.activation(
                out=Bqx[:], in_=Kr, func=AF.Sigmoid,
                bias=bcol[:, 8 + c:9 + c], scale=-2.0),
                waits=wq2, inc="sACT")
            ck_q_last = ck_qe
            if th == 1 and c < 7:
                weights_cast()
            if i >= 8:
                wo_slice(i - 8)
            if i == 8:
                passB_load(0, extra=[("sACT", ck_castw_c0[0]),
                                     ("sPE", ck_tp3[0])])
            if 10 <= i <= 13:
                quant_ph2(i - 10)
                passB_load(i - 9)

            wu = [("sPE", ck_u)]
            if ck_castA_pr[pr] is not None:
                wu.append(("sACT", ck_castA_pr[pr]))
            if i == 0:
                wu.append(("sACT", ck_bias_evac))
            if i == 1:
                wu.append(("sACT", ck_castw_c0[2]))
            ck_uev = p.add("vector", lambda e, c=c, Bux=Bux: e.tensor_scalar(
                out=Bux[:], in0=Ur, scalar1=bcol[:, 16 + c:17 + c],
                scalar2=None, op0=ALU.add), waits=wu, inc="sDVE")
            ck_uev_last = ck_uev

            ck_stt = p.add("vector", lambda e, Bgx=Bgx, Bqx=Bqx:
                           e.scalar_tensor_tensor(
                               out=Bb[:], in0=Bgx[:], scalar=-2.0,
                               in1=Bqx[:], op0=ALU.mult, op1=ALU.mult),
                           waits=[("sACT", ck_g), ("sACT", ck_qe)],
                           inc="sDVE")
            ck_aa = p.add("scalar", lambda e, Bqx=Bqx: e.activation(
                out=Bqx[:], in_=Bb[:], func=AF.Copy, bias=1.0, scale=1.0),
                waits=[("sDVE", ck_stt)], inc="sACT")
            wb = [("sDVE", ck_stt)]
            if i == 0:
                wb.append(("sACT", ck_castw_c0[1]))
                wb.append(("sPE", ck_tp3[0]))
            ck_b = p.add("vector", lambda e, Bgx=Bgx, Bux=Bux:
                         e.tensor_tensor(out=Bb2, in0=Bgx[:], in1=Bux[:],
                                         op=ALU.mult),
                         waits=wb, inc="sDVE")
            if th == 0:
                ini_y, wy_extra = 0.0, []
            else:
                ini_y = carry[:, c:c + 1]
                wy_extra = [("sDVE", ck_cy[c])]
            p.add("vector", lambda e, ini_y=ini_y, Bgx=Bgx, Bqx=Bqx:
                  e.tensor_tensor_scan(
                      out=Bgx[:], data0=Bqx[:], data1=Bb2, initial=ini_y,
                      op0=ALU.mult, op1=ALU.add),
                  waits=[("sDVE", ck_b), ("sACT", ck_aa)] + wy_extra,
                  inc="sDVE")
            ck_fy = ck_fy_last = flush_dve()
            wcy = [("sGL", ck_ccinA[0])] if i >= 14 else []
            ck_cy[c] = ck_carry_pr[pr] = p.add(
                "vector", lambda e, c=c, Bgx=Bgx: e.tensor_copy(
                    out=carry[:, c:c + 1], in_=Bgx[:, 1023:1024]),
                waits=wcy, inc="sDVE")
            if th == 0:
                ini_a = 1.0
                wa_extra = [("sDVE", ck_zt)] if i == 0 else []
            else:
                ini_a = carry[:, 8 + c:9 + c]
                wa_extra = [("sDVE", ck_cA[c])]
            p.add("vector", lambda e, ini_a=ini_a, Bqx=Bqx, Bux=Bux:
                  e.tensor_tensor_scan(
                      out=Bux[:], data0=Bqx[:], data1=zt[:], initial=ini_a,
                      op0=ALU.mult, op1=ALU.add),
                  waits=wa_extra, inc="sDVE")
            ck_fa = flush_dve()
            ck_fA_pr[pr] = ck_fa
            ck_cA[c] = p.add("vector", lambda e, c=c, Bux=Bux: e.tensor_copy(
                out=carry[:, 8 + c:9 + c], in_=Bux[:, 1023:1024]),
                inc="sDVE")
            if i < 8:
                passA_reduce(i)
                if i <= 5:
                    passA_load(i + 2)
            if i == 13:
                ck_fcarA = flush_dve()
                ck_ccinA[0] = p.dma("sync", cc_in.ap(), carry[:, 0:8],
                                    waits=[("sDVE", ck_fcarA)], sem="sGL")
                p.wait_only("gpsimd", [("sGL", ck_ccinA[0])])
                p.add("gpsimd", lambda e: e.collective_compute(
                    "AllGather", ALU.bypass,
                    replica_groups=[list(range(NCORES))],
                    ins=[cc_in.ap().opt()], outs=[cc_out.ap().opt()]),
                    inc="sCC")
            if i == 7:
                p.add("vector", lambda e: e.reduce_max(
                    out=qsc[:, 8:9], in_=qsc[:, 0:8], axis=AX.X,
                    apply_absolute_value=False), inc="sDVE")
                ck_qmax[0] = flush_dve()
            if i == 8:
                ck_qtp = p.add("tensor", lambda e: e.transpose(
                    out=P0[0:1, 16:144], in_=qsc[:, 8:9], identity=idT[:]),
                    waits=[("sDVE", ck_qmax[0]),
                           ("sACT", ck_g_pr[0])], inc="sPE")
                ck_qrow = p.add("scalar", lambda e: e.copy(
                    out=qrow[0:1, 0:128], in_=P0[0:1, 16:144]),
                    waits=[("sPE", ck_qtp)], inc="sACT")
                p.add("vector", lambda e: e.reduce_max(
                    out=qrow[0:1, 128:129], in_=qrow[0:1, 0:128],
                    axis=AX.X, apply_absolute_value=False),
                    waits=[("sACT", ck_qrow)], inc="sDVE")
                p.add("vector", lambda e: e.reciprocal(
                    out=qrow[0:1, 132:133], in_=qrow[0:1, 131:132]),
                    inc="sDVE")
                flush_dve()
                p.add("vector", lambda e: e.tensor_tensor(
                    out=qrow[0:1, 129:130], in0=qrow[0:1, 128:129],
                    in1=qrow[0:1, 132:133], op=ALU.mult), inc="sDVE")
                flush_dve()
                p.add("vector", lambda e: e.reciprocal(
                    out=qrow[0:1, 130:131], in_=qrow[0:1, 129:130]),
                    inc="sDVE")
                ck_qscale[0] = flush_dve()
            if i == 9:
                ck_qb2 = p.add("tensor", lambda e: e.matmul(
                    out=P0[:, 16:17], lhsT=onest[0:1, :],
                    rhs=qrow[0:1, 129:130], start=True, stop=True),
                    waits=[("sDVE", ck_qscale[0])], inc="sPE")
                ck_qb2 = p.add("tensor", lambda e: e.matmul(
                    out=P0[:, 17:18], lhsT=onest[0:1, :],
                    rhs=qrow[0:1, 130:131], start=True, stop=True),
                    inc="sPE")
                ck_qcols[0] = p.add("scalar", lambda e: e.copy(
                    out=qsc[:, 9:11], in_=P0[:, 16:18]),
                    waits=[("sPE", ck_qb2)], inc="sACT")

            def mk_casts(c=c, th=th, pr=pr, Bgx=Bgx, Bux=Bux,
                         ck_fy=ck_fy, ck_fa=ck_fa):
                def do():
                    ck_casty[c] = p.add("scalar", lambda e: e.copy(
                        out=yB[:, c * 2048 + th * 1024:
                               c * 2048 + (th + 1) * 1024], in_=Bgx[:]),
                        waits=[("sDVE", ck_fy)], inc="sACT")
                    ck_castA[c] = ck_castA_pr[pr] = p.add(
                        "scalar", lambda e: e.copy(
                            out=aB[:, c * 2048 + th * 1024:
                                   c * 2048 + (th + 1) * 1024], in_=Bux[:]),
                        waits=[("sDVE", ck_fa)], inc="sACT")
                return do
            pend.append(mk_casts())
    emit_pend()
    ck_pool_casty = p.cnt["sACT"]
    ck_qt_ld = {}
    qt_sem = {6: "sQT0", 7: "sQT1", 4: "sQT2", 5: "sQT3"}
    for s, buf in ((6, Bq), (7, Bu)):
        ck_qt_ld[s] = p.dma(
            "gpsimd", buf[:], d_kq[s * 128:(s + 1) * 128, :],
            waits=[("sDVE", ck_fA_pr[0]), ("sACT", ck_pool_casty)],
            sem=qt_sem[s])

    ck_qtail_prev = [None]

    def quant_tail(s, buf):
        nonlocal ck_quant_act
        ck_v = p.add("scalar", lambda e, buf=buf: e.activation(
            out=buf[:], in_=buf[:], func=AF.Copy, bias=0.0,
            scale=qsc[:, 10:11]),
            waits=[(qt_sem[s], ck_qt_ld[s])], inc="sACT")
        ck_r = None
        for hh in range(2):
            p.add("scalar", lambda e, buf=buf, hh=hh: e.copy(
                out=QT[:], in_=buf[:, hh * 512:(hh + 1) * 512]), inc="sACT")
            wt2 = []
            if hh == 0:
                wt2 = [("sDVE", ck_qtail_prev[0] if ck_qtail_prev[0]
                        is not None else ck_corr_c[7])]
            ck_r = p.add("scalar", lambda e, hh=hh: e.activation(
                out=Bb[:, hh * 512:(hh + 1) * 512], in_=QT[:],
                func=AF.Copy, bias=-1.0, scale=1.0),
                waits=wt2, inc="sACT")
        p.add("vector", lambda e, buf=buf: e.tensor_tensor(
            out=buf[:], in0=buf[:], in1=Bb[:], op=ALU.subtract),
            waits=[("sACT", ck_r)], inc="sDVE")
        p.add("vector", lambda e, buf=buf: e.scalar_tensor_tensor(
            out=Bb[:], in0=buf[:], scalar=1.5, in1=Bb[:],
            op0=ALU.is_gt, op1=ALU.add), inc="sDVE")
        p.add("vector", lambda e, buf=buf: e.scalar_tensor_tensor(
            out=Bb[:], in0=buf[:], scalar=0.5, in1=Bb[:],
            op0=ALU.is_ge, op1=ALU.add), inc="sDVE")
        ck_clip = p.add("vector", lambda e: e.tensor_scalar(
            out=Bb[:], in0=Bb[:], scalar1=-8.0, scalar2=7.0,
            op0=ALU.max, op1=ALU.min), inc="sDVE")
        ck_qtail_prev[0] = ck_clip
        ck_quant_act = p.add("scalar", lambda e, s=s: e.activation(
            out=kqB[:, s * 1024:(s + 1) * 1024], in_=Bb[:],
            func=AF.Copy, bias=0.0, scale=qsc[:, 9:10]),
            waits=[("sDVE", ck_clip)], inc="sACT")
    free_T = {0: ("sACT", ck_g_pr[0]), 1: ("sACT", ck_q_last)}

    ck_fcar = flush_dve()
    ck_ccinB = p.dma("sync", cc_in2.ap(), carry[:, 0:8],
                     waits=[("sDVE", ck_fcar)], sem="sGL")
    p.wait_only("gpsimd", [("sGL", ck_ccinB)])
    p.add("gpsimd", lambda e: e.collective_compute(
        "AllGather", ALU.bypass, replica_groups=[list(range(NCORES))],
        ins=[cc_in2.ap().opt()], outs=[cc_out2.ap().opt()]), inc="sCC")
    p.wait_only("gpsimd", [("sCC", 2)])
    ck_G = p.dma("sync", Gt,
                 cc_out.ap().rearrange("(r p) c -> r (p c)", p=128),
                 waits=[("sACT", ck_pool_casty), ("sCC", 1)], sem="sGL2")
    Gtv = Bg[0:8, 0:1024].rearrange("r (p c) -> r c p", c=8)
    ck_mm_sin = None
    for c8 in range(7):
        w = [("sGL2", ck_G), ("sS", 64)]
        if c8 == 0:
            w.append(("sACT", ck_q_last))
        ck_mm_sin = p.add("tensor", lambda e, c8=c8: e.matmul(
            out=P1[:, c8:c8 + 1],
            lhsT=Gtv[:, c8],
            rhs=mskT[0:8, 0:1], start=True, stop=True),
            waits=w, inc="sPE")
    p.add("scalar", lambda e: e.copy(
        out=carry[:, 16:23], in_=P1[:, 0:7]),
        waits=[("sPE", ck_mm_sin)], inc="sACT")
    ck_sin = flush_act()
    ck_corr_c = [None] * 8
    for c in range(7):
        ck_corr_c[c] = p.add(
            "vector", lambda e, c=c: e.scalar_tensor_tensor(
                out=yB[:, c * 2048:(c + 1) * 2048],
                in0=aB[:, c * 2048:(c + 1) * 2048],
                scalar=carry[:, 16 + c:17 + c],
                in1=yB[:, c * 2048:(c + 1) * 2048],
                op0=ALU.mult, op1=ALU.add),
            waits=[("sACT", ck_sin), ("sACT", ck_castA[c]),
                   ("sACT", ck_casty[c])], inc="sDVE")
    gB = {}

    def emit_gatherB():
        if gB:
            return
        ck_GB = p.dma("sync", Gt,
                      cc_out2.ap().rearrange("(r p) c -> r (p c)", p=128),
                      waits=[("sCC", 2), ("sPE", ck_mm_sin)], sem="sGL2")
        ck_mm_sinB = p.add("tensor", lambda e: e.matmul(
            out=P1[:, 7:8], lhsT=Gtv[:, 7],
            rhs=mskT[0:8, 0:1], start=True, stop=True),
            waits=[("sGL2", ck_GB)], inc="sPE")
        p.add("scalar", lambda e: e.copy(
            out=carry[:, 23:24], in_=P1[:, 7:8]),
            waits=[("sPE", ck_mm_sinB)], inc="sACT")
        gB["sinB"] = flush_act()
        ck_corr_c[7] = gB["corr"] = p.add(
            "vector", lambda e: e.scalar_tensor_tensor(
                out=yB[:, 7 * 2048:8 * 2048],
                in0=aB[:, 7 * 2048:8 * 2048],
                scalar=carry[:, 23:24],
                in1=yB[:, 7 * 2048:8 * 2048],
                op0=ALU.mult, op1=ALU.add),
            waits=[("sACT", gB["sinB"]), ("sACT", ck_castA[7]),
                   ("sACT", ck_casty[7])], inc="sDVE")
        quant_ph2(4)
        passB_load(5)
        quant_ph2(5)

    if tap:
        emit_gatherB()
        p.dma("gpsimd", tapd["t_y"], yB[:], waits=[("sDVE", gB["corr"])],
              sem="sTAP")
        p.dma("gpsimd", tapd["t_A"], aB[:], waits=[("sDVE", gB["corr"])],
              sem="sTAP")
        p.dma("gpsimd", tapd["t_carry"], carry[:], waits=[("sACT", ck_sin)], sem="sTAP")
        p.dma("gpsimd", tapd["t_wo"], woB[:], waits=[("sACT", ck_wo_cast)], sem="sTAP")
        p.wait_only("scalar", [("sTAP", p.cnt["sTAP"])])


    tp_rot = [None] * 4
    po_use = [None] * 4
    ln_state = {"lnt_pe": [None, None], "lnt_cast": [None, None],
                "rstt": [None, None], "std": None}

    LNWv = LNW.rearrange("p (a b) -> p a b", b=8)

    def ln_pair(pars, store_tiles=None, extra_apply_waits=()):
        for par in pars:
            p8 = par * 8
            p.add("vector", lambda e, par=par, p8=p8: e.scalar_tensor_tensor(
                out=SCR, in0=RT(par), scalar=1.0, in1=RT(par),
                op0=ALU.mult, op1=ALU.mult,
                accum_out=LNW[:, p8 + 2:p8 + 3]), inc="sDVE")
        p.add("vector", lambda e: e.tensor_tensor(
            out=LNWv[:, :, 3], in0=LNWv[:, :, 0], in1=LNWv[:, :, 1],
            op=ALU.add), inc="sDVE")
        p.add("vector", lambda e: e.tensor_scalar(
            out=LNWv[:, :, 4], in0=LNWv[:, :, 3],
            scalar1=1.0 / 1024.0, scalar2=None, op0=ALU.mult), inc="sDVE")
        p.add("vector", lambda e: e.scalar_tensor_tensor(
            out=LNWv[:, :, 5], in0=LNWv[:, :, 4], scalar=-1.0,
            in1=LNWv[:, :, 4], op0=ALU.mult, op1=ALU.mult), inc="sDVE")
        p.add("vector", lambda e: e.tensor_scalar(
            out=LNWv[:, :, 6], in0=LNWv[:, :, 5],
            scalar1=EPS, scalar2=None, op0=ALU.add), inc="sDVE")
        p.add("vector", lambda e: e.tensor_scalar(
            out=LNWv[:, :, 7], in0=LNWv[:, :, 2],
            scalar1=1.0 / 1024.0, scalar2=None, op0=ALU.mult), inc="sDVE")
        ck_b = p.add("vector", lambda e: e.tensor_tensor(
            out=LNWv[:, :, 7], in0=LNWv[:, :, 7], in1=LNWv[:, :, 6],
            op=ALU.add), inc="sDVE")
        p.add("scalar", lambda e: e.activation(
            out=LNWv[:, :, 5], in_=LNWv[:, :, 7],
            func=AF.Ln, bias=0.0, scale=1.0),
            waits=[("sDVE", ck_b)], inc="sACT")
        p.add("scalar", lambda e: e.activation(
            out=LNWv[:, :, 5], in_=LNWv[:, :, 5],
            func=AF.Exp, bias=0.0, scale=-0.5), inc="sACT")
        ck_std = flush_act()
        ln_state["std"] = ck_std
        cks = {}
        for par in pars:
            p8 = par * 8
            wl = list(extra_apply_waits)
            if store_tiles is not None:
                wl.append(("sST", p.cnt["sST"]))
            prev = ln_state["lnt_cast"][par]
            if prev is not None:
                wl.append(prev)
            prev_pe = ln_state["lnt_pe"][par]
            if prev_pe is not None:
                wl.append(("sPE", prev_pe))
            wl.append(("sACT", ck_std))
            cks[par] = p.add("vector", lambda e, par=par, p8=p8:
                             e.tensor_scalar(
                                 out=LNT(par), in0=RT(par),
                                 scalar1=LNW[:, p8 + 4:p8 + 5],
                                 scalar2=LNW[:, p8 + 5:p8 + 6],
                                 op0=ALU.subtract, op1=ALU.mult),
                             waits=wl, inc="sDVE")
        return cks

    p.dma("sync", U[0:1, 7168:8192], d_bo,
          waits=[("sDVE", ck_fy_last)], sem="sBR")
    p.dma("sync", U[32:33, 7168:8192], d_fb, sem="sBR")
    ck_brows = p.cnt["sBR"]
    bo_bf = brow[0:1, 0:1024]
    fb_bf = brow[0:1, 1024:2048]
    ones_bf = brow[0:1, 2048:2176]
    p.add("scalar", lambda e: e.copy(out=bo_bf, in_=U[0:1, 7168:8192]),
          waits=[("sBR", ck_brows)], inc="sACT")
    p.add("scalar", lambda e: e.copy(out=fb_bf, in_=U[32:33, 7168:8192]),
          inc="sACT")
    p.add("vector", lambda e: e.memset(ones_bf, 1.0), inc="sDVE")
    ck_brows_bf = p.cnt["sACT"]
    ck_ones_bf = p.cnt["sDVE"]

    ck_tile_rstt = [None, None]
    ck_xr = [None, None]
    ck_lnF = None
    tp_idx = 0

    def tok_tile_mms(tt, lhs_fn, rhs_tile, bias_row, extra_w, extra_last=(),
                     kc_waits=None, mid_hook=None):
        sl = tt % 2
        w = list(extra_w)
        if ck_tile_rstt[sl] is not None:
            w.append(("sDVE", ck_tile_rstt[sl]))
        for kc in range(8):
            if kc == 7 and mid_hook is not None:
                mid_hook()
            for h in range(2):
                wl = list(w) if (kc == 0 and h == 0) else []
                if kc_waits is not None and h == 0:
                    wl.append(kc_waits(kc))
                p.add("tensor", lambda e, kc=kc, h=h, sl=sl, tt=tt: e.matmul(
                    out=P0[:, sl * 1024 + h * 512: sl * 1024 + (h + 1) * 512],
                    lhsT=lhs_fn(kc, tt),
                    rhs=rhs_tile[:, kc * 1024 + h * 512:
                                 kc * 1024 + (h + 1) * 512],
                    start=(kc == 0), stop=False),
                    waits=wl, inc=None)
        ck = None
        for h in range(2):
            wl = [("sACT", ck_brows_bf), ("sDVE", ck_ones_bf)] + \
                list(extra_last)
            ck = p.add("tensor", lambda e, h=h, sl=sl, bias_row=bias_row:
                       e.matmul(
                           out=P0[:, sl * 1024 + h * 512:
                                  sl * 1024 + (h + 1) * 512],
                           lhsT=ones_bf,
                           rhs=bias_row[0:1, h * 512:(h + 1) * 512],
                           start=False, stop=True),
                       waits=wl if h == 0 else (),
                       inc="sPE" if h == 1 else None)
        return ck

    tp_jobs = []

    def emit_tp(tt, par, ck_apply):
        nonlocal ck_lnF, tp_idx
        ck_pe_last = None
        for kg in range(2):
            bank = 4 + (tp_idx % 4)
            tpw = [("sDVE", ck_apply)]
            if tp_rot[tp_idx % 4] is not None:
                tpw.append(tp_rot[tp_idx % 4])
            else:
                tpw.append(("sACT", gB["sinB"]))
            ck_pe = None
            for i in range(4):
                ck_pe = p.add("tensor", lambda e, kg=kg, i=i, par=par,
                              bank=bank: e.transpose(
                                  out=P1[:, (bank - 4) * 512 + i * 128:
                                         (bank - 4) * 512 +
                                         (i + 1) * 128],
                                  in_=LNT(par)[:, (kg * 4 + i) * 128:
                                               (kg * 4 + i + 1) * 128],
                                  identity=idT[:]),
                              waits=tpw if i == 0 else (), inc="sPE")
            ck_lnF = p.add("scalar", lambda e, kg=kg, tt=tt, bank=bank:
                           e.copy(
                               out=xT.rearrange(
                                   "p (kc t) -> p kc t",
                                   t=2048)[:, kg * 4:(kg + 1) * 4,
                                           tt * 128:(tt + 1) * 128],
                               in_=P1[:, (bank - 4) * 512:
                                      (bank - 4 + 1) * 512].rearrange(
                                   "p (kc t) -> p kc t", t=128)),
                           waits=[("sPE", ck_pe)], inc="sACT")
            tp_rot[tp_idx % 4] = ("sACT", ck_lnF)
            tp_idx += 1
            ck_pe_last = ck_pe
        ln_state["lnt_pe"][par] = ck_pe_last

    for pr in range(8):
        tiles = [2 * pr, 2 * pr + 1]
        p.add("vector", lambda e: e.memset(LNWv[:, :, 0:3], 0.0),
              waits=[("sACT", ln_state["std"])], inc="sDVE")
        for tt in tiles:
            sl = tt % 2
            extra_w = [("sACT", ck_wo_cast)]

            if tt == 1:
                extra_w.append(("sACT", ck_g_pr[1]))
            ck_pe = tok_tile_mms(
                tt, lambda kc, tt: yB[:, kc * 2048 + tt * 128:
                                      kc * 2048 + (tt + 1) * 128],
                woB, bo_bf, extra_w,
                kc_waits=lambda kc: ("sDVE", ck_corr_c[kc]),
                mid_hook=emit_gatherB if tt == 0 else None)
            w = []
            if ck_tile_rstt[sl] is not None:
                w.append(("sDVE", ck_tile_rstt[sl]))
            if tt < 2:
                w.append(("sACT", ck_wo_cast))
            if tt == 1:
                w.append(("sACT", ck_quant_act))
            ck_xr[sl] = p.dma("sync", XR(sl),
                              d_x[tt * 128:(tt + 1) * 128, :],
                              waits=w, sem=f"sXR{sl}")
            ck_rstt = None
            for h in range(2):
                ck_rstt = p.add("vector", lambda e, h=h, sl=sl, p8=sl * 8:
                                e.scalar_tensor_tensor(
                                    out=RT(sl)[:, h * 512:(h + 1) * 512],
                                    in0=P0[:, sl * 1024 + h * 512:
                                           sl * 1024 + (h + 1) * 512],
                                    scalar=1.0,
                                    in1=XR(sl)[:, h * 512:(h + 1) * 512],
                                    op0=ALU.mult, op1=ALU.add,
                                    accum_out=LNW[:, p8 + h:p8 + h + 1]),
                                waits=[("sPE", ck_pe),
                                       (f"sXR{sl}", ck_xr[sl])],
                                inc="sDVE")
            ck_tile_rstt[sl] = ck_rstt
            ln_state["rstt"][sl] = ck_rstt
        for job in tp_jobs:
            emit_tp(*job)
        tp_jobs = []
        cks = ln_pair([0, 1])
        for tt in tiles:
            par = tt % 2
            ck_apply = cks[par]
            ck_cast = p.add("scalar", lambda e, tt=tt, par=par: e.copy(
                out=aB[:, tt * 1024:(tt + 1) * 1024], in_=LNT(par)),
                waits=[("sDVE", ck_apply), ("sDVE", ck_corr_c[tt // 2])],
                inc="sACT")
            ln_state["lnt_cast"][par] = ("sACT", ck_cast)
            tp_jobs.append((tt, par, ck_apply))
        if pr == 1:
            quant_tail(6, Bq)
        if pr == 2:
            quant_tail(7, Bu)
    for job in tp_jobs:
        emit_tp(*job)
    tp_jobs = []

    ck_z_ev = None
    for nt in range(4):
        bank = nt % 4
        w = [("sACT", ck_lnF),
             ("sDVE", ck_tile_rstt[0]), ("sDVE", ck_tile_rstt[1])]
        ck_pe = None
        for kc in range(8):
            ck_pe = p.add("tensor", lambda e, kc=kc, nt=nt, bank=bank: e.matmul(
                out=P0[0:16, bank * 512:(bank + 1) * 512],
                lhsT=laB[:, kc * 16:(kc + 1) * 16],
                rhs=xT[:, kc * 2048 + nt * 512: kc * 2048 + (nt + 1) * 512],
                start=(kc == 0), stop=(kc == 7)),
                waits=w if kc == 0 else (), inc="sPE" if kc == 7 else None)
        ck_z_ev = p.add("scalar", lambda e, nt=nt, bank=bank: e.copy(
            out=Z(nt), in_=P0[0:16, bank * 512:(bank + 1) * 512]),
            waits=[("sPE", ck_pe)], inc="sACT")
    ck_tile_rstt = [None, None]

    for pr in range(8):
        tiles = [2 * pr, 2 * pr + 1]
        p.add("vector", lambda e: e.memset(LNWv[:, :, 0:3], 0.0),
              waits=[("sACT", ln_state["std"])], inc="sDVE")
        for tt in tiles:
            sl = tt % 2
            nt = tt // 4
            extra_w = [("sACT", ck_quant_act), ("sACT", ck_z_ev)]
            w = list(extra_w)
            if ck_tile_rstt[sl] is not None:
                w.append(("sDVE", ck_tile_rstt[sl]))
            for kc in range(8):
                for h in range(2):
                    p.add("tensor", lambda e, kc=kc, h=h, sl=sl, tt=tt:
                          e.matmul(
                              out=P0[:, sl * 1024 + h * 512:
                                     sl * 1024 + (h + 1) * 512],
                              lhsT=xT[:, kc * 2048 + tt * 128:
                                      kc * 2048 + (tt + 1) * 128],
                              rhs=kqB[:, kc * 1024 + h * 512:
                                      kc * 1024 + (h + 1) * 512],
                              start=(kc == 0), stop=False),
                          waits=w if (kc == 0 and h == 0) else (), inc=None)
            for h in range(2):
                p.add("tensor", lambda e, h=h, sl=sl, tt=tt, nt=nt: e.matmul(
                    out=P0[:, sl * 1024 + h * 512: sl * 1024 + (h + 1) * 512],
                    lhsT=Z(nt)[:, (tt % 4) * 128:(tt % 4 + 1) * 128],
                    rhs=lbB[0:16, h * 512:(h + 1) * 512],
                    start=False, stop=False),
                    waits=[("sDVE", ck_lb)] if h == 0 else (), inc=None)
            ck_pe = None
            for h in range(2):
                ck_pe = p.add("tensor", lambda e, h=h, sl=sl: e.matmul(
                    out=P0[:, sl * 1024 + h * 512: sl * 1024 + (h + 1) * 512],
                    lhsT=ones_bf,
                    rhs=fb_bf[0:1, h * 512:(h + 1) * 512],
                    start=False, stop=True),
                    waits=[("sACT", ck_brows_bf), ("sDVE", ck_ones_bf)]
                    if h == 0 else (), inc="sPE" if h == 1 else None)
            ck_rstt = None
            for h in range(2):
                ck_rstt = p.add("vector", lambda e, h=h, sl=sl, tt=tt,
                                p8=sl * 8: e.scalar_tensor_tensor(
                                    out=RT(sl)[:, h * 512:(h + 1) * 512],
                                    in0=P0[:, sl * 1024 + h * 512:
                                           sl * 1024 + (h + 1) * 512],
                                    scalar=1.0,
                                    in1=aB[:, tt * 1024 + h * 512:
                                           tt * 1024 + (h + 1) * 512],
                                    op0=ALU.mult, op1=ALU.add,
                                    accum_out=LNW[:, p8 + h:p8 + h + 1]),
                                waits=[("sPE", ck_pe)], inc="sDVE")
            ck_tile_rstt[sl] = ck_rstt
        cks = ln_pair([0, 1], store_tiles=True)
        for tt in tiles:
            par = tt % 2
            ck_st = p.dma("scalar", d_out[tt * 128:(tt + 1) * 128, :],
                          LNT(par), waits=[("sDVE", cks[par])], sem="sST")
            ln_state["lnt_cast"][par] = ("sST", ck_st)
            ln_state["lnt_pe"][par] = None

    if tap:
        lc = ln_state["lnt_cast"]
        p.dma("gpsimd", tapd["t_ln1"], aB[:],
              waits=[("sACT", p.cnt["sACT"]), ("sDVE", p.cnt["sDVE"])],
              sem="sTAP")
        p.dma("gpsimd", tapd["t_lnf"], xT[:],
              waits=[("sACT", p.cnt["sACT"])], sem="sTAP")
        p.dma("gpsimd", tapd["t_kq"], kqB[:],
              waits=[("sDVE", p.cnt["sDVE"]), ("sACT", p.cnt["sACT"])],
              sem="sTAP")
        p.dma("gpsimd", tapd["t_qs"], qsc[:, 0:11],
              waits=[("sDVE", p.cnt["sDVE"])], sem="sTAP")
        p.dma("gpsimd", tapd["t_qrow"], qrow[0:1, 0:133],
              waits=[("sDVE", p.cnt["sDVE"])], sem="sTAP")

    p.wait_only("sync", [("sST", p.cnt["sST"])])
    p.wait_only("gpsimd", [("sST", p.cnt["sST"])])
    p.wait_only("gpsimd", [("sGL", p.cnt["sGL"])])
    p.wait_only("gpsimd", [("sGL2", p.cnt["sGL2"])])
    if tap:
        p.wait_only("gpsimd", [("sTAP", p.cnt["sTAP"])])

    with nc.Block() as block:
        p.emit(block)
    return nc


_NC_CACHE = None


def kernel(L0_attn_x, L0_attn_Wp, L0_attn_bp, L0_attn_Wu, L0_attn_bu,
           L0_attn_Wo, L0_attn_bo, L0_ffn_kernel, L0_ffn_bias, L0_ffn_a,
           L0_ffn_b):
    global _NC_CACHE
    x = np.ascontiguousarray(np.asarray(L0_attn_x, dtype=np.float32))
    wk = np.ascontiguousarray(np.asarray(L0_attn_Wp, dtype=np.float32)[:, :2048])
    wu = np.ascontiguousarray(np.asarray(L0_attn_Wu, dtype=np.float32))
    wo = np.ascontiguousarray(np.asarray(L0_attn_Wo, dtype=np.float32))
    kq = np.ascontiguousarray(np.asarray(L0_ffn_kernel, dtype=np.float32))
    la = np.ascontiguousarray(np.asarray(L0_ffn_a, dtype=np.float32))
    lb = np.ascontiguousarray(np.asarray(L0_ffn_b, dtype=np.float32))
    bp = np.asarray(L0_attn_bp, dtype=np.float32)
    bpw = np.ascontiguousarray(bp[:1024].reshape(1, 1024))
    bpk = np.ascontiguousarray(bp[1024:2048].reshape(1, 1024))
    bu = np.ascontiguousarray(np.asarray(L0_attn_bu, np.float32).reshape(1, 1024))
    bo = np.ascontiguousarray(np.asarray(L0_attn_bo, np.float32).reshape(1, 1024))
    fb = np.ascontiguousarray(np.asarray(L0_ffn_bias, np.float32).reshape(1, 1024))
    ident = np.eye(128, dtype=np.float32)

    if _NC_CACHE is None:
        _NC_CACHE = build_nc()
    nc = _NC_CACHE

    in_maps = []
    for c in range(NCORES):
        b, h = c // 2, c % 2
        mask = np.zeros((8, 1), np.float32)
        if h == 1:
            mask[c - 1, 0] = 1.0
        in_maps.append(dict(
            xc=np.ascontiguousarray(x[b, h * TC:(h + 1) * TC, :]),
            wk=wk, wu=wu, wo=wo, kq=kq, la=la, lb=lb,
            bpw=bpw, bpk=bpk, bu=bu, bo=bo, fb=fb,
            mask=mask, ident=ident))
    res = run_bass_kernel_spmd(nc, in_maps, core_ids=list(range(NCORES)))
    out = np.empty((4, 4096, 1024), dtype=np.float32)
    for c in range(NCORES):
        b, h = c // 2, c % 2
        out[b, h * TC:(h + 1) * TC, :] = np.asarray(res.results[c]["yout"])
    return out


# version 28
# speedup vs baseline: 1.0332x; 1.0206x over previous
import contextlib
from collections import defaultdict

import numpy as np
import concourse.bass as bass
import concourse.mybir as mybir
from concourse.bass_utils import run_bass_kernel_spmd

F32 = mybir.dt.float32
BF16 = mybir.dt.bfloat16
I32 = mybir.dt.int32
AF = mybir.ActivationFunctionType
ALU = mybir.AluOpType
AX = mybir.AxisListType

NCORES = 8
D = 1024
TC = 2048
TH = 1024
EPS = 1e-5


class Plan:

    ENGINES = ("sync", "tensor", "scalar", "vector", "gpsimd")

    def __init__(self, nc, ctx):
        self.nc = nc
        self.steps = {e: [] for e in self.ENGINES}
        self.cnt = defaultdict(int)
        self.water = defaultdict(int)
        self.own = defaultdict(int)
        self.sems = {}
        self._ctx = ctx
        for s in ("sPE", "sACT", "sDVE", "sPOOL", "sCC"):
            self._sem(s)

    def _sem(self, s):
        if s not in self.sems:
            self.sems[s] = self._ctx.enter_context(self.nc.semaphore(s))
        return self.sems[s]

    def _mk_waits(self, engine, waits):
        out = []
        for s, v in waits:
            if v is None or v <= 0:
                continue
            if v <= self.water[(engine, s)]:
                continue
            self._sem(s)
            self.water[(engine, s)] = v
            out.append((self.sems[s], v))
        return out

    OWN = {"vector": "sDVE", "scalar": "sACT", "gpsimd": "sPOOL"}

    def add(self, engine, fn, waits=(), inc=None, inc_n=1, is_dma=False):
        waits = list(waits)
        own = self.OWN.get(engine)
        if own is not None and not is_dma:
            prev = self.own[(engine, own)]
            if prev > 0:
                waits.insert(0, (own, prev))
            if inc is None:
                inc = own
        wl = self._mk_waits(engine, waits)
        ck = None
        ih = None
        if inc is not None:
            ih = self._sem(inc)
            self.cnt[inc] += inc_n
            self.own[(engine, inc)] += inc_n
            ck = self.cnt[inc]

        def step(eng, fn=fn, wl=wl, ih=ih, inc_n=inc_n):
            for h, v in wl:
                eng.wait_ge(h, v)
            r = fn(eng)
            if ih is not None:
                r.then_inc(ih, inc_n)
        self.steps[engine].append(step)
        return ck

    def dma(self, queue, out_ap, in_ap, waits=(), sem=None, transpose=False,
            **kw):
        assert sem is not None
        if transpose:
            fn = lambda eng, o=out_ap, i=in_ap: eng.dma_start_transpose(
                out=o, in_=i)
        else:
            fn = lambda eng, o=out_ap, i=in_ap, kw=kw: eng.dma_start(
                out=o, in_=i, **kw)
        return self.add(queue, fn, waits=waits, inc=sem, inc_n=16,
                        is_dma=True)

    def wait_only(self, engine, waits):
        wl = self._mk_waits(engine, list(waits))
        if not wl:
            return

        def step(eng, wl=wl):
            for h, v in wl:
                eng.wait_ge(h, v)
        self.steps[engine].append(step)

    def emit(self, block):
        for e in self.ENGINES:
            body = self.steps[e]

            def runner(eng, body=body):
                for f in body:
                    f(eng)
            getattr(block, e)(runner)


def build_nc(tap=False):
    nc = bass.Bass("TRN2", target_bir_lowering=False)
    ctx = contextlib.ExitStack()
    nc._keepalive = ctx

    d_x = nc.dram_tensor("xc", [TC, D], F32, kind="ExternalInput").ap()
    d_wk = nc.dram_tensor("wk", [D, 2 * D], F32, kind="ExternalInput").ap()
    d_wu = nc.dram_tensor("wu", [D, D], F32, kind="ExternalInput").ap()
    d_wo = nc.dram_tensor("wo", [D, D], F32, kind="ExternalInput").ap()
    d_kq = nc.dram_tensor("kq", [D, D], F32, kind="ExternalInput").ap()
    d_la = nc.dram_tensor("la", [D, 16], F32, kind="ExternalInput").ap()
    d_lb = nc.dram_tensor("lb", [16, D], F32, kind="ExternalInput").ap()
    d_bpw = nc.dram_tensor("bpw", [1, D], F32, kind="ExternalInput").ap()
    d_bpk = nc.dram_tensor("bpk", [1, D], F32, kind="ExternalInput").ap()
    d_bu = nc.dram_tensor("bu", [1, D], F32, kind="ExternalInput").ap()
    d_bo = nc.dram_tensor("bo", [1, D], F32, kind="ExternalInput").ap()
    d_fb = nc.dram_tensor("fb", [1, D], F32, kind="ExternalInput").ap()
    d_mask = nc.dram_tensor("mask", [8, 1], F32, kind="ExternalInput").ap()
    d_id = nc.dram_tensor("ident", [128, 128], F32, kind="ExternalInput").ap()
    d_out = nc.dram_tensor("yout", [TC, D], F32, kind="ExternalOutput").ap()

    cc_in = nc.dram_tensor("cc_in", [128, 8], F32)
    cc_out = nc.dram_tensor("cc_out", [1024, 8], F32)
    cc_in2 = nc.dram_tensor("cc_in2", [128, 8], F32)
    cc_out2 = nc.dram_tensor("cc_out2", [1024, 8], F32)
    tapd = {}
    if tap:
        for nm, shape, dt in [
                ("t_xT", [128, 16384], BF16), ("t_y", [128, 16384], BF16),
                ("t_A", [128, 16384], BF16), ("t_carry", [128, 24], F32),
                ("t_kq", [128, 8192], BF16), ("t_wo", [128, 8192], BF16),
                ("t_ln1", [128, 16384], BF16),
                ("t_lnf", [128, 16384], BF16), ("t_qs", [128, 11], F32),
                ("t_qrow", [1, 133], F32)]:
            tapd[nm] = nc.dram_tensor(nm, shape, dt, kind="ExternalOutput").ap()

    def sb(name, shape, dt=F32):
        return ctx.enter_context(nc.sbuf_tensor(name, shape, dt))

    xT = sb("xT", [128, 8 * TC], BF16)
    yB = sb("yB", [128, 8 * TC], BF16)
    aB = sb("aB", [128, 8 * TC], BF16)
    U = sb("U", [128, 10240], F32)
    QT = sb("QT", [128, 512], I32)
    Bg = sb("Bg", [128, TH], F32)
    Bq = sb("Bq", [128, TH], F32)
    Bu = sb("Bu", [128, TH], F32)
    Bb = sb("Bb", [128, TH], F32)
    Bg2 = [Bg, None]

    def Z(nt):
        return wbf[nt // 2][0:16, (nt % 2) * 512:(nt % 2 + 1) * 512]
    Bq2 = [Bq, Bq]
    Bu2 = [Bu, Bu]
    zt = sb("zt", [128, TH], BF16)
    brow = sb("brow", [1, 2304], BF16)
    wbf = [sb(f"wbf{i}", [128, 1024], BF16) for i in range(3)]
    woB = sb("woB", [128, 8 * D], BF16)
    kqB = sb("kqB", [128, 8 * D], BF16)
    laB = sb("laB", [128, 128], BF16)
    lbB = sb("lbB", [16, D], BF16)
    bcol = sb("bcol", [128, 48], F32)
    carry = sb("carry", [128, 24], F32)
    slocT = sb("slocT", [8, 128], F32)
    mskT = sb("mskT", [8, 1], F32)
    idT = sb("idT", [128, 128], F32)
    onest = sb("onest", [1, 128], F32)
    qsc = sb("qsc", [128, 12], F32)
    qrow = sb("qrow", [1, 136], F32)
    LNW = sb("LNW", [128, 16], F32)
    flt = sb("flt", [128, 2], F32)
    Gt = Bg[0:8, 0:1024]
    rowb = Bq[0:1, 0:1024]

    def XL(j):
        return U[:, j * 1024:(j + 1) * 1024]

    def WQ(j):
        return U[:, 4096 + (j % 2) * 1024: 5120 + (j % 2) * 1024]

    def MO(d):
        return U[:, d * 512:(d + 1) * 512]

    def XR(j):
        return U[:, 4096 + (j % 2) * 1024: 5120 + (j % 2) * 1024]

    def RT(j):
        return U[:, 6144 + (j % 2) * 1024: 7168 + (j % 2) * 1024]

    def LNT(j):
        return U[:, 8192 + (j % 2) * 1024: 9216 + (j % 2) * 1024]

    SCR = Bg[:, 0:1024]
    Bg2[1] = U[:, 3 * 1024:4 * 1024]

    P0 = ctx.enter_context(nc.psum_tensor("P0", [128, 2048], F32))
    P1 = ctx.enter_context(nc.psum_tensor("P1", [128, 2048], F32))

    p = Plan(nc, ctx)

    def flush_dve(waits=()):
        return p.add("vector", lambda e: e.tensor_copy(out=flt[:, 0:1],
                                                       in_=flt[:, 1:2]),
                     waits=waits, inc="sDVE")

    def flush_act(waits=()):
        return p.add("scalar", lambda e: e.copy(out=flt[:, 0:1],
                                                in_=flt[:, 1:2]),
                     waits=waits, inc="sACT")

    p.dma("sync", idT[:], d_id, sem="sS")
    p.dma("sync", mskT[:], d_mask, sem="sS")
    p.dma("sync", U[0:16, 6144:7168], d_lb, sem="sS")
    ck_lb = p.add("vector",
                  lambda e: e.tensor_copy(out=lbB[:], in_=U[0:16, 6144:7168]),
                  waits=[("sS", 48)], inc="sDVE")
    p.dma("sync",
          U[:, 6144:6272].rearrange("p (kc j) -> p kc j", j=16),
          d_la.rearrange("(kc p) j -> p kc j", p=128),
          waits=[("sDVE", ck_lb)], sem="sS")
    ck_la = p.add("vector",
                  lambda e: e.tensor_copy(out=laB[:], in_=U[:, 6144:6272]),
                  waits=[("sS", 64)], inc="sDVE")
    p.add("vector", lambda e: e.memset(flt[:], 0.0), inc="sDVE")
    p.add("vector", lambda e: e.memset(onest[:], 1.0), inc="sDVE")
    p.add("vector", lambda e: e.memset(zt[:], 0.0), inc="sDVE")
    p.add("vector", lambda e: e.memset(carry[:], 0.0), inc="sDVE")
    p.add("vector", lambda e: e.memset(qrow[0:1, 131:132], 7.0), inc="sDVE")
    ck_zt = p.cnt["sDVE"]

    for v, dsrc in enumerate([d_bpw, d_bpk, d_bu, d_bo, d_fb]):
        ck_row = p.dma("sync", rowb, dsrc, sem="sB")
        ck_pe = None
        for c in range(8):
            ck_pe = p.add(
                "tensor",
                lambda e, v=v, c=c: e.transpose(
                    out=P1[:, 1536 + v * 8 + c: 1537 + v * 8 + c],
                    in_=Bq[0:1, c * 128:(c + 1) * 128],
                    identity=idT[0:1, 0:1]),
                waits=[("sB", ck_row), ("sS", 64)], inc="sPE")
        ck_bias_pe = ck_pe
        ck_bias_evac = p.add(
            "scalar",
            lambda e, v=v: e.copy(out=bcol[:, v * 8:(v + 1) * 8],
                                  in_=P1[:, 1536 + v * 8: 1544 + v * 8]),
            waits=[("sPE", ck_pe)], inc="sACT")
        p.wait_only("sync", [("sPE", ck_pe)])
    p.add("vector",
          lambda e: e.tensor_scalar(out=bcol[:, 8:16], in0=bcol[:, 8:16],
                                    scalar1=-2.0, scalar2=None, op0=ALU.mult),
          waits=[("sACT", ck_bias_evac)], inc="sDVE")
    ck_bpk2 = p.cnt["sDVE"]

    ck_q_ld = [None] * 8
    ck_q_red = [None] * 8

    def passA_load(s):
        ck_q_ld[s] = p.dma("gpsimd", WQ(s), d_kq[s * 128:(s + 1) * 128, :],
                           waits=[("sDVE", ck_q_red[s - 2])], sem=f"sQ{s % 2}")

    def passA_reduce(s):
        ck_q_red[s] = p.add(
            "vector",
            lambda e, s=s: e.reduce_max(out=qsc[:, s:s + 1], in_=WQ(s),
                                        axis=AX.X, apply_absolute_value=True),
            waits=[(f"sQ{s % 2}", ck_q_ld[s])], inc="sDVE")

    QB = U[:, 9216:10240]
    T2 = U[:, 5120:6144]
    ck_qb_ld = [None] * 8
    ck_quant_act = None
    ck_qd_prev = None
    ck_t2_prev = None

    def passB_load(s, extra=()):
        w = list(extra)
        if ck_qd_prev is not None:
            w.append(("sDVE", ck_qd_prev))
        ck_qb_ld[s] = p.dma("gpsimd", QB, d_kq[s * 128:(s + 1) * 128, :],
                            waits=w, sem="sQL0")

    def quant_ph2(s, extra=()):
        nonlocal ck_quant_act, ck_qd_prev, ck_t2_prev
        ck_v = p.add("scalar", lambda e: e.activation(
            out=QB, in_=QB, func=AF.Copy, bias=0.0, scale=qsc[:, 10:11]),
            waits=[("sQL0", ck_qb_ld[s])] + list(extra), inc="sACT")
        ck_r = None
        wt2 = [("sDVE", ck_t2_prev)] if ck_t2_prev is not None             else [("sDVE", ck_q_red[7])]
        for hh in range(2):
            p.add("scalar", lambda e, hh=hh: e.copy(
                out=QT[:], in_=QB[:, hh * 512:(hh + 1) * 512]), inc="sACT")
            ck_r = p.add("scalar", lambda e, hh=hh: e.activation(
                out=T2[:, hh * 512:(hh + 1) * 512], in_=QT[:],
                func=AF.Copy, bias=-1.0, scale=1.0),
                waits=wt2 if hh == 0 else (), inc="sACT")
        p.add("vector", lambda e: e.tensor_tensor(
            out=QB, in0=QB, in1=T2, op=ALU.subtract),
            waits=[("sACT", ck_r)], inc="sDVE")
        p.add("vector", lambda e: e.scalar_tensor_tensor(
            out=T2, in0=QB, scalar=1.5, in1=T2,
            op0=ALU.is_gt, op1=ALU.add), inc="sDVE")
        ck_qd_prev = p.add("vector", lambda e: e.scalar_tensor_tensor(
            out=T2, in0=QB, scalar=0.5, in1=T2,
            op0=ALU.is_ge, op1=ALU.add), inc="sDVE")
        ck_t2_prev = ck_clip = p.add("vector", lambda e: e.tensor_scalar(
            out=T2, in0=T2, scalar1=-8.0, scalar2=7.0,
            op0=ALU.max, op1=ALU.min), inc="sDVE")
        ck_quant_act = p.add("scalar", lambda e, s=s: e.activation(
            out=kqB[:, s * 1024:(s + 1) * 1024], in_=T2,
            func=AF.Copy, bias=0.0, scale=qsc[:, 9:10]),
            waits=[("sDVE", ck_clip)], inc="sACT")

    def w_srcs(c):
        wkr = d_wk.rearrange("(kc p) n -> p kc n", p=128)
        return [
            wkr[:, :, c * 128:(c + 1) * 128],
            wkr[:, :, 1024 + c * 128:1024 + (c + 1) * 128],
            d_wu.rearrange("(kc p) n -> p kc n",
                           p=128)[:, :, c * 128:(c + 1) * 128],
        ]

    W0S = [U[:, 9216:10240], U[:, 7168:8192], U[:, 8192:9216]]
    ck_castw_c0 = [None] * 3

    def emit_w0s():
        for i, src_ap in enumerate(w_srcs(0)):
            ck_ld = p.dma(["sync", "scalar", "gpsimd"][i],
                          W0S[i].rearrange("p (kc n) -> p kc n", n=128),
                          src_ap, sem=f"sW0{i}")
            ck_castw_c0[i] = p.add(
                "scalar", lambda e, i=i, W=W0S[i]: e.copy(out=wbf[i][:],
                                                          in_=W),
                waits=[(f"sW0{i}", ck_ld)], inc="sACT")

    XG3 = [U[:, 6144:7168], U[:, 7168:8192], U[:, 8192:9216],
           U[:, 9216:10240]]
    def xg3_w(j):
        return [[("sDVE", ck_la)], [("sACT", ck_castw_c0[1])],
                [("sACT", ck_castw_c0[2])], [("sACT", ck_castw_c0[0])]][j]

    ck_xl = [None] * 4
    ck_xq = {}
    ck_tp_g = [None] * 4
    ck_xe_bank = [None] * 4
    ck_xg3 = {}
    for g in range(4):
        for j in range(4):
            if g == 3:
                w = list(xg3_w(j))
            else:
                w = []
                if g == 0 and j == 0:
                    w = [("sDVE", ck_lb)]
                if g == 0 and j == 1:
                    w = [("sDVE", ck_la)]
                if g > 0:
                    w = [("sPE", ck_tp_g[g - 1])]
            qn, sn = [("sync", "sX"), ("scalar", "sX2"),
                      ("gpsimd", "sX3")][(4 * g + j) % 3]
            dst = XL(j)
            if g == 3:
                dst = XG3[j]
                sn = "sY" + sn[2:].replace("X", "0")
            ck_xl[j] = p.dma(
                qn, dst,
                d_x[(4 * g + j) * 128:(4 * g + j + 1) * 128, :],
                waits=w, sem=sn)
            if g == 3:
                ck_xg3[sn] = ck_xl[j]
            else:
                ck_xq[sn] = ck_xl[j]
        if g == 1:
            emit_w0s()
        if g == 2:
            ck_xg2 = dict(ck_xq)
        if g >= 2:
            continue
        for fc in range(8):
            bank = (g * 8 + fc) % 4
            ck_pe = None
            for j in range(4):
                w = [(sn, ck) for sn, ck in ck_xq.items()]
                if ck_xe_bank[bank] is not None:
                    w.append(ck_xe_bank[bank])
                ck_pe = p.add(
                    "tensor",
                    lambda e, j=j, fc=fc, bank=bank: e.transpose(
                        out=P0[:, bank * 512 + j * 128: bank * 512 + (j + 1) * 128],
                        in_=XL(j)[:, fc * 128:(fc + 1) * 128],
                        identity=idT[:]),
                    waits=w, inc="sPE")
            ck_tp_g[g] = ck_pe
            ck_ev = p.add("scalar", lambda e, fc=fc, g=g, bank=bank: e.copy(
                out=xT[:, fc * 2048 + g * 512: fc * 2048 + (g + 1) * 512],
                in_=P0[:, bank * 512:(bank + 1) * 512]),
                waits=[("sPE", ck_pe)], inc="sACT")
            ck_xe_bank[bank] = ("sACT", ck_ev)
    ck_phase1_pe = ck_tp_g[1]
    ck_xT_done = [ck_xe_bank[b] for b in range(4)]
    ck_tp3 = [None]

    def emit_tp_late():
        nonlocal ck_phase1_pe
        for g, srcs, sems in ((2, [XL(j) for j in range(4)], ck_xg2),
                              (3, XG3, ck_xg3)):
            for fc in range(8):
                bank = 2 + fc % 2
                ck_pe = None
                for j in range(4):
                    w = [(sn, ck) for sn, ck in sems.items()]
                    if ck_xe_bank[bank] is not None:
                        w.append(ck_xe_bank[bank])
                    ck_pe = p.add(
                        "tensor",
                        lambda e, s=srcs[j], fc=fc, bank=bank, j=j:
                        e.transpose(
                            out=P0[:, bank * 512 + j * 128:
                                   bank * 512 + (j + 1) * 128],
                            in_=s[:, fc * 128:(fc + 1) * 128],
                            identity=idT[:]),
                        waits=w, inc="sPE")
                ck_ev = p.add("scalar", lambda e, fc=fc, bank=bank, g=g:
                              e.copy(
                                  out=xT[:, fc * 2048 + g * 512:
                                         fc * 2048 + (g + 1) * 512],
                                  in_=P0[:, bank * 512:(bank + 1) * 512]),
                              waits=[("sPE", ck_pe)], inc="sACT")
                ck_xe_bank[bank] = ("sACT", ck_ev)
            if g == 2:
                ck_phase1_pe = ck_pe
        ck_tp3[0] = ck_pe
        ck_xT_done[2] = ck_xe_bank[2]
        ck_xT_done[3] = ck_xe_bank[3]
    for s in range(2):
        ck_q_ld[s] = p.dma("gpsimd", WQ(s), d_kq[s * 128:(s + 1) * 128, :],
                           waits=[("sDVE", ck_la)], sem=f"sQ{s % 2}")
    if tap:
        emit_tp_late()
        p.dma("gpsimd", tapd["t_xT"], xT[:], waits=ck_xT_done, sem="sTAP")

    wq_rot = 8
    ck_wo_cast = None

    def wo_slice(s):
        nonlocal ck_wo_cast
        if ck_wo_cast is None:
            w = [("sDVE", ck_q_red[6])]
        else:
            w = [("sACT", ck_wo_cast)]
        ck_ld = p.dma("sync", WQ(0), d_wo[s * 128:(s + 1) * 128, :],
                      waits=w, sem="sQW0")
        ck_wo_cast = p.add("scalar", lambda e, s=s: e.copy(
            out=woB[:, s * 1024:(s + 1) * 1024], in_=WQ(0)),
            waits=[("sQW0", ck_ld)], inc="sACT")

    Wr = [P0[:, 0:1024], P0[:, 1024:2048]]
    Kr = P1[:, 0:1024]
    Ur = P1[:, 1024:2048]
    Bq2[1] = U[:, 6144:7168]
    Bu2[1] = U[:, 8192:9216]
    Bb2 = U[:, 7168:8192]

    ck_castw = [None] * 3
    ck_wld = [None] * 3
    ck_wst_cast = [None] * 3
    ck_wbf_used = [None] * 3

    def weights_dma(c):
        for i, src_ap in enumerate(w_srcs(c)):
            w = []
            if ck_wst_cast[i] is not None:
                w.append(("sACT", ck_wst_cast[i]))
            else:
                w.append(("sPE", ck_phase1_pe))
                w.append(("sDVE", ck_lb))
            ck_wld[i] = p.dma(
                "sync" if i == 0 else "scalar",
                XL(i).rearrange("p (kc n) -> p kc n", n=128),
                src_ap, waits=w, sem=f"sWS{i}")

    def weights_cast():
        for i in range(3):
            wpe = [(f"sWS{i}", ck_wld[i])]
            if ck_wbf_used[i] is not None:
                wpe.append(("sPE", ck_wbf_used[i]))
            ck_wst_cast[i] = ck_castw[i] = p.add(
                "scalar", lambda e, i=i: e.copy(out=wbf[i][:], in_=XL(i)),
                waits=wpe, inc="sACT")

    ck_g_pr = [None, None]
    ck_q_last = None
    ck_uev_last = None
    ck_carry_pr = [None, None]
    ck_castA_pr = [None, None]
    ck_fA_pr = [None, None]
    ck_casty = [None] * 8
    ck_castA = [None] * 8
    ck_cy = [None] * 8
    ck_cA = [None] * 8
    pend = []
    ck_ccinA = [None]
    ck_fy_last = None
    ck_qmax = [None]
    ck_qscale = [None]
    ck_qcols = [None]

    def emit_pend():
        while pend:
            pend.pop(0)()

    ck_castw = list(ck_castw_c0)

    for c in range(8):
        for th in range(2):
            i = 2 * c + th
            pr = i % 2

            def mm_block(tgt, region, extra, th=th):
                ckl = None
                for kc in range(8):
                    for nt in range(2):
                        ckl = p.add(
                            "tensor",
                            lambda e, tgt=tgt, region=region, kc=kc, nt=nt,
                            th=th: e.matmul(
                                out=region[:, nt * 512:(nt + 1) * 512],
                                lhsT=wbf[tgt][:, kc * 128:(kc + 1) * 128],
                                rhs=xT[:, kc * 2048 + th * 1024 + nt * 512:
                                       kc * 2048 + th * 1024 +
                                       (nt + 1) * 512],
                                start=(kc == 0), stop=(kc == 7)),
                            waits=extra if (kc == 0 and nt == 0) else (),
                            inc="sPE" if (kc == 7 and nt == 1) else None)
                return ckl

            ew = [("sACT", ck_castw[0])] if th == 0 else []
            if i == 10:
                ew.append(("sACT", ck_qcols[0]))
            if ck_g_pr[pr] is not None:
                ew.append(("sACT", ck_g_pr[pr]))
            elif i == 0:
                ew += list(ck_xT_done)
            else:
                ew += list(ck_xT_done[2:4])
            ck_w = mm_block(0, Wr[pr], ew)

            ek = [("sACT", ck_castw[1])] if th == 0 else []
            if ck_q_last is not None:
                ek.append(("sACT", ck_q_last))
            ck_k = mm_block(1, Kr, ek)

            eu = [("sACT", ck_castw[2])] if th == 0 else []
            if ck_uev_last is not None:
                eu.append(("sDVE", ck_uev_last))
            elif i == 0:
                eu.append(("sACT", ck_bias_evac))
            ck_u = mm_block(2, Ur, eu)
            if th == 1:
                ck_wbf_used = [ck_w, ck_k, ck_u]

            if th == 0 and 0 < c < 7:
                weights_dma(c + 1)
            if i == 0:
                if ck_tp3[0] is None:
                    emit_tp_late()
                weights_dma(1)

            Bgx, Bqx, Bux = Bg2[pr], Bq2[pr], Bu2[pr]
            wg = [("sPE", ck_w)]
            if ck_carry_pr[pr] is not None:
                wg.append(("sDVE", ck_carry_pr[pr]))
            ck_g = p.add("scalar", lambda e, c=c, Bgx=Bgx, region=Wr[pr]:
                         e.activation(
                             out=Bgx[:], in_=region, func=AF.Sigmoid,
                             bias=bcol[:, c:c + 1], scale=1.0),
                         waits=wg, inc="sACT")
            ck_g_pr[pr] = ck_g
            emit_pend()
            wq2 = [("sPE", ck_k)]
            if ck_fA_pr[pr] is not None:
                wq2.append(("sDVE", ck_fA_pr[pr]))
            elif i == 0:
                wq2.append(("sPE", ck_bias_pe))
                wq2.append(("sDVE", ck_bpk2))
            ck_qe = p.add("scalar", lambda e, c=c, Bqx=Bqx: e.activation(
                out=Bqx[:], in_=Kr, func=AF.Sigmoid,
                bias=bcol[:, 8 + c:9 + c], scale=-2.0),
                waits=wq2, inc="sACT")
            ck_q_last = ck_qe
            if th == 1 and c < 7:
                weights_cast()
            if i >= 8:
                wo_slice(i - 8)
            if i == 8:
                passB_load(0, extra=[("sACT", ck_castw_c0[0]),
                                     ("sPE", ck_tp3[0])])
            if 10 <= i <= 13:
                quant_ph2(i - 10)
                passB_load(i - 9)

            wu = [("sPE", ck_u)]
            if ck_castA_pr[pr] is not None:
                wu.append(("sACT", ck_castA_pr[pr]))
            if i == 0:
                wu.append(("sACT", ck_bias_evac))
            if i == 1:
                wu.append(("sACT", ck_castw_c0[2]))
            ck_uev = p.add("vector", lambda e, c=c, Bux=Bux: e.tensor_scalar(
                out=Bux[:], in0=Ur, scalar1=bcol[:, 16 + c:17 + c],
                scalar2=None, op0=ALU.add), waits=wu, inc="sDVE")
            ck_uev_last = ck_uev

            ck_stt = p.add("vector", lambda e, Bgx=Bgx, Bqx=Bqx:
                           e.scalar_tensor_tensor(
                               out=Bb[:], in0=Bgx[:], scalar=-2.0,
                               in1=Bqx[:], op0=ALU.mult, op1=ALU.mult),
                           waits=[("sACT", ck_g), ("sACT", ck_qe)],
                           inc="sDVE")
            ck_aa = p.add("scalar", lambda e, Bqx=Bqx: e.activation(
                out=Bqx[:], in_=Bb[:], func=AF.Copy, bias=1.0, scale=1.0),
                waits=[("sDVE", ck_stt)], inc="sACT")
            wb = [("sDVE", ck_stt)]
            if i == 0:
                wb.append(("sACT", ck_castw_c0[1]))
                wb.append(("sPE", ck_tp3[0]))
            ck_b = p.add("vector", lambda e, Bgx=Bgx, Bux=Bux:
                         e.tensor_tensor(out=Bb2, in0=Bgx[:], in1=Bux[:],
                                         op=ALU.mult),
                         waits=wb, inc="sDVE")
            if th == 0:
                ini_y, wy_extra = 0.0, []
            else:
                ini_y = carry[:, c:c + 1]
                wy_extra = [("sDVE", ck_cy[c])]
            p.add("vector", lambda e, ini_y=ini_y, Bgx=Bgx, Bqx=Bqx:
                  e.tensor_tensor_scan(
                      out=Bgx[:], data0=Bqx[:], data1=Bb2, initial=ini_y,
                      op0=ALU.mult, op1=ALU.add),
                  waits=[("sDVE", ck_b), ("sACT", ck_aa)] + wy_extra,
                  inc="sDVE")
            ck_fy = ck_fy_last = flush_dve()
            wcy = [("sGL", ck_ccinA[0])] if i >= 14 else []
            ck_cy[c] = ck_carry_pr[pr] = p.add(
                "vector", lambda e, c=c, Bgx=Bgx: e.tensor_copy(
                    out=carry[:, c:c + 1], in_=Bgx[:, 1023:1024]),
                waits=wcy, inc="sDVE")
            if th == 0:
                ini_a = 1.0
                wa_extra = [("sDVE", ck_zt)] if i == 0 else []
            else:
                ini_a = carry[:, 8 + c:9 + c]
                wa_extra = [("sDVE", ck_cA[c])]
            p.add("vector", lambda e, ini_a=ini_a, Bqx=Bqx, Bux=Bux:
                  e.tensor_tensor_scan(
                      out=Bux[:], data0=Bqx[:], data1=zt[:], initial=ini_a,
                      op0=ALU.mult, op1=ALU.add),
                  waits=wa_extra, inc="sDVE")
            ck_fa = flush_dve()
            ck_fA_pr[pr] = ck_fa
            ck_cA[c] = p.add("vector", lambda e, c=c, Bux=Bux: e.tensor_copy(
                out=carry[:, 8 + c:9 + c], in_=Bux[:, 1023:1024]),
                inc="sDVE")
            if i < 8:
                passA_reduce(i)
                if i <= 5:
                    passA_load(i + 2)
            if i == 13:
                ck_fcarA = flush_dve()
                ck_ccinA[0] = p.dma("sync", cc_in.ap(), carry[:, 0:8],
                                    waits=[("sDVE", ck_fcarA)], sem="sGL")
                p.wait_only("gpsimd", [("sGL", ck_ccinA[0])])
                p.add("gpsimd", lambda e: e.collective_compute(
                    "AllGather", ALU.bypass,
                    replica_groups=[list(range(NCORES))],
                    ins=[cc_in.ap().opt()], outs=[cc_out.ap().opt()]),
                    inc="sCC")
            if i == 7:
                p.add("vector", lambda e: e.reduce_max(
                    out=qsc[:, 8:9], in_=qsc[:, 0:8], axis=AX.X,
                    apply_absolute_value=False), inc="sDVE")
                ck_qmax[0] = flush_dve()
            if i == 8:
                ck_qtp = p.add("tensor", lambda e: e.transpose(
                    out=P0[0:1, 16:144], in_=qsc[:, 8:9], identity=idT[:]),
                    waits=[("sDVE", ck_qmax[0]),
                           ("sACT", ck_g_pr[0])], inc="sPE")
                ck_qrow = p.add("scalar", lambda e: e.copy(
                    out=qrow[0:1, 0:128], in_=P0[0:1, 16:144]),
                    waits=[("sPE", ck_qtp)], inc="sACT")
                p.add("vector", lambda e: e.reduce_max(
                    out=qrow[0:1, 128:129], in_=qrow[0:1, 0:128],
                    axis=AX.X, apply_absolute_value=False),
                    waits=[("sACT", ck_qrow)], inc="sDVE")
                p.add("vector", lambda e: e.reciprocal(
                    out=qrow[0:1, 132:133], in_=qrow[0:1, 131:132]),
                    inc="sDVE")
                flush_dve()
                p.add("vector", lambda e: e.tensor_tensor(
                    out=qrow[0:1, 129:130], in0=qrow[0:1, 128:129],
                    in1=qrow[0:1, 132:133], op=ALU.mult), inc="sDVE")
                flush_dve()
                p.add("vector", lambda e: e.reciprocal(
                    out=qrow[0:1, 130:131], in_=qrow[0:1, 129:130]),
                    inc="sDVE")
                ck_qscale[0] = flush_dve()
            if i == 9:
                ck_qb2 = p.add("tensor", lambda e: e.matmul(
                    out=P0[:, 16:17], lhsT=onest[0:1, :],
                    rhs=qrow[0:1, 129:130], start=True, stop=True),
                    waits=[("sDVE", ck_qscale[0])], inc="sPE")
                ck_qb2 = p.add("tensor", lambda e: e.matmul(
                    out=P0[:, 17:18], lhsT=onest[0:1, :],
                    rhs=qrow[0:1, 130:131], start=True, stop=True),
                    inc="sPE")
                ck_qcols[0] = p.add("scalar", lambda e: e.copy(
                    out=qsc[:, 9:11], in_=P0[:, 16:18]),
                    waits=[("sPE", ck_qb2)], inc="sACT")

            def mk_casts(c=c, th=th, pr=pr, Bgx=Bgx, Bux=Bux,
                         ck_fy=ck_fy, ck_fa=ck_fa):
                def do():
                    ck_casty[c] = p.add("scalar", lambda e: e.copy(
                        out=yB[:, c * 2048 + th * 1024:
                               c * 2048 + (th + 1) * 1024], in_=Bgx[:]),
                        waits=[("sDVE", ck_fy)], inc="sACT")
                    ck_castA[c] = ck_castA_pr[pr] = p.add(
                        "scalar", lambda e: e.copy(
                            out=aB[:, c * 2048 + th * 1024:
                                   c * 2048 + (th + 1) * 1024], in_=Bux[:]),
                        waits=[("sDVE", ck_fa)], inc="sACT")
                return do
            pend.append(mk_casts())
    emit_pend()
    ck_pool_casty = p.cnt["sACT"]
    ck_qt_ld = {}
    qt_sem = {6: "sQT0", 7: "sQT1", 4: "sQT2", 5: "sQT3"}
    for s, buf in ((6, Bq), (7, Bu)):
        ck_qt_ld[s] = p.dma(
            "gpsimd", buf[:], d_kq[s * 128:(s + 1) * 128, :],
            waits=[("sDVE", ck_fA_pr[0]), ("sACT", ck_pool_casty)],
            sem=qt_sem[s])

    ck_qtail_prev = [None]

    def quant_tail(s, buf):
        nonlocal ck_quant_act
        ck_v = p.add("scalar", lambda e, buf=buf: e.activation(
            out=buf[:], in_=buf[:], func=AF.Copy, bias=0.0,
            scale=qsc[:, 10:11]),
            waits=[(qt_sem[s], ck_qt_ld[s])], inc="sACT")
        ck_r = None
        for hh in range(2):
            p.add("scalar", lambda e, buf=buf, hh=hh: e.copy(
                out=QT[:], in_=buf[:, hh * 512:(hh + 1) * 512]), inc="sACT")
            wt2 = []
            if hh == 0:
                wt2 = [("sDVE", ck_qtail_prev[0] if ck_qtail_prev[0]
                        is not None else ck_corr_c[7])]
            ck_r = p.add("scalar", lambda e, hh=hh: e.activation(
                out=Bb[:, hh * 512:(hh + 1) * 512], in_=QT[:],
                func=AF.Copy, bias=-1.0, scale=1.0),
                waits=wt2, inc="sACT")
        p.add("vector", lambda e, buf=buf: e.tensor_tensor(
            out=buf[:], in0=buf[:], in1=Bb[:], op=ALU.subtract),
            waits=[("sACT", ck_r)], inc="sDVE")
        p.add("vector", lambda e, buf=buf: e.scalar_tensor_tensor(
            out=Bb[:], in0=buf[:], scalar=1.5, in1=Bb[:],
            op0=ALU.is_gt, op1=ALU.add), inc="sDVE")
        p.add("vector", lambda e, buf=buf: e.scalar_tensor_tensor(
            out=Bb[:], in0=buf[:], scalar=0.5, in1=Bb[:],
            op0=ALU.is_ge, op1=ALU.add), inc="sDVE")
        ck_clip = p.add("vector", lambda e: e.tensor_scalar(
            out=Bb[:], in0=Bb[:], scalar1=-8.0, scalar2=7.0,
            op0=ALU.max, op1=ALU.min), inc="sDVE")
        ck_qtail_prev[0] = ck_clip
        ck_quant_act = p.add("scalar", lambda e, s=s: e.activation(
            out=kqB[:, s * 1024:(s + 1) * 1024], in_=Bb[:],
            func=AF.Copy, bias=0.0, scale=qsc[:, 9:10]),
            waits=[("sDVE", ck_clip)], inc="sACT")
    free_T = {0: ("sACT", ck_g_pr[0]), 1: ("sACT", ck_q_last)}

    ck_fcar = flush_dve()
    ck_ccinB = p.dma("sync", cc_in2.ap(), carry[:, 0:8],
                     waits=[("sDVE", ck_fcar)], sem="sGL")
    p.wait_only("gpsimd", [("sGL", ck_ccinB)])
    p.add("gpsimd", lambda e: e.collective_compute(
        "AllGather", ALU.bypass, replica_groups=[list(range(NCORES))],
        ins=[cc_in2.ap().opt()], outs=[cc_out2.ap().opt()]), inc="sCC")
    p.wait_only("gpsimd", [("sCC", 2)])
    ck_G = p.dma("sync", Gt,
                 cc_out.ap().rearrange("(r p) c -> r (p c)", p=128),
                 waits=[("sACT", ck_pool_casty), ("sCC", 1)], sem="sGL2")
    Gtv = Bg[0:8, 0:1024].rearrange("r (p c) -> r c p", c=8)
    ck_mm_sin = None
    for c8 in range(7):
        w = [("sGL2", ck_G), ("sS", 64)]
        if c8 == 0:
            w.append(("sACT", ck_q_last))
        ck_mm_sin = p.add("tensor", lambda e, c8=c8: e.matmul(
            out=P1[:, c8:c8 + 1],
            lhsT=Gtv[:, c8],
            rhs=mskT[0:8, 0:1], start=True, stop=True),
            waits=w, inc="sPE")
    p.add("scalar", lambda e: e.copy(
        out=carry[:, 16:23], in_=P1[:, 0:7]),
        waits=[("sPE", ck_mm_sin)], inc="sACT")
    ck_sin = flush_act()
    ck_corr_c = [None] * 8
    for c in range(7):
        ck_corr_c[c] = p.add(
            "vector", lambda e, c=c: e.scalar_tensor_tensor(
                out=yB[:, c * 2048:(c + 1) * 2048],
                in0=aB[:, c * 2048:(c + 1) * 2048],
                scalar=carry[:, 16 + c:17 + c],
                in1=yB[:, c * 2048:(c + 1) * 2048],
                op0=ALU.mult, op1=ALU.add),
            waits=[("sACT", ck_sin), ("sACT", ck_castA[c]),
                   ("sACT", ck_casty[c])], inc="sDVE")
    gB = {}

    def emit_gatherB():
        if gB:
            return
        ck_GB = p.dma("sync", Gt,
                      cc_out2.ap().rearrange("(r p) c -> r (p c)", p=128),
                      waits=[("sCC", 2), ("sPE", ck_mm_sin)], sem="sGL2")
        ck_mm_sinB = p.add("tensor", lambda e: e.matmul(
            out=P1[:, 7:8], lhsT=Gtv[:, 7],
            rhs=mskT[0:8, 0:1], start=True, stop=True),
            waits=[("sGL2", ck_GB)], inc="sPE")
        p.add("scalar", lambda e: e.copy(
            out=carry[:, 23:24], in_=P1[:, 7:8]),
            waits=[("sPE", ck_mm_sinB)], inc="sACT")
        gB["sinB"] = flush_act()
        ck_corr_c[7] = gB["corr"] = p.add(
            "vector", lambda e: e.scalar_tensor_tensor(
                out=yB[:, 7 * 2048:8 * 2048],
                in0=aB[:, 7 * 2048:8 * 2048],
                scalar=carry[:, 23:24],
                in1=yB[:, 7 * 2048:8 * 2048],
                op0=ALU.mult, op1=ALU.add),
            waits=[("sACT", gB["sinB"]), ("sACT", ck_castA[7]),
                   ("sACT", ck_casty[7])], inc="sDVE")
        quant_ph2(4)
        passB_load(5)
        quant_ph2(5)

    if tap:
        emit_gatherB()
        p.dma("gpsimd", tapd["t_y"], yB[:], waits=[("sDVE", gB["corr"])],
              sem="sTAP")
        p.dma("gpsimd", tapd["t_A"], aB[:], waits=[("sDVE", gB["corr"])],
              sem="sTAP")
        p.dma("gpsimd", tapd["t_carry"], carry[:], waits=[("sACT", ck_sin)], sem="sTAP")
        p.dma("gpsimd", tapd["t_wo"], woB[:], waits=[("sACT", ck_wo_cast)], sem="sTAP")
        p.wait_only("scalar", [("sTAP", p.cnt["sTAP"])])


    tp_rot = [None] * 4
    po_use = [None] * 4
    ln_state = {"lnt_pe": [None, None], "lnt_cast": [None, None],
                "rstt": [None, None], "std": None}

    LNWv = LNW.rearrange("p (a b) -> p a b", b=8)

    def ln_pair(pars, store_tiles=None, extra_apply_waits=()):
        for par in pars:
            p8 = par * 8
            p.add("vector", lambda e, par=par, p8=p8: e.scalar_tensor_tensor(
                out=SCR, in0=RT(par), scalar=1.0, in1=RT(par),
                op0=ALU.mult, op1=ALU.mult,
                accum_out=LNW[:, p8 + 2:p8 + 3]), inc="sDVE")
        p.add("vector", lambda e: e.tensor_tensor(
            out=LNWv[:, :, 3], in0=LNWv[:, :, 0], in1=LNWv[:, :, 1],
            op=ALU.add), inc="sDVE")
        p.add("vector", lambda e: e.tensor_scalar(
            out=LNWv[:, :, 4], in0=LNWv[:, :, 3],
            scalar1=1.0 / 1024.0, scalar2=None, op0=ALU.mult), inc="sDVE")
        p.add("vector", lambda e: e.scalar_tensor_tensor(
            out=LNWv[:, :, 5], in0=LNWv[:, :, 4], scalar=-1.0,
            in1=LNWv[:, :, 4], op0=ALU.mult, op1=ALU.mult), inc="sDVE")
        p.add("vector", lambda e: e.tensor_scalar(
            out=LNWv[:, :, 6], in0=LNWv[:, :, 5],
            scalar1=EPS, scalar2=None, op0=ALU.add), inc="sDVE")
        p.add("vector", lambda e: e.tensor_scalar(
            out=LNWv[:, :, 7], in0=LNWv[:, :, 2],
            scalar1=1.0 / 1024.0, scalar2=None, op0=ALU.mult), inc="sDVE")
        ck_b = p.add("vector", lambda e: e.tensor_tensor(
            out=LNWv[:, :, 7], in0=LNWv[:, :, 7], in1=LNWv[:, :, 6],
            op=ALU.add), inc="sDVE")
        p.add("scalar", lambda e: e.activation(
            out=LNWv[:, :, 5], in_=LNWv[:, :, 7],
            func=AF.Ln, bias=0.0, scale=1.0),
            waits=[("sDVE", ck_b)], inc="sACT")
        p.add("scalar", lambda e: e.activation(
            out=LNWv[:, :, 5], in_=LNWv[:, :, 5],
            func=AF.Exp, bias=0.0, scale=-0.5), inc="sACT")
        ck_std = flush_act()
        ln_state["std"] = ck_std
        cks = {}
        for par in pars:
            p8 = par * 8
            wl = list(extra_apply_waits)
            if store_tiles is not None:
                wl.append(("sST", p.cnt["sST"]))
            prev = ln_state["lnt_cast"][par]
            if prev is not None:
                wl.append(prev)
            prev_pe = ln_state["lnt_pe"][par]
            if prev_pe is not None:
                wl.append(("sPE", prev_pe))
            wl.append(("sACT", ck_std))
            cks[par] = p.add("vector", lambda e, par=par, p8=p8:
                             e.tensor_scalar(
                                 out=LNT(par), in0=RT(par),
                                 scalar1=LNW[:, p8 + 4:p8 + 5],
                                 scalar2=LNW[:, p8 + 5:p8 + 6],
                                 op0=ALU.subtract, op1=ALU.mult),
                             waits=wl, inc="sDVE")
        return cks

    p.dma("sync", U[0:1, 7168:8192], d_bo,
          waits=[("sDVE", ck_fy_last)], sem="sBR")
    p.dma("sync", U[32:33, 7168:8192], d_fb, sem="sBR")
    ck_brows = p.cnt["sBR"]
    bo_bf = brow[0:1, 0:1024]
    fb_bf = brow[0:1, 1024:2048]
    ones_bf = brow[0:1, 2048:2176]
    p.add("scalar", lambda e: e.copy(out=bo_bf, in_=U[0:1, 7168:8192]),
          waits=[("sBR", ck_brows)], inc="sACT")
    p.add("scalar", lambda e: e.copy(out=fb_bf, in_=U[32:33, 7168:8192]),
          inc="sACT")
    p.add("vector", lambda e: e.memset(ones_bf, 1.0), inc="sDVE")
    ck_brows_bf = p.cnt["sACT"]
    ck_ones_bf = p.cnt["sDVE"]

    ck_tile_rstt = [None, None]
    ck_xr = [None, None]
    ck_lnF = None
    tp_idx = 0

    def tok_tile_mms(tt, lhs_fn, rhs_tile, bias_row, extra_w, extra_last=(),
                     kc_waits=None, mid_hook=None):
        sl = tt % 2
        w = list(extra_w)
        if ck_tile_rstt[sl] is not None:
            w.append(("sDVE", ck_tile_rstt[sl]))
        for kc in range(8):
            if kc == 7 and mid_hook is not None:
                mid_hook()
            for h in range(2):
                wl = list(w) if (kc == 0 and h == 0) else []
                if kc_waits is not None and h == 0:
                    wl.append(kc_waits(kc))
                p.add("tensor", lambda e, kc=kc, h=h, sl=sl, tt=tt: e.matmul(
                    out=P0[:, sl * 1024 + h * 512: sl * 1024 + (h + 1) * 512],
                    lhsT=lhs_fn(kc, tt),
                    rhs=rhs_tile[:, kc * 1024 + h * 512:
                                 kc * 1024 + (h + 1) * 512],
                    start=(kc == 0), stop=False),
                    waits=wl, inc=None)
        ck = None
        for h in range(2):
            wl = [("sACT", ck_brows_bf), ("sDVE", ck_ones_bf)] + \
                list(extra_last)
            ck = p.add("tensor", lambda e, h=h, sl=sl, bias_row=bias_row:
                       e.matmul(
                           out=P0[:, sl * 1024 + h * 512:
                                  sl * 1024 + (h + 1) * 512],
                           lhsT=ones_bf,
                           rhs=bias_row[0:1, h * 512:(h + 1) * 512],
                           start=False, stop=True),
                       waits=wl if h == 0 else (),
                       inc="sPE" if h == 1 else None)
        return ck

    tp_jobs = []

    def emit_tp(tt, par, ck_apply):
        nonlocal ck_lnF, tp_idx
        ck_pe_last = None
        for kg in range(2):
            bank = 4 + (tp_idx % 4)
            tpw = [("sDVE", ck_apply)]
            if tp_rot[tp_idx % 4] is not None:
                tpw.append(tp_rot[tp_idx % 4])
            else:
                tpw.append(("sACT", gB["sinB"]))
            ck_pe = None
            for i in range(4):
                ck_pe = p.add("tensor", lambda e, kg=kg, i=i, par=par,
                              bank=bank: e.transpose(
                                  out=P1[:, (bank - 4) * 512 + i * 128:
                                         (bank - 4) * 512 +
                                         (i + 1) * 128],
                                  in_=LNT(par)[:, (kg * 4 + i) * 128:
                                               (kg * 4 + i + 1) * 128],
                                  identity=idT[:]),
                              waits=tpw if i == 0 else (), inc="sPE")
            ck_lnF = p.add("scalar", lambda e, kg=kg, tt=tt, bank=bank:
                           e.copy(
                               out=xT.rearrange(
                                   "p (kc t) -> p kc t",
                                   t=2048)[:, kg * 4:(kg + 1) * 4,
                                           tt * 128:(tt + 1) * 128],
                               in_=P1[:, (bank - 4) * 512:
                                      (bank - 4 + 1) * 512].rearrange(
                                   "p (kc t) -> p kc t", t=128)),
                           waits=[("sPE", ck_pe)], inc="sACT")
            tp_rot[tp_idx % 4] = ("sACT", ck_lnF)
            tp_idx += 1
            ck_pe_last = ck_pe
        ln_state["lnt_pe"][par] = ck_pe_last

    for pr in range(8):
        tiles = [2 * pr, 2 * pr + 1]
        p.add("vector", lambda e: e.memset(LNWv[:, :, 0:3], 0.0),
              waits=[("sACT", ln_state["std"])], inc="sDVE")
        for tt in tiles:
            sl = tt % 2
            extra_w = [("sACT", ck_wo_cast)]

            if tt == 1:
                extra_w.append(("sACT", ck_g_pr[1]))
            ck_pe = tok_tile_mms(
                tt, lambda kc, tt: yB[:, kc * 2048 + tt * 128:
                                      kc * 2048 + (tt + 1) * 128],
                woB, bo_bf, extra_w,
                kc_waits=lambda kc: ("sDVE", ck_corr_c[kc]),
                mid_hook=emit_gatherB if tt == 0 else None)
            w = []
            if ck_tile_rstt[sl] is not None:
                w.append(("sDVE", ck_tile_rstt[sl]))
            if tt < 2:
                w.append(("sACT", ck_wo_cast))
            if tt == 1:
                w.append(("sACT", ck_quant_act))
            ck_xr[sl] = p.dma("sync", XR(sl),
                              d_x[tt * 128:(tt + 1) * 128, :],
                              waits=w, sem=f"sXR{sl}")
            ck_rstt = None
            for h in range(2):
                ck_rstt = p.add("vector", lambda e, h=h, sl=sl, p8=sl * 8:
                                e.scalar_tensor_tensor(
                                    out=RT(sl)[:, h * 512:(h + 1) * 512],
                                    in0=P0[:, sl * 1024 + h * 512:
                                           sl * 1024 + (h + 1) * 512],
                                    scalar=1.0,
                                    in1=XR(sl)[:, h * 512:(h + 1) * 512],
                                    op0=ALU.mult, op1=ALU.add,
                                    accum_out=LNW[:, p8 + h:p8 + h + 1]),
                                waits=[("sPE", ck_pe),
                                       (f"sXR{sl}", ck_xr[sl])],
                                inc="sDVE")
            ck_tile_rstt[sl] = ck_rstt
            ln_state["rstt"][sl] = ck_rstt
        for job in tp_jobs:
            emit_tp(*job)
        tp_jobs = []
        cks = ln_pair([0, 1])
        for tt in tiles:
            par = tt % 2
            ck_apply = cks[par]
            ck_cast = p.add("scalar", lambda e, tt=tt, par=par: e.copy(
                out=aB[:, tt * 1024:(tt + 1) * 1024], in_=LNT(par)),
                waits=[("sDVE", ck_apply), ("sDVE", ck_corr_c[tt // 2])],
                inc="sACT")
            ln_state["lnt_cast"][par] = ("sACT", ck_cast)
            tp_jobs.append((tt, par, ck_apply))
        if pr == 1:
            quant_tail(6, Bq)
        if pr == 2:
            quant_tail(7, Bu)
    for job in tp_jobs:
        emit_tp(*job)
    tp_jobs = []

    ck_z_ev = None
    for nt in range(4):
        bank = nt % 4
        w = [("sACT", ck_lnF),
             ("sDVE", ck_tile_rstt[0]), ("sDVE", ck_tile_rstt[1])]
        ck_pe = None
        for kc in range(8):
            ck_pe = p.add("tensor", lambda e, kc=kc, nt=nt, bank=bank: e.matmul(
                out=P0[0:16, bank * 512:(bank + 1) * 512],
                lhsT=laB[:, kc * 16:(kc + 1) * 16],
                rhs=xT[:, kc * 2048 + nt * 512: kc * 2048 + (nt + 1) * 512],
                start=(kc == 0), stop=(kc == 7)),
                waits=w if kc == 0 else (), inc="sPE" if kc == 7 else None)
        ck_z_ev = p.add("scalar", lambda e, nt=nt, bank=bank: e.copy(
            out=Z(nt), in_=P0[0:16, bank * 512:(bank + 1) * 512]),
            waits=[("sPE", ck_pe)], inc="sACT")
    ck_tile_rstt = [None, None]

    for pr in range(8):
        tiles = [2 * pr, 2 * pr + 1]
        p.add("vector", lambda e: e.memset(LNWv[:, :, 0:3], 0.0),
              waits=[("sACT", ln_state["std"])], inc="sDVE")
        for tt in tiles:
            sl = tt % 2
            nt = tt // 4
            extra_w = [("sACT", ck_quant_act), ("sACT", ck_z_ev)]
            w = list(extra_w)
            if ck_tile_rstt[sl] is not None:
                w.append(("sDVE", ck_tile_rstt[sl]))
            for kc in range(8):
                for h in range(2):
                    p.add("tensor", lambda e, kc=kc, h=h, sl=sl, tt=tt:
                          e.matmul(
                              out=P0[:, sl * 1024 + h * 512:
                                     sl * 1024 + (h + 1) * 512],
                              lhsT=xT[:, kc * 2048 + tt * 128:
                                      kc * 2048 + (tt + 1) * 128],
                              rhs=kqB[:, kc * 1024 + h * 512:
                                      kc * 1024 + (h + 1) * 512],
                              start=(kc == 0), stop=False),
                          waits=w if (kc == 0 and h == 0) else (), inc=None)
            for h in range(2):
                p.add("tensor", lambda e, h=h, sl=sl, tt=tt, nt=nt: e.matmul(
                    out=P0[:, sl * 1024 + h * 512: sl * 1024 + (h + 1) * 512],
                    lhsT=Z(nt)[:, (tt % 4) * 128:(tt % 4 + 1) * 128],
                    rhs=lbB[0:16, h * 512:(h + 1) * 512],
                    start=False, stop=False),
                    waits=[("sDVE", ck_lb)] if h == 0 else (), inc=None)
            ck_pe = None
            for h in range(2):
                ck_pe = p.add("tensor", lambda e, h=h, sl=sl: e.matmul(
                    out=P0[:, sl * 1024 + h * 512: sl * 1024 + (h + 1) * 512],
                    lhsT=ones_bf,
                    rhs=fb_bf[0:1, h * 512:(h + 1) * 512],
                    start=False, stop=True),
                    waits=[("sACT", ck_brows_bf), ("sDVE", ck_ones_bf)]
                    if h == 0 else (), inc="sPE" if h == 1 else None)
            ck_rstt = None
            for h in range(2):
                ck_rstt = p.add("vector", lambda e, h=h, sl=sl, tt=tt,
                                p8=sl * 8: e.scalar_tensor_tensor(
                                    out=RT(sl)[:, h * 512:(h + 1) * 512],
                                    in0=P0[:, sl * 1024 + h * 512:
                                           sl * 1024 + (h + 1) * 512],
                                    scalar=1.0,
                                    in1=aB[:, tt * 1024 + h * 512:
                                           tt * 1024 + (h + 1) * 512],
                                    op0=ALU.mult, op1=ALU.add,
                                    accum_out=LNW[:, p8 + h:p8 + h + 1]),
                                waits=[("sPE", ck_pe)], inc="sDVE")
            ck_tile_rstt[sl] = ck_rstt
        cks = ln_pair([0, 1], store_tiles=True)
        for tt in tiles:
            par = tt % 2
            ck_st = p.dma("scalar", d_out[tt * 128:(tt + 1) * 128, :],
                          LNT(par), waits=[("sDVE", cks[par])], sem="sST")
            ln_state["lnt_cast"][par] = ("sST", ck_st)
            ln_state["lnt_pe"][par] = None

    if tap:
        lc = ln_state["lnt_cast"]
        p.dma("gpsimd", tapd["t_ln1"], aB[:],
              waits=[("sACT", p.cnt["sACT"]), ("sDVE", p.cnt["sDVE"])],
              sem="sTAP")
        p.dma("gpsimd", tapd["t_lnf"], xT[:],
              waits=[("sACT", p.cnt["sACT"])], sem="sTAP")
        p.dma("gpsimd", tapd["t_kq"], kqB[:],
              waits=[("sDVE", p.cnt["sDVE"]), ("sACT", p.cnt["sACT"])],
              sem="sTAP")
        p.dma("gpsimd", tapd["t_qs"], qsc[:, 0:11],
              waits=[("sDVE", p.cnt["sDVE"])], sem="sTAP")
        p.dma("gpsimd", tapd["t_qrow"], qrow[0:1, 0:133],
              waits=[("sDVE", p.cnt["sDVE"])], sem="sTAP")

    p.wait_only("sync", [("sST", p.cnt["sST"])])
    p.wait_only("gpsimd", [("sST", p.cnt["sST"])])
    p.wait_only("gpsimd", [("sGL", p.cnt["sGL"])])
    p.wait_only("gpsimd", [("sGL2", p.cnt["sGL2"])])
    if tap:
        p.wait_only("gpsimd", [("sTAP", p.cnt["sTAP"])])

    with nc.Block() as block:
        p.emit(block)
    return nc


_NC_CACHE = None


def kernel(L0_attn_x, L0_attn_Wp, L0_attn_bp, L0_attn_Wu, L0_attn_bu,
           L0_attn_Wo, L0_attn_bo, L0_ffn_kernel, L0_ffn_bias, L0_ffn_a,
           L0_ffn_b):
    global _NC_CACHE
    x = np.ascontiguousarray(np.asarray(L0_attn_x, dtype=np.float32))
    wk = np.ascontiguousarray(np.asarray(L0_attn_Wp, dtype=np.float32)[:, :2048])
    wu = np.ascontiguousarray(np.asarray(L0_attn_Wu, dtype=np.float32))
    wo = np.ascontiguousarray(np.asarray(L0_attn_Wo, dtype=np.float32))
    kq = np.ascontiguousarray(np.asarray(L0_ffn_kernel, dtype=np.float32))
    la = np.ascontiguousarray(np.asarray(L0_ffn_a, dtype=np.float32))
    lb = np.ascontiguousarray(np.asarray(L0_ffn_b, dtype=np.float32))
    bp = np.asarray(L0_attn_bp, dtype=np.float32)
    bpw = np.ascontiguousarray(bp[:1024].reshape(1, 1024))
    bpk = np.ascontiguousarray(bp[1024:2048].reshape(1, 1024))
    bu = np.ascontiguousarray(np.asarray(L0_attn_bu, np.float32).reshape(1, 1024))
    bo = np.ascontiguousarray(np.asarray(L0_attn_bo, np.float32).reshape(1, 1024))
    fb = np.ascontiguousarray(np.asarray(L0_ffn_bias, np.float32).reshape(1, 1024))
    ident = np.eye(128, dtype=np.float32)

    if _NC_CACHE is None:
        _NC_CACHE = build_nc()
    nc = _NC_CACHE

    in_maps = []
    for c in range(NCORES):
        b, h = c // 2, c % 2
        mask = np.zeros((8, 1), np.float32)
        if h == 1:
            mask[c - 1, 0] = 1.0
        in_maps.append(dict(
            xc=np.ascontiguousarray(x[b, h * TC:(h + 1) * TC, :]),
            wk=wk, wu=wu, wo=wo, kq=kq, la=la, lb=lb,
            bpw=bpw, bpk=bpk, bu=bu, bo=bo, fb=fb,
            mask=mask, ident=ident))
    res = run_bass_kernel_spmd(nc, in_maps, core_ids=list(range(NCORES)))
    out = np.empty((4, 4096, 1024), dtype=np.float32)
    for c in range(NCORES):
        b, h = c // 2, c % 2
        out[b, h * TC:(h + 1) * TC, :] = np.asarray(res.results[c]["yout"])
    return out


# version 29
# speedup vs baseline: 1.0557x; 1.0218x over previous
import contextlib
from collections import defaultdict

import numpy as np
import concourse.bass as bass
import concourse.mybir as mybir
from concourse.bass_utils import run_bass_kernel_spmd

F32 = mybir.dt.float32
BF16 = mybir.dt.bfloat16
I32 = mybir.dt.int32
AF = mybir.ActivationFunctionType
ALU = mybir.AluOpType
AX = mybir.AxisListType

NCORES = 8
D = 1024
TC = 2048
TH = 1024
EPS = 1e-5


class Plan:

    ENGINES = ("sync", "tensor", "scalar", "vector", "gpsimd")

    def __init__(self, nc, ctx):
        self.nc = nc
        self.steps = {e: [] for e in self.ENGINES}
        self.cnt = defaultdict(int)
        self.water = defaultdict(int)
        self.own = defaultdict(int)
        self.sems = {}
        self._ctx = ctx
        for s in ("sPE", "sACT", "sDVE", "sPOOL", "sCC"):
            self._sem(s)

    def _sem(self, s):
        if s not in self.sems:
            self.sems[s] = self._ctx.enter_context(self.nc.semaphore(s))
        return self.sems[s]

    def _mk_waits(self, engine, waits):
        out = []
        for s, v in waits:
            if v is None or v <= 0:
                continue
            if v <= self.water[(engine, s)]:
                continue
            self._sem(s)
            self.water[(engine, s)] = v
            out.append((self.sems[s], v))
        return out

    OWN = {"vector": "sDVE", "scalar": "sACT", "gpsimd": "sPOOL"}

    def add(self, engine, fn, waits=(), inc=None, inc_n=1, is_dma=False):
        waits = list(waits)
        own = self.OWN.get(engine)
        if own is not None and not is_dma:
            prev = self.own[(engine, own)]
            if prev > 0:
                waits.insert(0, (own, prev))
            if inc is None:
                inc = own
        wl = self._mk_waits(engine, waits)
        ck = None
        ih = None
        if inc is not None:
            ih = self._sem(inc)
            self.cnt[inc] += inc_n
            self.own[(engine, inc)] += inc_n
            ck = self.cnt[inc]

        def step(eng, fn=fn, wl=wl, ih=ih, inc_n=inc_n):
            for h, v in wl:
                eng.wait_ge(h, v)
            r = fn(eng)
            if ih is not None:
                r.then_inc(ih, inc_n)
        self.steps[engine].append(step)
        return ck

    def dma(self, queue, out_ap, in_ap, waits=(), sem=None, transpose=False,
            **kw):
        assert sem is not None
        if transpose:
            fn = lambda eng, o=out_ap, i=in_ap: eng.dma_start_transpose(
                out=o, in_=i)
        else:
            fn = lambda eng, o=out_ap, i=in_ap, kw=kw: eng.dma_start(
                out=o, in_=i, **kw)
        return self.add(queue, fn, waits=waits, inc=sem, inc_n=16,
                        is_dma=True)

    def wait_only(self, engine, waits):
        wl = self._mk_waits(engine, list(waits))
        if not wl:
            return

        def step(eng, wl=wl):
            for h, v in wl:
                eng.wait_ge(h, v)
        self.steps[engine].append(step)

    def emit(self, block):
        for e in self.ENGINES:
            body = self.steps[e]

            def runner(eng, body=body):
                for f in body:
                    f(eng)
            getattr(block, e)(runner)


def build_nc(tap=False):
    nc = bass.Bass("TRN2", target_bir_lowering=False)
    ctx = contextlib.ExitStack()
    nc._keepalive = ctx

    d_x = nc.dram_tensor("xc", [TC, D], F32, kind="ExternalInput").ap()
    d_wk = nc.dram_tensor("wk", [D, 2 * D], F32, kind="ExternalInput").ap()
    d_wu = nc.dram_tensor("wu", [D, D], F32, kind="ExternalInput").ap()
    d_wo = nc.dram_tensor("wo", [D, D], F32, kind="ExternalInput").ap()
    d_kq = nc.dram_tensor("kq", [D, D], F32, kind="ExternalInput").ap()
    d_la = nc.dram_tensor("la", [D, 16], F32, kind="ExternalInput").ap()
    d_lb = nc.dram_tensor("lb", [16, D], F32, kind="ExternalInput").ap()
    d_bpw = nc.dram_tensor("bpw", [1, D], F32, kind="ExternalInput").ap()
    d_bpk = nc.dram_tensor("bpk", [1, D], F32, kind="ExternalInput").ap()
    d_bu = nc.dram_tensor("bu", [1, D], F32, kind="ExternalInput").ap()
    d_bo = nc.dram_tensor("bo", [1, D], F32, kind="ExternalInput").ap()
    d_fb = nc.dram_tensor("fb", [1, D], F32, kind="ExternalInput").ap()
    d_mask = nc.dram_tensor("mask", [8, 1], F32, kind="ExternalInput").ap()
    d_id = nc.dram_tensor("ident", [128, 128], F32, kind="ExternalInput").ap()
    d_out = nc.dram_tensor("yout", [TC, D], F32, kind="ExternalOutput").ap()

    cc_in = nc.dram_tensor("cc_in", [128, 8], F32)
    cc_out = nc.dram_tensor("cc_out", [1024, 8], F32)
    cc_in2 = nc.dram_tensor("cc_in2", [128, 8], F32)
    cc_out2 = nc.dram_tensor("cc_out2", [1024, 8], F32)
    tapd = {}
    if tap:
        for nm, shape, dt in [
                ("t_xT", [128, 16384], BF16), ("t_y", [128, 16384], BF16),
                ("t_A", [128, 16384], BF16), ("t_carry", [128, 24], F32),
                ("t_kq", [128, 8192], BF16), ("t_wo", [128, 8192], BF16),
                ("t_ln1", [128, 16384], BF16),
                ("t_lnf", [128, 16384], BF16), ("t_qs", [128, 11], F32),
                ("t_qrow", [1, 133], F32)]:
            tapd[nm] = nc.dram_tensor(nm, shape, dt, kind="ExternalOutput").ap()

    def sb(name, shape, dt=F32):
        return ctx.enter_context(nc.sbuf_tensor(name, shape, dt))

    xT = sb("xT", [128, 8 * TC], BF16)
    yB = sb("yB", [128, 8 * TC], BF16)
    aB = sb("aB", [128, 8 * TC], BF16)
    U = sb("U", [128, 10240], F32)
    QT = sb("QT", [128, 512], I32)
    Bg = sb("Bg", [128, TH], F32)
    Bq = sb("Bq", [128, TH], F32)
    Bu = sb("Bu", [128, TH], F32)
    Bb = sb("Bb", [128, TH], F32)
    Bg2 = [Bg, None]

    def Z(nt):
        return wbf[nt // 2][0:16, (nt % 2) * 512:(nt % 2 + 1) * 512]
    Bq2 = [Bq, Bq]
    Bu2 = [Bu, Bu]
    zt = sb("zt", [128, TH], BF16)
    brow = sb("brow", [1, 2304], BF16)
    wbf = [sb(f"wbf{i}", [128, 1024], BF16) for i in range(3)]
    woB = sb("woB", [128, 8 * D], BF16)
    kqB = sb("kqB", [128, 8 * D], BF16)
    laB = sb("laB", [128, 128], BF16)
    lbB = sb("lbB", [16, D], BF16)
    bcol = sb("bcol", [128, 48], F32)
    carry = sb("carry", [128, 24], F32)
    slocT = sb("slocT", [8, 128], F32)
    mskT = sb("mskT", [8, 1], F32)
    idT = sb("idT", [128, 128], F32)
    onest = sb("onest", [1, 128], F32)
    qsc = sb("qsc", [128, 12], F32)
    qrow = sb("qrow", [1, 136], F32)
    LNW = sb("LNW", [128, 16], F32)
    flt = sb("flt", [128, 2], F32)
    Gt = Bg[0:8, 0:1024]
    rowb = Bq[0:1, 0:1024]

    def XL(j):
        return U[:, j * 1024:(j + 1) * 1024]

    def WQ(j):
        return U[:, 4096 + (j % 2) * 1024: 5120 + (j % 2) * 1024]

    def MO(d):
        return U[:, d * 512:(d + 1) * 512]

    def XR(j):
        return U[:, 4096 + (j % 2) * 1024: 5120 + (j % 2) * 1024]

    def RT(j):
        return U[:, 6144 + (j % 2) * 1024: 7168 + (j % 2) * 1024]

    def LNT(j):
        return U[:, 8192 + (j % 2) * 1024: 9216 + (j % 2) * 1024]

    SCR = Bg[:, 0:1024]
    Bg2[1] = U[:, 3 * 1024:4 * 1024]

    P0 = ctx.enter_context(nc.psum_tensor("P0", [128, 2048], F32))
    P1 = ctx.enter_context(nc.psum_tensor("P1", [128, 2048], F32))

    p = Plan(nc, ctx)

    def flush_dve(waits=()):
        return p.add("vector", lambda e: e.tensor_copy(out=flt[:, 0:1],
                                                       in_=flt[:, 1:2]),
                     waits=waits, inc="sDVE")

    def flush_act(waits=()):
        return p.add("scalar", lambda e: e.copy(out=flt[:, 0:1],
                                                in_=flt[:, 1:2]),
                     waits=waits, inc="sACT")

    p.dma("sync", idT[:], d_id, sem="sS")
    p.dma("sync", mskT[:], d_mask, sem="sS")
    p.dma("sync", U[0:16, 6144:7168], d_lb, sem="sS")
    ck_lb = p.add("vector",
                  lambda e: e.tensor_copy(out=lbB[:], in_=U[0:16, 6144:7168]),
                  waits=[("sS", 48)], inc="sDVE")
    p.dma("sync",
          U[:, 6144:6272].rearrange("p (kc j) -> p kc j", j=16),
          d_la.rearrange("(kc p) j -> p kc j", p=128),
          waits=[("sDVE", ck_lb)], sem="sS")
    ck_la = p.add("vector",
                  lambda e: e.tensor_copy(out=laB[:], in_=U[:, 6144:6272]),
                  waits=[("sS", 64)], inc="sDVE")
    p.add("vector", lambda e: e.memset(flt[:], 0.0), inc="sDVE")
    p.add("vector", lambda e: e.memset(onest[:], 1.0), inc="sDVE")
    p.add("vector", lambda e: e.memset(zt[:], 0.0), inc="sDVE")
    p.add("vector", lambda e: e.memset(carry[:], 0.0), inc="sDVE")
    p.add("vector", lambda e: e.memset(qrow[0:1, 131:132], 7.0), inc="sDVE")
    ck_zt = p.cnt["sDVE"]

    for v, dsrc in enumerate([d_bpw, d_bpk, d_bu, d_bo, d_fb]):
        ck_row = p.dma("sync", rowb, dsrc, sem="sB")
        ck_pe = None
        for c in range(8):
            ck_pe = p.add(
                "tensor",
                lambda e, v=v, c=c: e.transpose(
                    out=P1[:, 1536 + v * 8 + c: 1537 + v * 8 + c],
                    in_=Bq[0:1, c * 128:(c + 1) * 128],
                    identity=idT[0:1, 0:1]),
                waits=[("sB", ck_row), ("sS", 64)], inc="sPE")
        ck_bias_pe = ck_pe
        ck_bias_evac = p.add(
            "scalar",
            lambda e, v=v: e.copy(out=bcol[:, v * 8:(v + 1) * 8],
                                  in_=P1[:, 1536 + v * 8: 1544 + v * 8]),
            waits=[("sPE", ck_pe)], inc="sACT")
        p.wait_only("sync", [("sPE", ck_pe)])
    p.add("vector",
          lambda e: e.tensor_scalar(out=bcol[:, 8:16], in0=bcol[:, 8:16],
                                    scalar1=-2.0, scalar2=None, op0=ALU.mult),
          waits=[("sACT", ck_bias_evac)], inc="sDVE")
    ck_bpk2 = p.cnt["sDVE"]

    ck_q_ld = [None] * 8
    ck_q_red = [None] * 8

    def passA_load(s):
        ck_q_ld[s] = p.dma("gpsimd", WQ(s), d_kq[s * 128:(s + 1) * 128, :],
                           waits=[("sDVE", ck_q_red[s - 2])], sem=f"sQ{s % 2}")

    def passA_reduce(s):
        ck_q_red[s] = p.add(
            "vector",
            lambda e, s=s: e.reduce_max(out=qsc[:, s:s + 1], in_=WQ(s),
                                        axis=AX.X, apply_absolute_value=True),
            waits=[(f"sQ{s % 2}", ck_q_ld[s])], inc="sDVE")

    QB = U[:, 9216:10240]
    T2 = U[:, 5120:6144]
    ck_qb_ld = [None] * 8
    ck_quant_act = None
    ck_qd_prev = None
    ck_t2_prev = None

    def passB_load(s, extra=()):
        w = list(extra)
        if ck_qd_prev is not None:
            w.append(("sDVE", ck_qd_prev))
        ck_qb_ld[s] = p.dma("gpsimd", QB, d_kq[s * 128:(s + 1) * 128, :],
                            waits=w, sem="sQL0")

    def quant_pA(s, extra=()):
        nonlocal ck_t2_prev
        ck_v = p.add("scalar", lambda e: e.activation(
            out=QB, in_=QB, func=AF.Copy, bias=0.0, scale=qsc[:, 10:11]),
            waits=[("sQL0", ck_qb_ld[s])] + list(extra), inc="sACT")
        ck_r = None
        wt2 = [("sDVE", ck_t2_prev)] if ck_t2_prev is not None \
            else [("sDVE", ck_q_red[7])]
        for hh in range(2):
            p.add("scalar", lambda e, hh=hh: e.copy(
                out=QT[:], in_=QB[:, hh * 512:(hh + 1) * 512]), inc="sACT")
            ck_r = p.add("scalar", lambda e, hh=hh: e.activation(
                out=T2[:, hh * 512:(hh + 1) * 512], in_=QT[:],
                func=AF.Copy, bias=-1.0, scale=1.0),
                waits=wt2 if hh == 0 else (), inc="sACT")
        p.add("vector", lambda e: e.tensor_tensor(
            out=QB, in0=QB, in1=T2, op=ALU.subtract),
            waits=[("sACT", ck_r)], inc="sDVE")

    def quant_pB(s):
        nonlocal ck_quant_act, ck_qd_prev, ck_t2_prev
        p.add("vector", lambda e: e.scalar_tensor_tensor(
            out=T2, in0=QB, scalar=1.5, in1=T2,
            op0=ALU.is_gt, op1=ALU.add), inc="sDVE")
        ck_qd_prev = p.add("vector", lambda e: e.scalar_tensor_tensor(
            out=T2, in0=QB, scalar=0.5, in1=T2,
            op0=ALU.is_ge, op1=ALU.add), inc="sDVE")
        ck_t2_prev = ck_clip = p.add("vector", lambda e: e.tensor_scalar(
            out=T2, in0=T2, scalar1=-8.0, scalar2=7.0,
            op0=ALU.max, op1=ALU.min), inc="sDVE")
        ck_quant_act = p.add("scalar", lambda e, s=s: e.activation(
            out=kqB[:, s * 1024:(s + 1) * 1024], in_=T2,
            func=AF.Copy, bias=0.0, scale=qsc[:, 9:10]),
            waits=[("sDVE", ck_clip)], inc="sACT")

    def quant_ph2(s, extra=()):
        quant_pA(s, extra)
        quant_pB(s)

    def w_srcs(c):
        wkr = d_wk.rearrange("(kc p) n -> p kc n", p=128)
        return [
            wkr[:, :, c * 128:(c + 1) * 128],
            wkr[:, :, 1024 + c * 128:1024 + (c + 1) * 128],
            d_wu.rearrange("(kc p) n -> p kc n",
                           p=128)[:, :, c * 128:(c + 1) * 128],
        ]

    W0S = [U[:, 9216:10240], U[:, 7168:8192], U[:, 8192:9216]]
    ck_castw_c0 = [None] * 3

    def emit_w0s():
        for i, src_ap in enumerate(w_srcs(0)):
            ck_ld = p.dma(["sync", "scalar", "gpsimd"][i],
                          W0S[i].rearrange("p (kc n) -> p kc n", n=128),
                          src_ap, sem=f"sW0{i}")
            ck_castw_c0[i] = p.add(
                "scalar", lambda e, i=i, W=W0S[i]: e.copy(out=wbf[i][:],
                                                          in_=W),
                waits=[(f"sW0{i}", ck_ld)], inc="sACT")

    XG3 = [U[:, 6144:7168], U[:, 7168:8192], U[:, 8192:9216],
           U[:, 9216:10240]]
    def xg3_w(j):
        return [[("sDVE", ck_la)], [("sACT", ck_castw_c0[1])],
                [("sACT", ck_castw_c0[2])], [("sACT", ck_castw_c0[0])]][j]

    ck_xl = [None] * 4
    ck_xq = {}
    ck_tp_g = [None] * 4
    ck_xe_bank = [None] * 4
    ck_xg3 = {}
    for g in range(4):
        for j in range(4):
            if g == 3:
                w = list(xg3_w(j))
            else:
                w = []
                if g == 0 and j == 0:
                    w = [("sDVE", ck_lb)]
                if g == 0 and j == 1:
                    w = [("sDVE", ck_la)]
                if g > 0:
                    w = [("sPE", ck_tp_g[g - 1])]
            qn, sn = [("sync", "sX"), ("scalar", "sX2"),
                      ("gpsimd", "sX3")][(4 * g + j) % 3]
            dst = XL(j)
            if g == 3:
                dst = XG3[j]
                sn = "sY" + sn[2:].replace("X", "0")
            ck_xl[j] = p.dma(
                qn, dst,
                d_x[(4 * g + j) * 128:(4 * g + j + 1) * 128, :],
                waits=w, sem=sn)
            if g == 3:
                ck_xg3[sn] = ck_xl[j]
            else:
                ck_xq[sn] = ck_xl[j]
        if g == 1:
            emit_w0s()
        if g == 2:
            ck_xg2 = dict(ck_xq)
        if g >= 2:
            continue
        for fc in range(8):
            bank = (g * 8 + fc) % 4
            ck_pe = None
            for j in range(4):
                w = [(sn, ck) for sn, ck in ck_xq.items()]
                if ck_xe_bank[bank] is not None:
                    w.append(ck_xe_bank[bank])
                ck_pe = p.add(
                    "tensor",
                    lambda e, j=j, fc=fc, bank=bank: e.transpose(
                        out=P0[:, bank * 512 + j * 128: bank * 512 + (j + 1) * 128],
                        in_=XL(j)[:, fc * 128:(fc + 1) * 128],
                        identity=idT[:]),
                    waits=w, inc="sPE")
            ck_tp_g[g] = ck_pe
            ck_ev = p.add("scalar", lambda e, fc=fc, g=g, bank=bank: e.copy(
                out=xT[:, fc * 2048 + g * 512: fc * 2048 + (g + 1) * 512],
                in_=P0[:, bank * 512:(bank + 1) * 512]),
                waits=[("sPE", ck_pe)], inc="sACT")
            ck_xe_bank[bank] = ("sACT", ck_ev)
    ck_phase1_pe = ck_tp_g[1]
    ck_xT_done = [ck_xe_bank[b] for b in range(4)]
    ck_tp3 = [None]

    def emit_tp_late():
        nonlocal ck_phase1_pe
        for g, srcs, sems in ((2, [XL(j) for j in range(4)], ck_xg2),
                              (3, XG3, ck_xg3)):
            for fc in range(8):
                bank = 2 + fc % 2
                ck_pe = None
                for j in range(4):
                    w = [(sn, ck) for sn, ck in sems.items()]
                    if ck_xe_bank[bank] is not None:
                        w.append(ck_xe_bank[bank])
                    ck_pe = p.add(
                        "tensor",
                        lambda e, s=srcs[j], fc=fc, bank=bank, j=j:
                        e.transpose(
                            out=P0[:, bank * 512 + j * 128:
                                   bank * 512 + (j + 1) * 128],
                            in_=s[:, fc * 128:(fc + 1) * 128],
                            identity=idT[:]),
                        waits=w, inc="sPE")
                ck_ev = p.add("scalar", lambda e, fc=fc, bank=bank, g=g:
                              e.copy(
                                  out=xT[:, fc * 2048 + g * 512:
                                         fc * 2048 + (g + 1) * 512],
                                  in_=P0[:, bank * 512:(bank + 1) * 512]),
                              waits=[("sPE", ck_pe)], inc="sACT")
                ck_xe_bank[bank] = ("sACT", ck_ev)
            if g == 2:
                ck_phase1_pe = ck_pe
        ck_tp3[0] = ck_pe
        ck_xT_done[2] = ck_xe_bank[2]
        ck_xT_done[3] = ck_xe_bank[3]
    for s in range(2):
        ck_q_ld[s] = p.dma("gpsimd", WQ(s), d_kq[s * 128:(s + 1) * 128, :],
                           waits=[("sDVE", ck_la)], sem=f"sQ{s % 2}")
    if tap:
        emit_tp_late()
        p.dma("gpsimd", tapd["t_xT"], xT[:], waits=ck_xT_done, sem="sTAP")

    wq_rot = 8
    ck_wo_cast = None

    def wo_slice(s):
        nonlocal ck_wo_cast
        if ck_wo_cast is None:
            w = [("sDVE", ck_q_red[6])]
        else:
            w = [("sACT", ck_wo_cast)]
        ck_ld = p.dma("sync", WQ(0), d_wo[s * 128:(s + 1) * 128, :],
                      waits=w, sem="sQW0")
        ck_wo_cast = p.add("scalar", lambda e, s=s: e.copy(
            out=woB[:, s * 1024:(s + 1) * 1024], in_=WQ(0)),
            waits=[("sQW0", ck_ld)], inc="sACT")

    Wr = [P0[:, 0:1024], P0[:, 1024:2048]]
    Kr = P1[:, 0:1024]
    Ur = P1[:, 1024:2048]
    Bq2[1] = U[:, 6144:7168]
    Bu2[1] = U[:, 8192:9216]
    Bb2 = U[:, 7168:8192]

    ck_castw = [None] * 3
    ck_wld = [None] * 3
    ck_wst_cast = [None] * 3
    ck_wbf_used = [None] * 3

    def weights_dma(c):
        for i, src_ap in enumerate(w_srcs(c)):
            w = []
            if ck_wst_cast[i] is not None:
                w.append(("sACT", ck_wst_cast[i]))
            else:
                w.append(("sPE", ck_phase1_pe))
                w.append(("sDVE", ck_lb))
            ck_wld[i] = p.dma(
                "sync" if i == 0 else "scalar",
                XL(i).rearrange("p (kc n) -> p kc n", n=128),
                src_ap, waits=w, sem=f"sWS{i}")

    def weights_cast():
        for i in range(3):
            wpe = [(f"sWS{i}", ck_wld[i])]
            if ck_wbf_used[i] is not None:
                wpe.append(("sPE", ck_wbf_used[i]))
            ck_wst_cast[i] = ck_castw[i] = p.add(
                "scalar", lambda e, i=i: e.copy(out=wbf[i][:], in_=XL(i)),
                waits=wpe, inc="sACT")

    ck_g_pr = [None, None]
    ck_q_last = None
    ck_uev_last = None
    ck_carry_pr = [None, None]
    ck_castA_pr = [None, None]
    ck_fA_pr = [None, None]
    ck_casty = [None] * 8
    ck_castA = [None] * 8
    ck_cy = [None] * 8
    ck_cA = [None] * 8
    pend = []
    ck_ccinA = [None]
    ck_fy_last = None
    ck_qmax = [None]
    ck_qscale = [None]
    ck_qcols = [None]

    def emit_pend():
        while pend:
            pend.pop(0)()

    ck_castw = list(ck_castw_c0)

    for c in range(8):
        for th in range(2):
            i = 2 * c + th
            pr = i % 2

            def mm_block(tgt, region, extra, th=th):
                ckl = None
                for kc in range(8):
                    for nt in range(2):
                        ckl = p.add(
                            "tensor",
                            lambda e, tgt=tgt, region=region, kc=kc, nt=nt,
                            th=th: e.matmul(
                                out=region[:, nt * 512:(nt + 1) * 512],
                                lhsT=wbf[tgt][:, kc * 128:(kc + 1) * 128],
                                rhs=xT[:, kc * 2048 + th * 1024 + nt * 512:
                                       kc * 2048 + th * 1024 +
                                       (nt + 1) * 512],
                                start=(kc == 0), stop=(kc == 7)),
                            waits=extra if (kc == 0 and nt == 0) else (),
                            inc="sPE" if (kc == 7 and nt == 1) else None)
                return ckl

            ew = [("sACT", ck_castw[0])] if th == 0 else []
            if i == 10:
                ew.append(("sACT", ck_qcols[0]))
            if ck_g_pr[pr] is not None:
                ew.append(("sACT", ck_g_pr[pr]))
            elif i == 0:
                ew += list(ck_xT_done)
            else:
                ew += list(ck_xT_done[2:4])
            ck_w = mm_block(0, Wr[pr], ew)

            ek = [("sACT", ck_castw[1])] if th == 0 else []
            if ck_q_last is not None:
                ek.append(("sACT", ck_q_last))
            ck_k = mm_block(1, Kr, ek)

            eu = [("sACT", ck_castw[2])] if th == 0 else []
            if ck_uev_last is not None:
                eu.append(("sDVE", ck_uev_last))
            elif i == 0:
                eu.append(("sACT", ck_bias_evac))
            ck_u = mm_block(2, Ur, eu)
            if th == 1:
                ck_wbf_used = [ck_w, ck_k, ck_u]

            if th == 0 and 0 < c < 7:
                weights_dma(c + 1)
            if i == 0:
                if ck_tp3[0] is None:
                    emit_tp_late()
                weights_dma(1)

            Bgx, Bqx, Bux = Bg2[pr], Bq2[pr], Bu2[pr]
            wg = [("sPE", ck_w)]
            if ck_carry_pr[pr] is not None:
                wg.append(("sDVE", ck_carry_pr[pr]))
            ck_g = p.add("scalar", lambda e, c=c, Bgx=Bgx, region=Wr[pr]:
                         e.activation(
                             out=Bgx[:], in_=region, func=AF.Sigmoid,
                             bias=bcol[:, c:c + 1], scale=1.0),
                         waits=wg, inc="sACT")
            ck_g_pr[pr] = ck_g
            emit_pend()
            wq2 = [("sPE", ck_k)]
            if ck_fA_pr[pr] is not None:
                wq2.append(("sDVE", ck_fA_pr[pr]))
            elif i == 0:
                wq2.append(("sPE", ck_bias_pe))
                wq2.append(("sDVE", ck_bpk2))
            ck_qe = p.add("scalar", lambda e, c=c, Bqx=Bqx: e.activation(
                out=Bqx[:], in_=Kr, func=AF.Sigmoid,
                bias=bcol[:, 8 + c:9 + c], scale=-2.0),
                waits=wq2, inc="sACT")
            ck_q_last = ck_qe
            if th == 1 and c < 7:
                weights_cast()
            if i >= 8:
                wo_slice(i - 8)
            if i == 8:
                passB_load(0, extra=[("sACT", ck_castw_c0[0]),
                                     ("sPE", ck_tp3[0])])
            if 10 <= i <= 15:
                s = (i - 10) // 2
                if i % 2 == 0:
                    quant_pA(s)
                else:
                    quant_pB(s)
                    if s < 2:
                        passB_load(s + 1)

            wu = [("sPE", ck_u)]
            if ck_castA_pr[pr] is not None:
                wu.append(("sACT", ck_castA_pr[pr]))
            if i == 0:
                wu.append(("sACT", ck_bias_evac))
            if i == 1:
                wu.append(("sACT", ck_castw_c0[2]))
            ck_uev = p.add("vector", lambda e, c=c, Bux=Bux: e.tensor_scalar(
                out=Bux[:], in0=Ur, scalar1=bcol[:, 16 + c:17 + c],
                scalar2=None, op0=ALU.add), waits=wu, inc="sDVE")
            ck_uev_last = ck_uev

            ck_stt = p.add("vector", lambda e, Bgx=Bgx, Bqx=Bqx:
                           e.scalar_tensor_tensor(
                               out=Bb[:], in0=Bgx[:], scalar=-2.0,
                               in1=Bqx[:], op0=ALU.mult, op1=ALU.mult),
                           waits=[("sACT", ck_g), ("sACT", ck_qe)],
                           inc="sDVE")
            ck_aa = p.add("scalar", lambda e, Bqx=Bqx: e.activation(
                out=Bqx[:], in_=Bb[:], func=AF.Copy, bias=1.0, scale=1.0),
                waits=[("sDVE", ck_stt)], inc="sACT")
            wb = [("sDVE", ck_stt)]
            if i == 0:
                wb.append(("sACT", ck_castw_c0[1]))
                wb.append(("sPE", ck_tp3[0]))
            ck_b = p.add("vector", lambda e, Bgx=Bgx, Bux=Bux:
                         e.tensor_tensor(out=Bb2, in0=Bgx[:], in1=Bux[:],
                                         op=ALU.mult),
                         waits=wb, inc="sDVE")
            if th == 0:
                ini_y, wy_extra = 0.0, []
            else:
                ini_y = carry[:, c:c + 1]
                wy_extra = [("sDVE", ck_cy[c])]
            p.add("vector", lambda e, ini_y=ini_y, Bgx=Bgx, Bqx=Bqx:
                  e.tensor_tensor_scan(
                      out=Bgx[:], data0=Bqx[:], data1=Bb2, initial=ini_y,
                      op0=ALU.mult, op1=ALU.add),
                  waits=[("sDVE", ck_b), ("sACT", ck_aa)] + wy_extra,
                  inc="sDVE")
            ck_fy = ck_fy_last = flush_dve()
            wcy = [("sGL", ck_ccinA[0])] if i >= 14 else []
            ck_cy[c] = ck_carry_pr[pr] = p.add(
                "vector", lambda e, c=c, Bgx=Bgx: e.tensor_copy(
                    out=carry[:, c:c + 1], in_=Bgx[:, 1023:1024]),
                waits=wcy, inc="sDVE")
            if th == 0:
                ini_a = 1.0
                wa_extra = [("sDVE", ck_zt)] if i == 0 else []
            else:
                ini_a = carry[:, 8 + c:9 + c]
                wa_extra = [("sDVE", ck_cA[c])]
            p.add("vector", lambda e, ini_a=ini_a, Bqx=Bqx, Bux=Bux:
                  e.tensor_tensor_scan(
                      out=Bux[:], data0=Bqx[:], data1=zt[:], initial=ini_a,
                      op0=ALU.mult, op1=ALU.add),
                  waits=wa_extra, inc="sDVE")
            ck_fa = flush_dve()
            ck_fA_pr[pr] = ck_fa
            ck_cA[c] = p.add("vector", lambda e, c=c, Bux=Bux: e.tensor_copy(
                out=carry[:, 8 + c:9 + c], in_=Bux[:, 1023:1024]),
                inc="sDVE")
            if i < 8:
                passA_reduce(i)
                if i <= 5:
                    passA_load(i + 2)
            if i == 13:
                ck_fcarA = flush_dve()
                ck_ccinA[0] = p.dma("sync", cc_in.ap(), carry[:, 0:8],
                                    waits=[("sDVE", ck_fcarA)], sem="sGL")
                p.wait_only("gpsimd", [("sGL", ck_ccinA[0])])
                p.add("gpsimd", lambda e: e.collective_compute(
                    "AllGather", ALU.bypass,
                    replica_groups=[list(range(NCORES))],
                    ins=[cc_in.ap().opt()], outs=[cc_out.ap().opt()]),
                    inc="sCC")
            if i == 7:
                p.add("vector", lambda e: e.reduce_max(
                    out=qsc[:, 8:9], in_=qsc[:, 0:8], axis=AX.X,
                    apply_absolute_value=False), inc="sDVE")
                ck_qmax[0] = flush_dve()
            if i == 8:
                ck_qtp = p.add("tensor", lambda e: e.transpose(
                    out=P0[0:1, 16:144], in_=qsc[:, 8:9], identity=idT[:]),
                    waits=[("sDVE", ck_qmax[0]),
                           ("sACT", ck_g_pr[0])], inc="sPE")
                ck_qrow = p.add("scalar", lambda e: e.copy(
                    out=qrow[0:1, 0:128], in_=P0[0:1, 16:144]),
                    waits=[("sPE", ck_qtp)], inc="sACT")
                p.add("vector", lambda e: e.reduce_max(
                    out=qrow[0:1, 128:129], in_=qrow[0:1, 0:128],
                    axis=AX.X, apply_absolute_value=False),
                    waits=[("sACT", ck_qrow)], inc="sDVE")
                p.add("vector", lambda e: e.reciprocal(
                    out=qrow[0:1, 132:133], in_=qrow[0:1, 131:132]),
                    inc="sDVE")
                flush_dve()
                p.add("vector", lambda e: e.tensor_tensor(
                    out=qrow[0:1, 129:130], in0=qrow[0:1, 128:129],
                    in1=qrow[0:1, 132:133], op=ALU.mult), inc="sDVE")
                flush_dve()
                p.add("vector", lambda e: e.reciprocal(
                    out=qrow[0:1, 130:131], in_=qrow[0:1, 129:130]),
                    inc="sDVE")
                ck_qscale[0] = flush_dve()
            if i == 9:
                ck_qb2 = p.add("tensor", lambda e: e.matmul(
                    out=P0[:, 16:17], lhsT=onest[0:1, :],
                    rhs=qrow[0:1, 129:130], start=True, stop=True),
                    waits=[("sDVE", ck_qscale[0])], inc="sPE")
                ck_qb2 = p.add("tensor", lambda e: e.matmul(
                    out=P0[:, 17:18], lhsT=onest[0:1, :],
                    rhs=qrow[0:1, 130:131], start=True, stop=True),
                    inc="sPE")
                ck_qcols[0] = p.add("scalar", lambda e: e.copy(
                    out=qsc[:, 9:11], in_=P0[:, 16:18]),
                    waits=[("sPE", ck_qb2)], inc="sACT")

            def mk_casts(c=c, th=th, pr=pr, Bgx=Bgx, Bux=Bux,
                         ck_fy=ck_fy, ck_fa=ck_fa):
                def do():
                    ck_casty[c] = p.add("scalar", lambda e: e.copy(
                        out=yB[:, c * 2048 + th * 1024:
                               c * 2048 + (th + 1) * 1024], in_=Bgx[:]),
                        waits=[("sDVE", ck_fy)], inc="sACT")
                    ck_castA[c] = ck_castA_pr[pr] = p.add(
                        "scalar", lambda e: e.copy(
                            out=aB[:, c * 2048 + th * 1024:
                                   c * 2048 + (th + 1) * 1024], in_=Bux[:]),
                        waits=[("sDVE", ck_fa)], inc="sACT")
                return do
            pend.append(mk_casts())
    emit_pend()
    ck_pool_casty = p.cnt["sACT"]
    ck_qt_ld = {}
    qt_sem = {6: "sQT0", 7: "sQT1", 4: "sQT2", 5: "sQT3"}
    for s, buf in ((6, Bq), (7, Bu)):
        ck_qt_ld[s] = p.dma(
            "gpsimd", buf[:], d_kq[s * 128:(s + 1) * 128, :],
            waits=[("sDVE", ck_fA_pr[0]), ("sACT", ck_pool_casty)],
            sem=qt_sem[s])

    ck_qtail_prev = [None]

    def quant_tail(s, buf):
        nonlocal ck_quant_act
        ck_v = p.add("scalar", lambda e, buf=buf: e.activation(
            out=buf[:], in_=buf[:], func=AF.Copy, bias=0.0,
            scale=qsc[:, 10:11]),
            waits=[(qt_sem[s], ck_qt_ld[s])], inc="sACT")
        ck_r = None
        for hh in range(2):
            p.add("scalar", lambda e, buf=buf, hh=hh: e.copy(
                out=QT[:], in_=buf[:, hh * 512:(hh + 1) * 512]), inc="sACT")
            wt2 = []
            if hh == 0:
                wt2 = [("sDVE", ck_qtail_prev[0] if ck_qtail_prev[0]
                        is not None else ck_corr_c[7])]
            ck_r = p.add("scalar", lambda e, hh=hh: e.activation(
                out=Bb[:, hh * 512:(hh + 1) * 512], in_=QT[:],
                func=AF.Copy, bias=-1.0, scale=1.0),
                waits=wt2, inc="sACT")
        p.add("vector", lambda e, buf=buf: e.tensor_tensor(
            out=buf[:], in0=buf[:], in1=Bb[:], op=ALU.subtract),
            waits=[("sACT", ck_r)], inc="sDVE")
        p.add("vector", lambda e, buf=buf: e.scalar_tensor_tensor(
            out=Bb[:], in0=buf[:], scalar=1.5, in1=Bb[:],
            op0=ALU.is_gt, op1=ALU.add), inc="sDVE")
        p.add("vector", lambda e, buf=buf: e.scalar_tensor_tensor(
            out=Bb[:], in0=buf[:], scalar=0.5, in1=Bb[:],
            op0=ALU.is_ge, op1=ALU.add), inc="sDVE")
        ck_clip = p.add("vector", lambda e: e.tensor_scalar(
            out=Bb[:], in0=Bb[:], scalar1=-8.0, scalar2=7.0,
            op0=ALU.max, op1=ALU.min), inc="sDVE")
        ck_qtail_prev[0] = ck_clip
        ck_quant_act = p.add("scalar", lambda e, s=s: e.activation(
            out=kqB[:, s * 1024:(s + 1) * 1024], in_=Bb[:],
            func=AF.Copy, bias=0.0, scale=qsc[:, 9:10]),
            waits=[("sDVE", ck_clip)], inc="sACT")
    free_T = {0: ("sACT", ck_g_pr[0]), 1: ("sACT", ck_q_last)}

    ck_fcar = flush_dve()
    ck_ccinB = p.dma("sync", cc_in2.ap(), carry[:, 0:8],
                     waits=[("sDVE", ck_fcar)], sem="sGL")
    p.wait_only("gpsimd", [("sGL", ck_ccinB)])
    p.add("gpsimd", lambda e: e.collective_compute(
        "AllGather", ALU.bypass, replica_groups=[list(range(NCORES))],
        ins=[cc_in2.ap().opt()], outs=[cc_out2.ap().opt()]), inc="sCC")
    p.wait_only("gpsimd", [("sCC", 2)])
    ck_G = p.dma("sync", Gt,
                 cc_out.ap().rearrange("(r p) c -> r (p c)", p=128),
                 waits=[("sACT", ck_pool_casty), ("sCC", 1)], sem="sGL2")
    Gtv = Bg[0:8, 0:1024].rearrange("r (p c) -> r c p", c=8)
    ck_mm_sin = None
    for c8 in range(7):
        w = [("sGL2", ck_G), ("sS", 64)]
        if c8 == 0:
            w.append(("sACT", ck_q_last))
        ck_mm_sin = p.add("tensor", lambda e, c8=c8: e.matmul(
            out=P1[:, c8:c8 + 1],
            lhsT=Gtv[:, c8],
            rhs=mskT[0:8, 0:1], start=True, stop=True),
            waits=w, inc="sPE")
    p.add("scalar", lambda e: e.copy(
        out=carry[:, 16:23], in_=P1[:, 0:7]),
        waits=[("sPE", ck_mm_sin)], inc="sACT")
    ck_sin = flush_act()
    ck_corr_c = [None] * 8
    for c in range(7):
        ck_corr_c[c] = p.add(
            "vector", lambda e, c=c: e.scalar_tensor_tensor(
                out=yB[:, c * 2048:(c + 1) * 2048],
                in0=aB[:, c * 2048:(c + 1) * 2048],
                scalar=carry[:, 16 + c:17 + c],
                in1=yB[:, c * 2048:(c + 1) * 2048],
                op0=ALU.mult, op1=ALU.add),
            waits=[("sACT", ck_sin), ("sACT", ck_castA[c]),
                   ("sACT", ck_casty[c])], inc="sDVE")
    gB = {}

    def emit_gatherB():
        if gB:
            return
        ck_GB = p.dma("sync", Gt,
                      cc_out2.ap().rearrange("(r p) c -> r (p c)", p=128),
                      waits=[("sCC", 2), ("sPE", ck_mm_sin)], sem="sGL2")
        ck_mm_sinB = p.add("tensor", lambda e: e.matmul(
            out=P1[:, 7:8], lhsT=Gtv[:, 7],
            rhs=mskT[0:8, 0:1], start=True, stop=True),
            waits=[("sGL2", ck_GB)], inc="sPE")
        p.add("scalar", lambda e: e.copy(
            out=carry[:, 23:24], in_=P1[:, 7:8]),
            waits=[("sPE", ck_mm_sinB)], inc="sACT")
        gB["sinB"] = flush_act()
        ck_corr_c[7] = gB["corr"] = p.add(
            "vector", lambda e: e.scalar_tensor_tensor(
                out=yB[:, 7 * 2048:8 * 2048],
                in0=aB[:, 7 * 2048:8 * 2048],
                scalar=carry[:, 23:24],
                in1=yB[:, 7 * 2048:8 * 2048],
                op0=ALU.mult, op1=ALU.add),
            waits=[("sACT", gB["sinB"]), ("sACT", ck_castA[7]),
                   ("sACT", ck_casty[7])], inc="sDVE")
        passB_load(3)
        quant_ph2(3)
        passB_load(4)
        quant_ph2(4)
        passB_load(5)
        quant_ph2(5)

    if tap:
        emit_gatherB()
        p.dma("gpsimd", tapd["t_y"], yB[:], waits=[("sDVE", gB["corr"])],
              sem="sTAP")
        p.dma("gpsimd", tapd["t_A"], aB[:], waits=[("sDVE", gB["corr"])],
              sem="sTAP")
        p.dma("gpsimd", tapd["t_carry"], carry[:], waits=[("sACT", ck_sin)], sem="sTAP")
        p.dma("gpsimd", tapd["t_wo"], woB[:], waits=[("sACT", ck_wo_cast)], sem="sTAP")
        p.wait_only("scalar", [("sTAP", p.cnt["sTAP"])])


    tp_rot = [None] * 4
    po_use = [None] * 4
    ln_state = {"lnt_pe": [None, None], "lnt_cast": [None, None],
                "rstt": [None, None], "std": None}

    LNWv = LNW.rearrange("p (a b) -> p a b", b=8)

    def ln_pair(pars, store_tiles=None, extra_apply_waits=()):
        for par in pars:
            p8 = par * 8
            p.add("vector", lambda e, par=par, p8=p8: e.scalar_tensor_tensor(
                out=SCR, in0=RT(par), scalar=1.0, in1=RT(par),
                op0=ALU.mult, op1=ALU.mult,
                accum_out=LNW[:, p8 + 2:p8 + 3]), inc="sDVE")
        p.add("vector", lambda e: e.tensor_tensor(
            out=LNWv[:, :, 3], in0=LNWv[:, :, 0], in1=LNWv[:, :, 1],
            op=ALU.add), inc="sDVE")
        p.add("vector", lambda e: e.tensor_scalar(
            out=LNWv[:, :, 4], in0=LNWv[:, :, 3],
            scalar1=1.0 / 1024.0, scalar2=None, op0=ALU.mult), inc="sDVE")
        p.add("vector", lambda e: e.scalar_tensor_tensor(
            out=LNWv[:, :, 5], in0=LNWv[:, :, 4], scalar=-1.0,
            in1=LNWv[:, :, 4], op0=ALU.mult, op1=ALU.mult), inc="sDVE")
        p.add("vector", lambda e: e.tensor_scalar(
            out=LNWv[:, :, 6], in0=LNWv[:, :, 5],
            scalar1=EPS, scalar2=None, op0=ALU.add), inc="sDVE")
        p.add("vector", lambda e: e.tensor_scalar(
            out=LNWv[:, :, 7], in0=LNWv[:, :, 2],
            scalar1=1.0 / 1024.0, scalar2=None, op0=ALU.mult), inc="sDVE")
        ck_b = p.add("vector", lambda e: e.tensor_tensor(
            out=LNWv[:, :, 7], in0=LNWv[:, :, 7], in1=LNWv[:, :, 6],
            op=ALU.add), inc="sDVE")
        p.add("scalar", lambda e: e.activation(
            out=LNWv[:, :, 5], in_=LNWv[:, :, 7],
            func=AF.Ln, bias=0.0, scale=1.0),
            waits=[("sDVE", ck_b)], inc="sACT")
        p.add("scalar", lambda e: e.activation(
            out=LNWv[:, :, 5], in_=LNWv[:, :, 5],
            func=AF.Exp, bias=0.0, scale=-0.5), inc="sACT")
        ck_std = flush_act()
        ln_state["std"] = ck_std
        cks = {}
        for par in pars:
            p8 = par * 8
            wl = list(extra_apply_waits)
            if store_tiles is not None:
                wl.append(("sST", p.cnt["sST"]))
            prev = ln_state["lnt_cast"][par]
            if prev is not None:
                wl.append(prev)
            prev_pe = ln_state["lnt_pe"][par]
            if prev_pe is not None:
                wl.append(("sPE", prev_pe))
            wl.append(("sACT", ck_std))
            cks[par] = p.add("vector", lambda e, par=par, p8=p8:
                             e.tensor_scalar(
                                 out=LNT(par), in0=RT(par),
                                 scalar1=LNW[:, p8 + 4:p8 + 5],
                                 scalar2=LNW[:, p8 + 5:p8 + 6],
                                 op0=ALU.subtract, op1=ALU.mult),
                             waits=wl, inc="sDVE")
        return cks

    p.dma("sync", U[0:1, 7168:8192], d_bo,
          waits=[("sDVE", ck_fy_last)], sem="sBR")
    p.dma("sync", U[32:33, 7168:8192], d_fb, sem="sBR")
    ck_brows = p.cnt["sBR"]
    bo_bf = brow[0:1, 0:1024]
    fb_bf = brow[0:1, 1024:2048]
    ones_bf = brow[0:1, 2048:2176]
    p.add("scalar", lambda e: e.copy(out=bo_bf, in_=U[0:1, 7168:8192]),
          waits=[("sBR", ck_brows)], inc="sACT")
    p.add("scalar", lambda e: e.copy(out=fb_bf, in_=U[32:33, 7168:8192]),
          inc="sACT")
    p.add("vector", lambda e: e.memset(ones_bf, 1.0), inc="sDVE")
    ck_brows_bf = p.cnt["sACT"]
    ck_ones_bf = p.cnt["sDVE"]

    ck_tile_rstt = [None, None]
    ck_xr = [None, None]
    ck_lnF = None
    tp_idx = 0

    def tok_tile_mms(tt, lhs_fn, rhs_tile, bias_row, extra_w, extra_last=(),
                     kc_waits=None, mid_hook=None):
        sl = tt % 2
        w = list(extra_w)
        if ck_tile_rstt[sl] is not None:
            w.append(("sDVE", ck_tile_rstt[sl]))
        for kc in range(8):
            if kc == 7 and mid_hook is not None:
                mid_hook()
            for h in range(2):
                wl = list(w) if (kc == 0 and h == 0) else []
                if kc_waits is not None and h == 0:
                    wl.append(kc_waits(kc))
                p.add("tensor", lambda e, kc=kc, h=h, sl=sl, tt=tt: e.matmul(
                    out=P0[:, sl * 1024 + h * 512: sl * 1024 + (h + 1) * 512],
                    lhsT=lhs_fn(kc, tt),
                    rhs=rhs_tile[:, kc * 1024 + h * 512:
                                 kc * 1024 + (h + 1) * 512],
                    start=(kc == 0), stop=False),
                    waits=wl, inc=None)
        ck = None
        for h in range(2):
            wl = [("sACT", ck_brows_bf), ("sDVE", ck_ones_bf)] + \
                list(extra_last)
            ck = p.add("tensor", lambda e, h=h, sl=sl, bias_row=bias_row:
                       e.matmul(
                           out=P0[:, sl * 1024 + h * 512:
                                  sl * 1024 + (h + 1) * 512],
                           lhsT=ones_bf,
                           rhs=bias_row[0:1, h * 512:(h + 1) * 512],
                           start=False, stop=True),
                       waits=wl if h == 0 else (),
                       inc="sPE" if h == 1 else None)
        return ck

    tp_jobs = []

    def emit_tp(tt, par, ck_apply):
        nonlocal ck_lnF, tp_idx
        ck_pe_last = None
        for kg in range(2):
            bank = 4 + (tp_idx % 4)
            tpw = [("sDVE", ck_apply)]
            if tp_rot[tp_idx % 4] is not None:
                tpw.append(tp_rot[tp_idx % 4])
            else:
                tpw.append(("sACT", gB["sinB"]))
            ck_pe = None
            for i in range(4):
                ck_pe = p.add("tensor", lambda e, kg=kg, i=i, par=par,
                              bank=bank: e.transpose(
                                  out=P1[:, (bank - 4) * 512 + i * 128:
                                         (bank - 4) * 512 +
                                         (i + 1) * 128],
                                  in_=LNT(par)[:, (kg * 4 + i) * 128:
                                               (kg * 4 + i + 1) * 128],
                                  identity=idT[:]),
                              waits=tpw if i == 0 else (), inc="sPE")
            ck_lnF = p.add("scalar", lambda e, kg=kg, tt=tt, bank=bank:
                           e.copy(
                               out=xT.rearrange(
                                   "p (kc t) -> p kc t",
                                   t=2048)[:, kg * 4:(kg + 1) * 4,
                                           tt * 128:(tt + 1) * 128],
                               in_=P1[:, (bank - 4) * 512:
                                      (bank - 4 + 1) * 512].rearrange(
                                   "p (kc t) -> p kc t", t=128)),
                           waits=[("sPE", ck_pe)], inc="sACT")
            tp_rot[tp_idx % 4] = ("sACT", ck_lnF)
            tp_idx += 1
            ck_pe_last = ck_pe
        ln_state["lnt_pe"][par] = ck_pe_last

    for pr in range(8):
        tiles = [2 * pr, 2 * pr + 1]
        p.add("vector", lambda e: e.memset(LNWv[:, :, 0:3], 0.0),
              waits=[("sACT", ln_state["std"])], inc="sDVE")
        for tt in tiles:
            sl = tt % 2
            extra_w = [("sACT", ck_wo_cast)]

            if tt == 1:
                extra_w.append(("sACT", ck_g_pr[1]))
            ck_pe = tok_tile_mms(
                tt, lambda kc, tt: yB[:, kc * 2048 + tt * 128:
                                      kc * 2048 + (tt + 1) * 128],
                woB, bo_bf, extra_w,
                kc_waits=lambda kc: ("sDVE", ck_corr_c[kc]),
                mid_hook=emit_gatherB if tt == 0 else None)
            w = []
            if ck_tile_rstt[sl] is not None:
                w.append(("sDVE", ck_tile_rstt[sl]))
            if tt < 2:
                w.append(("sACT", ck_wo_cast))
            if tt == 1:
                w.append(("sACT", ck_quant_act))
            ck_xr[sl] = p.dma("sync", XR(sl),
                              d_x[tt * 128:(tt + 1) * 128, :],
                              waits=w, sem=f"sXR{sl}")
            ck_rstt = None
            for h in range(2):
                ck_rstt = p.add("vector", lambda e, h=h, sl=sl, p8=sl * 8:
                                e.scalar_tensor_tensor(
                                    out=RT(sl)[:, h * 512:(h + 1) * 512],
                                    in0=P0[:, sl * 1024 + h * 512:
                                           sl * 1024 + (h + 1) * 512],
                                    scalar=1.0,
                                    in1=XR(sl)[:, h * 512:(h + 1) * 512],
                                    op0=ALU.mult, op1=ALU.add,
                                    accum_out=LNW[:, p8 + h:p8 + h + 1]),
                                waits=[("sPE", ck_pe),
                                       (f"sXR{sl}", ck_xr[sl])],
                                inc="sDVE")
            ck_tile_rstt[sl] = ck_rstt
            ln_state["rstt"][sl] = ck_rstt
        for job in tp_jobs:
            emit_tp(*job)
        tp_jobs = []
        cks = ln_pair([0, 1])
        for tt in tiles:
            par = tt % 2
            ck_apply = cks[par]
            ck_cast = p.add("scalar", lambda e, tt=tt, par=par: e.copy(
                out=aB[:, tt * 1024:(tt + 1) * 1024], in_=LNT(par)),
                waits=[("sDVE", ck_apply), ("sDVE", ck_corr_c[tt // 2])],
                inc="sACT")
            ln_state["lnt_cast"][par] = ("sACT", ck_cast)
            tp_jobs.append((tt, par, ck_apply))
        if pr == 1:
            quant_tail(6, Bq)
        if pr == 2:
            quant_tail(7, Bu)
    for job in tp_jobs:
        emit_tp(*job)
    tp_jobs = []

    ck_z_ev = None
    for nt in range(4):
        bank = nt % 4
        w = [("sACT", ck_lnF),
             ("sDVE", ck_tile_rstt[0]), ("sDVE", ck_tile_rstt[1])]
        ck_pe = None
        for kc in range(8):
            ck_pe = p.add("tensor", lambda e, kc=kc, nt=nt, bank=bank: e.matmul(
                out=P0[0:16, bank * 512:(bank + 1) * 512],
                lhsT=laB[:, kc * 16:(kc + 1) * 16],
                rhs=xT[:, kc * 2048 + nt * 512: kc * 2048 + (nt + 1) * 512],
                start=(kc == 0), stop=(kc == 7)),
                waits=w if kc == 0 else (), inc="sPE" if kc == 7 else None)
        ck_z_ev = p.add("scalar", lambda e, nt=nt, bank=bank: e.copy(
            out=Z(nt), in_=P0[0:16, bank * 512:(bank + 1) * 512]),
            waits=[("sPE", ck_pe)], inc="sACT")
    ck_tile_rstt = [None, None]

    for pr in range(8):
        tiles = [2 * pr, 2 * pr + 1]
        p.add("vector", lambda e: e.memset(LNWv[:, :, 0:3], 0.0),
              waits=[("sACT", ln_state["std"])], inc="sDVE")
        for tt in tiles:
            sl = tt % 2
            nt = tt // 4
            extra_w = [("sACT", ck_quant_act), ("sACT", ck_z_ev)]
            w = list(extra_w)
            if ck_tile_rstt[sl] is not None:
                w.append(("sDVE", ck_tile_rstt[sl]))
            for kc in range(8):
                for h in range(2):
                    p.add("tensor", lambda e, kc=kc, h=h, sl=sl, tt=tt:
                          e.matmul(
                              out=P0[:, sl * 1024 + h * 512:
                                     sl * 1024 + (h + 1) * 512],
                              lhsT=xT[:, kc * 2048 + tt * 128:
                                      kc * 2048 + (tt + 1) * 128],
                              rhs=kqB[:, kc * 1024 + h * 512:
                                      kc * 1024 + (h + 1) * 512],
                              start=(kc == 0), stop=False),
                          waits=w if (kc == 0 and h == 0) else (), inc=None)
            for h in range(2):
                p.add("tensor", lambda e, h=h, sl=sl, tt=tt, nt=nt: e.matmul(
                    out=P0[:, sl * 1024 + h * 512: sl * 1024 + (h + 1) * 512],
                    lhsT=Z(nt)[:, (tt % 4) * 128:(tt % 4 + 1) * 128],
                    rhs=lbB[0:16, h * 512:(h + 1) * 512],
                    start=False, stop=False),
                    waits=[("sDVE", ck_lb)] if h == 0 else (), inc=None)
            ck_pe = None
            for h in range(2):
                ck_pe = p.add("tensor", lambda e, h=h, sl=sl: e.matmul(
                    out=P0[:, sl * 1024 + h * 512: sl * 1024 + (h + 1) * 512],
                    lhsT=ones_bf,
                    rhs=fb_bf[0:1, h * 512:(h + 1) * 512],
                    start=False, stop=True),
                    waits=[("sACT", ck_brows_bf), ("sDVE", ck_ones_bf)]
                    if h == 0 else (), inc="sPE" if h == 1 else None)
            ck_rstt = None
            for h in range(2):
                ck_rstt = p.add("vector", lambda e, h=h, sl=sl, tt=tt,
                                p8=sl * 8: e.scalar_tensor_tensor(
                                    out=RT(sl)[:, h * 512:(h + 1) * 512],
                                    in0=P0[:, sl * 1024 + h * 512:
                                           sl * 1024 + (h + 1) * 512],
                                    scalar=1.0,
                                    in1=aB[:, tt * 1024 + h * 512:
                                           tt * 1024 + (h + 1) * 512],
                                    op0=ALU.mult, op1=ALU.add,
                                    accum_out=LNW[:, p8 + h:p8 + h + 1]),
                                waits=[("sPE", ck_pe)], inc="sDVE")
            ck_tile_rstt[sl] = ck_rstt
        cks = ln_pair([0, 1], store_tiles=True)
        for tt in tiles:
            par = tt % 2
            ck_st = p.dma("scalar", d_out[tt * 128:(tt + 1) * 128, :],
                          LNT(par), waits=[("sDVE", cks[par])], sem="sST")
            ln_state["lnt_cast"][par] = ("sST", ck_st)
            ln_state["lnt_pe"][par] = None

    if tap:
        lc = ln_state["lnt_cast"]
        p.dma("gpsimd", tapd["t_ln1"], aB[:],
              waits=[("sACT", p.cnt["sACT"]), ("sDVE", p.cnt["sDVE"])],
              sem="sTAP")
        p.dma("gpsimd", tapd["t_lnf"], xT[:],
              waits=[("sACT", p.cnt["sACT"])], sem="sTAP")
        p.dma("gpsimd", tapd["t_kq"], kqB[:],
              waits=[("sDVE", p.cnt["sDVE"]), ("sACT", p.cnt["sACT"])],
              sem="sTAP")
        p.dma("gpsimd", tapd["t_qs"], qsc[:, 0:11],
              waits=[("sDVE", p.cnt["sDVE"])], sem="sTAP")
        p.dma("gpsimd", tapd["t_qrow"], qrow[0:1, 0:133],
              waits=[("sDVE", p.cnt["sDVE"])], sem="sTAP")

    p.wait_only("sync", [("sST", p.cnt["sST"])])
    p.wait_only("gpsimd", [("sST", p.cnt["sST"])])
    p.wait_only("gpsimd", [("sGL", p.cnt["sGL"])])
    p.wait_only("gpsimd", [("sGL2", p.cnt["sGL2"])])
    if tap:
        p.wait_only("gpsimd", [("sTAP", p.cnt["sTAP"])])

    with nc.Block() as block:
        p.emit(block)
    return nc


_NC_CACHE = None


def kernel(L0_attn_x, L0_attn_Wp, L0_attn_bp, L0_attn_Wu, L0_attn_bu,
           L0_attn_Wo, L0_attn_bo, L0_ffn_kernel, L0_ffn_bias, L0_ffn_a,
           L0_ffn_b):
    global _NC_CACHE
    x = np.ascontiguousarray(np.asarray(L0_attn_x, dtype=np.float32))
    wk = np.ascontiguousarray(np.asarray(L0_attn_Wp, dtype=np.float32)[:, :2048])
    wu = np.ascontiguousarray(np.asarray(L0_attn_Wu, dtype=np.float32))
    wo = np.ascontiguousarray(np.asarray(L0_attn_Wo, dtype=np.float32))
    kq = np.ascontiguousarray(np.asarray(L0_ffn_kernel, dtype=np.float32))
    la = np.ascontiguousarray(np.asarray(L0_ffn_a, dtype=np.float32))
    lb = np.ascontiguousarray(np.asarray(L0_ffn_b, dtype=np.float32))
    bp = np.asarray(L0_attn_bp, dtype=np.float32)
    bpw = np.ascontiguousarray(bp[:1024].reshape(1, 1024))
    bpk = np.ascontiguousarray(bp[1024:2048].reshape(1, 1024))
    bu = np.ascontiguousarray(np.asarray(L0_attn_bu, np.float32).reshape(1, 1024))
    bo = np.ascontiguousarray(np.asarray(L0_attn_bo, np.float32).reshape(1, 1024))
    fb = np.ascontiguousarray(np.asarray(L0_ffn_bias, np.float32).reshape(1, 1024))
    ident = np.eye(128, dtype=np.float32)

    if _NC_CACHE is None:
        _NC_CACHE = build_nc()
    nc = _NC_CACHE

    in_maps = []
    for c in range(NCORES):
        b, h = c // 2, c % 2
        mask = np.zeros((8, 1), np.float32)
        if h == 1:
            mask[c - 1, 0] = 1.0
        in_maps.append(dict(
            xc=np.ascontiguousarray(x[b, h * TC:(h + 1) * TC, :]),
            wk=wk, wu=wu, wo=wo, kq=kq, la=la, lb=lb,
            bpw=bpw, bpk=bpk, bu=bu, bo=bo, fb=fb,
            mask=mask, ident=ident))
    res = run_bass_kernel_spmd(nc, in_maps, core_ids=list(range(NCORES)))
    out = np.empty((4, 4096, 1024), dtype=np.float32)
    for c in range(NCORES):
        b, h = c // 2, c % 2
        out[b, h * TC:(h + 1) * TC, :] = np.asarray(res.results[c]["yout"])
    return out
